# version 01
# baseline (speedup 1.0000x reference)
import os
import numpy as np
import ml_dtypes
import concourse.bass as bass
import concourse.mybir as mybir
from concourse.bass_utils import run_bass_kernel_spmd

F32 = mybir.dt.float32
BF16 = mybir.dt.bfloat16
AF = mybir.ActivationFunctionType
ALU = mybir.AluOpType
AX = mybir.AxisListType

T = 8192
D = 1024
CIN = 4256
DIN = 2048
H = 32
PDIM = 64
NST = 64
Q = 128
NE = 8
TILE = 512
NTILES = T // TILE
NCH = TILE // Q
MC = CIN // 128 + (1 if CIN % 128 else 0)
TB = 1024
NB = T // TB
LN_EPS = 1e-5
F32EPS = float(np.finfo(np.float32).eps)
NEG_BIG = -1e30

DEBUG = os.environ.get("KDEBUG", "")


ENGS = ("pe", "act", "dve", "pool", "sp")
DMAQ = ("sp", "act", "pool")


class Prog:
    def __init__(self):
        self.nodes = []
        self.last_w = {}
        self.readers = {}
        self.bar = None

    def barrier(self):
        last = {}
        for i, n in enumerate(self.nodes):
            last[(n["eng"], n["dma"])] = i
        self.bar = None
        nid = self.add("sp", lambda e: e.nop(), extra_deps=list(last.values()))
        self.bar = nid
        return nid

    def add(self, eng, fn, reads=(), writes=(), dma=False, extra_deps=()):
        deps = set(int(d) for d in extra_deps if d is not None)
        if self.bar is not None:
            deps.add(self.bar)
        for b in reads:
            if b in self.last_w:
                deps.add(self.last_w[b])
        for b in writes:
            if b in self.last_w:
                deps.add(self.last_w[b])
            for r in self.readers.get(b, ()):
                deps.add(r)
        nid = len(self.nodes)
        deps = sorted(d for d in deps if d < nid)
        self.nodes.append(dict(eng=eng, fn=fn, dma=bool(dma), deps=deps))
        for b in reads:
            self.readers.setdefault(b, []).append(nid)
        for b in writes:
            self.last_w[b] = nid
            self.readers[b] = []
        return nid

    def emit(self, nc, block):
        nodes = self.nodes
        need_inc = [False] * len(nodes)
        for n in nodes:
            for d in n["deps"]:
                if not nodes[d]["dma"]:
                    need_inc[d] = True
        eng_inc_prefix = {}
        dma_prefix = {}
        cnt = {e: 0 for e in ENGS}
        dcnt = {e: 0 for e in ENGS}
        for i, n in enumerate(nodes):
            if n["dma"]:
                dcnt[n["eng"]] += 1
                dma_prefix[i] = dcnt[n["eng"]]
            else:
                if need_inc[i]:
                    cnt[n["eng"]] += 1
                eng_inc_prefix[i] = cnt[n["eng"]]
        sems = {}
        dsems = {}
        import contextlib
        stack = contextlib.ExitStack()
        for e in ENGS:
            sems[e] = stack.enter_context(nc.semaphore(f"s_{e}"))
        for e in DMAQ:
            dsems[e] = stack.enter_context(nc.semaphore(f"d_{e}"))

        def emit_for(eng_name):
            def _(eng):
                waited = {}
                for i, n in enumerate(nodes):
                    if n["eng"] != eng_name:
                        continue
                    for d in n["deps"]:
                        dn = nodes[d]
                        if dn["dma"]:
                            sem, val = dsems[dn["eng"]], 16 * dma_prefix[d]
                        else:
                            sem, val = sems[dn["eng"]], eng_inc_prefix[d]
                        key = id(sem)
                        if waited.get(key, -1) >= val:
                            continue
                        eng.wait_ge(sem, val)
                        waited[key] = val
                    ins = n["fn"](eng)
                    if n["dma"]:
                        ins.then_inc(dsems[eng_name], 16)
                    elif need_inc[i]:
                        if ins is None:
                            eng.nop().then_inc(sems[eng_name], 1)
                        else:
                            ins.then_inc(sems[eng_name], 1)
                for e in DMAQ:
                    if dcnt[e]:
                        eng.wait_ge(dsems[e], 16 * dcnt[e])
            return _

        block.sync(emit_for("sp"))
        block.scalar(emit_for("act"))
        block.vector(emit_for("dve"))
        block.tensor(emit_for("pe"))
        block.gpsimd(emit_for("pool"))
        return stack


def eng_obj(nc, name):
    return {"pe": nc.tensor, "act": nc.scalar, "dve": nc.vector,
            "pool": nc.gpsimd, "sp": nc.sync}[name]


def build_program(ntiles=NTILES, nblocks=NB, nexp=NE + 1, debug=DEBUG):
    nc = bass.Bass("TRN2", target_bir_lowering=False)
    p = Prog()

    def din(name, shape, dt=F32):
        return nc.dram_tensor(name, list(shape), dt, kind="ExternalInput").ap()

    x_d = din("x", (T, D))
    w_in_d = din("w_in", (CIN, D))
    convw_d = din("conv_w", (CIN, 4))
    convb_d = din("conv_b", (CIN,))
    alog_d = din("A_log", (H,))
    dparam_d = din("Dparam", (DIN,))
    dtb_d = din("dt_bias", (H,))
    mnw_d = din("mnorm_w", (DIN,))
    w_out_d = din("w_out", (D, DIN))
    w1_d = din("w1", (NE, DIN, D))
    w2_d = din("w2", (NE, D, DIN))
    sw1_d = din("sw1", (DIN, D))
    sw2_d = din("sw2", (D, DIN))
    rw_d = din("rw", (NE, D))
    lng_d = din("ln_g", (D,))
    lnb_d = din("ln_b", (D,))
    ident_d = din("ident", (128, 128))
    masklog_d = din("masklogT", (128, 128))
    iota8_d = din("iota8", (128, NE))

    out_d = nc.dram_tensor("out_fm", [8, 128, T], F32, kind="ExternalOutput").ap()

    def dscratch(name, shape, dt):
        return nc.dram_tensor(name, list(shape), dt, kind="Internal").ap()

    w_inT_s = dscratch("w_inT_s", (MC, 128, 8 * 128), BF16)
    w_outT_s = dscratch("w_outT_s", (8, 128, 16 * 128), BF16)
    w1T_s = dscratch("w1T_s", (nexp, 16, 128, 8 * 128), BF16)
    w2T_s = dscratch("w2T_s", (nexp, 8, 128, 16 * 128), BF16)
    h_s = dscratch("h_s", (8, 128, T), BF16)
    g_s = dscratch("g_s", (NE, T), F32)
    attn_s = dscratch("attn_s", (8, 128, T), F32)
    srep_s = dscratch("srep_s", (2, H * Q), F32)
    bcC_s = dscratch("bcC_s", (64, TILE), F32)
    scl_dbg = dscratch("scl_dbg", (256, 128), F32) if debug else None
    sum_dbg = dscratch("sum_dbg", (256, 128), F32) if debug else None

    dbg_outs = {}

    def dbg(name, src_ap, shape, dt=F32, reads=(), eng="sp"):
        if not debug:
            return
        o = nc.dram_tensor("dbg_" + name, list(shape), dt, kind="ExternalOutput").ap()
        dbg_outs[name] = o
        p.add(eng, lambda e, o=o, s=src_ap: e.dma_start(out=o[...], in_=s),
              reads=list(reads), dma=True)

    sb_allocs = []

    def sb(name, shape, dt, **kw):
        h = nc.alloc_sbuf_tensor(name, list(shape), dt, **kw)
        sb_allocs.append(h)
        return h.ap()

    ident = sb("sb_ident", (128, 128), F32)
    masklog = sb("sb_masklog", (128, 128), F32)
    iota8 = sb("sb_iota8", (128, NE), F32)
    convw_sb = sb("convw_sb", (128, MC * 4), F32)
    convb_sb = sb("convb_sb", (128, MC), F32)
    alog_sb = sb("alog_sb", (H, 1), F32)
    dtb_sb = sb("dtb_sb", (H, 1), F32)
    drep = sb("drep", (128, DIN), F32)
    nwrep = sb("nwrep", (128, DIN), F32)
    lngrep = sb("lngrep", (128, D), F32)
    lnbrep = sb("lnbrep", (128, D), F32)
    ones32 = sb("ones32", (H, Q), F32)
    rwT = sb("rwT", (128, 8 * NE), BF16)
    cbias = sb("cbias", (128, 4), F32)

    import contextlib
    ph0 = contextlib.ExitStack()

    def sb0(name, shape, dt, **kw):
        h = ph0.enter_context(nc.sbuf_tensor(name, list(shape), dt, **kw))
        return h.ap()

    q_f32 = [sb0(f"q_f32_{i}", (128, DIN), F32) for i in range(2)]
    q_aw = sb0("q_aw", (128, DIN), F32)
    q_aw2 = sb0("q_aw2", (128, DIN), F32)
    q_sg = sb0("q_sg", (128, DIN), F32)
    q_bf = [sb0(f"q_bf_{i}", (128, DIN), BF16) for i in range(2)]
    q_col = sb0("q_col", (128, 4), F32)
    stage = [sb0(f"stage_{i}", (128, 16 * 128), BF16, is_transpose_dest=True)
             for i in range(2)]
    rwpad = sb0("rwpad", (16, D), BF16, is_transpose_dest=True)
    rwtr = sb0("rwtr", (128, 16), BF16, is_transpose_dest=True)

    def A(e):
        return e

    p.add("sp", lambda e: e.dma_start(out=ident[...], in_=ident_d[...]), writes=["ident"], dma=True)
    p.add("sp", lambda e: e.dma_start(out=masklog[...], in_=masklog_d[...]), writes=["masklog"], dma=True)
    p.add("sp", lambda e: e.dma_start(out=iota8[...], in_=iota8_d[...]), writes=["iota8"], dma=True)
    p.add("sp", lambda e: e.dma_start(out=convw_sb[:, 0:33 * 4].rearrange("p (m k) -> p m k", k=4), in_=convw_d[0:33 * 128, :].rearrange("(m p) k -> p m k", p=128)),
          writes=["convw"], dma=True)
    p.add("sp", lambda e: e.dma_start(out=convw_sb[0:32, 33 * 4:34 * 4], in_=convw_d[33 * 128:CIN, :]),
          writes=["convw"], dma=True)
    p.add("sp", lambda e: e.dma_start(out=convb_sb[:, 0:33], in_=convb_d[0:33 * 128].rearrange("(m p) -> p m", p=128), allow_slow_non_contiguous=True),
          writes=["convb"], dma=True)
    p.add("sp", lambda e: e.dma_start(out=convb_sb[0:32, 33:34], in_=convb_d[33 * 128:CIN].unsqueeze(1)),
          writes=["convb"], dma=True)
    p.add("sp", lambda e: e.dma_start(out=alog_sb[...], in_=alog_d.unsqueeze(1)), writes=["alog"], dma=True)
    p.add("sp", lambda e: e.dma_start(out=dtb_sb[...], in_=dtb_d.unsqueeze(1)), writes=["dtb"], dma=True)
    p.add("act", lambda e: e.dma_start(out=drep[...], in_=dparam_d.unsqueeze(0).to_broadcast([128, DIN])),
          writes=["drep"], dma=True)
    p.add("act", lambda e: e.dma_start(out=nwrep[...], in_=mnw_d.unsqueeze(0).to_broadcast([128, DIN])),
          writes=["nwrep"], dma=True)
    p.add("act", lambda e: e.dma_start(out=lngrep[...], in_=lng_d.unsqueeze(0).to_broadcast([128, D])),
          writes=["lngrep"], dma=True)
    p.add("act", lambda e: e.dma_start(out=lnbrep[...], in_=lnb_d.unsqueeze(0).to_broadcast([128, D])),
          writes=["lnbrep"], dma=True)
    p.add("dve", lambda e: e.memset(ones32[...], 1.0), writes=["ones32"])
    p.add("dve", lambda e: e.memset(cbias[:, 0:1], LN_EPS), writes=["cbias"])
    p.add("dve", lambda e: e.memset(cbias[:, 1:2], 1.0), reads=["cbias"], writes=["cbias"])
    p.add("dve", lambda e: e.memset(cbias[:, 2:3], F32EPS), reads=["cbias"], writes=["cbias"])
    p.add("dve", lambda e: e.memset(convw_sb[32:64, 33 * 4:34 * 4], 0.0), reads=["convw"], writes=["convw"])
    p.add("dve", lambda e: e.memset(convw_sb[64:128, 33 * 4:34 * 4], 0.0), reads=["convw"], writes=["convw"])
    p.add("dve", lambda e: e.memset(convb_sb[32:64, 33:34], 0.0), reads=["convb"], writes=["convb"])
    p.add("dve", lambda e: e.memset(convb_sb[64:128, 33:34], 0.0), reads=["convb"], writes=["convb"])

    p.add("dve", lambda e: e.memset(rwpad[...], 0.0), writes=["rwpad"])
    p.add("pool", lambda e: e.dma_start(out=rwpad[0:NE, :], in_=rw_d[...]), reads=["rwpad"], writes=["rwpad"], dma=True)
    for k in range(8):
        p.add("act", lambda e, k=k: e.dma_start_transpose(rwtr[...], rwpad[:, k * 128:(k + 1) * 128]),
              reads=["rwpad"], writes=["rwtr"], dma=True)
        p.add("dve", lambda e, k=k: e.tensor_copy(rwT[:, k * NE:(k + 1) * NE], rwtr[:, 0:NE]),
              reads=["rwtr"], writes=["rwT"])

    for md in range(8):
        buf = md % 2
        p.add("pool", lambda e, md=md, buf=buf: e.dma_start(out=q_bf[buf][...], in_=w_out_d[md * 128:(md + 1) * 128, :]),
              reads=[f"qbf{buf}"], writes=[f"qbf{buf}"], dma=True)
        for fc in range(16):
            p.add("act", lambda e, md=md, fc=fc, buf=buf:
                  e.dma_start_transpose(stage[buf][:, fc * 128:(fc + 1) * 128],
                                        q_bf[buf][:, fc * 128:(fc + 1) * 128]),
                  reads=[f"qbf{buf}"], writes=[f"stage{buf}"], dma=True)
        p.add("sp", lambda e, md=md, buf=buf:
              e.dma_start(out=w_outT_s[md, :, :], in_=stage[buf][:, 0:16 * 128]),
              reads=[f"stage{buf}"], writes=[f"stage{buf}"], dma=True)

    for mc in range(MC):
        rows = 128 if mc < 33 else 32
        buf = mc % 2
        p.add("pool", lambda e, mc=mc, rows=rows, buf=buf:
              e.dma_start(out=q_bf[buf][0:rows, 0:D], in_=w_in_d[mc * 128:mc * 128 + rows, :]),
              reads=[f"qbf{buf}"], writes=[f"qbf{buf}"], dma=True)
        if rows < 128:
            p.add("dve", lambda e, buf=buf, rows=rows: e.memset(q_bf[buf][rows:64, 0:D], 0.0),
                  reads=[f"qbf{buf}"], writes=[f"qbf{buf}"])
            p.add("dve", lambda e, buf=buf: e.memset(q_bf[buf][64:128, 0:D], 0.0),
                  reads=[f"qbf{buf}"], writes=[f"qbf{buf}"])
        for k in range(8):
            p.add("act", lambda e, mc=mc, k=k, buf=buf:
                  e.dma_start_transpose(stage[buf][:, k * 128:(k + 1) * 128], q_bf[buf][:, k * 128:(k + 1) * 128]),
                  reads=[f"qbf{buf}"], writes=[f"stage{buf}"], dma=True)
        p.add("sp", lambda e, mc=mc, buf=buf:
              e.dma_start(out=w_inT_s[mc, :, 0:8 * 128], in_=stage[buf][:, 0:8 * 128]),
              reads=[f"stage{buf}"], writes=[f"stage{buf}"], dma=True)

    qchunk_i = [0]
    def quant_rows(src_slice, K, buf, tag):
        ci = qchunk_i[0]; qchunk_i[0] += 1
        p.add("sp", lambda e, s=src_slice: e.dma_start(out=q_f32[buf][:, 0:K], in_=s),
              reads=[f"qf{buf}"], writes=[f"qf{buf}"], dma=True)
        p.add("dve", lambda e: e.reduce_sum(q_col[:, 0:1], q_f32[buf][:, 0:K], axis=AX.X, apply_absolute_value=True),
              reads=[f"qf{buf}"], writes=["qcol"])
        p.add("dve", lambda e, K=K: e.tensor_scalar(q_col[:, 1:2], q_col[:, 0:1], 0.5 / K, 0.5e-5, ALU.mult, ALU.max),
              reads=["qcol"], writes=["qcol2"])
        p.add("dve", lambda e, K=K: e.tensor_scalar(q_col[:, 2:3], q_col[:, 0:1], 1.0 / K, 1e-5, ALU.mult, ALU.max),
              reads=["qcol"], writes=["qcol2"])
        p.add("act", lambda e, K=K: e.activation(q_aw[:, 0:K], q_f32[buf][:, 0:K], AF.Abs),
              reads=[f"qf{buf}"], writes=["qaw"])
        p.add("act", lambda e, K=K: e.activation(q_sg[:, 0:K], q_f32[buf][:, 0:K], AF.Sign),
              reads=[f"qf{buf}"], writes=["qsg"])
        p.add("dve", lambda e, K=K: e.tensor_scalar(q_aw2[:, 0:K], q_aw[:, 0:K], q_col[:, 1:2], None, ALU.is_gt),
              reads=["qaw", "qcol2"], writes=["qaw2"])
        p.add("dve", lambda e, K=K: e.scalar_tensor_tensor(q_bf[buf][:, 0:K], q_aw2[:, 0:K], q_col[:, 2:3], q_sg[:, 0:K], ALU.mult, ALU.mult),
              reads=["qaw2", "qsg", "qcol2"], writes=[f"qbf{buf}"])
        if debug:
            p.add("pool", lambda e, ci=ci: e.dma_start(out=scl_dbg[ci, :].unsqueeze(0).rearrange("a p -> (a p)").unsqueeze(1), in_=q_col[:, 2:3]),
                  reads=["qcol2"], dma=True)
            p.add("pool", lambda e, ci=ci: e.dma_start(out=sum_dbg[ci, :].unsqueeze(0).rearrange("a p -> (a p)").unsqueeze(1), in_=q_col[:, 0:1]),
                  reads=["qcol"], dma=True)

    for e_i in range(nexp):
        for fc in range(16):
            buf = fc % 2
            src = (w1_d[e_i, fc * 128:(fc + 1) * 128, :] if e_i < NE else sw1_d[fc * 128:(fc + 1) * 128, :])
            quant_rows(src, D, buf, f"w1_{e_i}_{fc}")
            for k in range(8):
                p.add("act", lambda e, k=k, buf=buf:
                      e.dma_start_transpose(stage[buf][:, k * 128:(k + 1) * 128], q_bf[buf][:, k * 128:(k + 1) * 128]),
                      reads=[f"qbf{buf}"], writes=[f"stage{buf}"], dma=True)
            p.add("sp", lambda e, e_i=e_i, fc=fc, buf=buf:
                  e.dma_start(out=w1T_s[e_i, fc, :, :], in_=stage[buf][:, 0:8 * 128]),
                  reads=[f"stage{buf}"], writes=[f"stage{buf}"], dma=True)
        for md in range(8):
            buf = md % 2
            src = (w2_d[e_i, md * 128:(md + 1) * 128, :] if e_i < NE else sw2_d[md * 128:(md + 1) * 128, :])
            quant_rows(src, DIN, buf, f"w2_{e_i}_{md}")
            for fc in range(16):
                p.add("act", lambda e, fc=fc, buf=buf:
                      e.dma_start_transpose(stage[buf][:, fc * 128:(fc + 1) * 128], q_bf[buf][:, fc * 128:(fc + 1) * 128]),
                      reads=[f"qbf{buf}"], writes=[f"stage{buf}"], dma=True)
            p.add("sp", lambda e, e_i=e_i, md=md, buf=buf:
                  e.dma_start(out=w2T_s[e_i, md, :, :], in_=stage[buf][:, 0:16 * 128]),
                  reads=[f"stage{buf}"], writes=[f"stage{buf}"], dma=True)

    if debug == "phase0":
        dbg("scl", scl_dbg, (256, 128), F32, reads=["qcol2"])
        dbg("sum", sum_dbg, (256, 128), F32, reads=["qcol"])
        dbg("w_inT", w_inT_s[...], (MC, 128, 8 * 128), BF16, reads=["stage0", "stage1"])
        dbg("w1T", w1T_s[...], (nexp, 16, 128, 8 * 128), BF16, reads=["stage0", "stage1"])
        dbg("w2T", w2T_s[...], (nexp, 8, 128, 16 * 128), BF16, reads=["stage0", "stage1"])
        dbg("w_outT_sb", w_outT_s[...], (8, 128, 16 * 128), BF16, reads=["stage0", "stage1"])
        dbg("rwT_sb", rwT[...], (128, 8 * NE), BF16, reads=["rwT"])
        dbg("drep", drep[...], (128, DIN), F32, reads=["drep"])
        dbg("convw", convw_sb[...], (128, MC * 4), F32, reads=["convw"])


    p.barrier()
    ph0.close()
    pa = contextlib.ExitStack()

    def sba(name, shape, dt, **kw):
        h = pa.enter_context(nc.sbuf_tensor(name, list(shape), dt, **kw))
        return h.ap()

    x_tm = sba("x_tm", (128, D), F32)
    t_tm = sba("t_tm", (128, D), F32)
    xn_tm = sba("xn_tm", (128, D), F32)
    h_tm = sba("h_tm", (128, D), F32)
    stat = sba("stat", (128, 16), F32)
    xn_fm = sba("xn_fm", (128, 8 * TILE), BF16)
    h_fm = sba("h_fm", (128, 8 * TILE), BF16)
    sz = sba("sz", (128, 16 * TILE), BF16)
    xh = sba("xh", (128, NCH * DIN), BF16)
    bc = sba("bc", (128, TILE), F32)
    bcn = sba("bcn", (64, TILE), F32)
    dtr = sba("dtr", (H, TILE), F32)
    dtw = sba("dtw", (H, TILE * 7), F32)
    cvp = [sba(f"cv{i}", (128, TILE), F32) for i in range(2)]
    zx = [sba(f"zx{i}", (128, TILE + 3), F32) for i in range(2)]
    tails = sba("tails", (128, MC * 3), F32)
    srep = sba("srep", (128, 16 * Q), F32)
    difft = sba("difft", (128, 16 * Q), F32)
    mt = sba("mt", (128, 16 * Q), BF16)
    cbt = sba("cbt", (128, Q), F32)
    stkT = sba("stkT", (128, 160), F32)
    expsT = sba("expsT", (128, H), F32)
    expsl = sba("expsl", (64, H), F32)
    ysb = sba("ysb", (128, DIN), F32)
    td = sba("td", (128, DIN), F32)
    ygat = sba("ygat", (128, 16 * TILE), BF16)
    attnb = sba("attnb", (128, TILE), F32)
    statebuf = [sba(f"state{i}", (64, DIN), F32) for i in range(2)]
    wt = [sba(f"wt{i}", (128, 8 * 128), BF16) for i in range(2)]
    wo = [sba(f"wo{i}", (128, 16 * 128), BF16) for i in range(2)]
    bw = [sba(f"bw{i}", (128, 64), BF16) for i in range(2)]
    logit = sba("logit", (128, NE), F32)
    gtile = sba("gtile", (8, Q), F32)
    gE = sba("gE", (128, NE), F32)
    ones8 = sba("ones8", (128, NE), F32)
    gF = sba("gF", (128, NE), F32)
    ps = [pa.enter_context(nc.psum_tensor(f"ps{i}", [128, 512], F32)).ap()
          for i in range(8)]

    p.add("dve", lambda e: e.memset(tails[...], 0.0), writes=["tails"])
    p.add("dve", lambda e: e.memset(ones8[...], 1.0), writes=["ones8"])
    p.add("dve", lambda e: e.memset(statebuf[0][...], 0.0), writes=["st0"])

    xn_fm3 = xn_fm.rearrange("p (k t) -> p k t", t=TILE)
    h_fm3 = h_fm.rearrange("p (k t) -> p k t", t=TILE)
    sz3 = sz.rearrange("p (k t) -> p k t", t=TILE)
    xh3 = xh.rearrange("p (s f) -> p s f", f=DIN)
    ygat3 = ygat.rearrange("p (k t) -> p k t", t=TILE)
    dtw3 = dtw.rearrange("p (i t) -> p i t", t=TILE)
    D_DT, D_LN, D_S, D_SML, D_W, D_S1, D_S2 = range(7)
    D_A = D_S1

    def cw_col(mc, k):
        return convw_sb[:, mc * 4 + k:mc * 4 + k + 1]

    def dsl(i, cols=None):
        a = dtw3[:, i, :] if cols is None else dtw3[:, i, cols]
        return a

    def ln_and_transpose(it, s):
        t0 = it * TILE
        p.add("sp", lambda e, t0=t0, s=s: e.dma_start(out=x_tm[...], in_=x_d[t0 + s * 128:t0 + (s + 1) * 128, :]),
              reads=["x_tm"], writes=["x_tm"], dma=True)
        p.add("dve", lambda e: e.reduce_sum(stat[:, 0:1], x_tm[...], axis=AX.X), reads=["x_tm"], writes=["stat"])
        p.add("dve", lambda e: e.tensor_scalar_mul(stat[:, 1:2], stat[:, 0:1], 1.0 / D), reads=["stat"], writes=["stat"])
        p.add("dve", lambda e: e.tensor_scalar(t_tm[...], x_tm[...], stat[:, 1:2], None, ALU.subtract),
              reads=["x_tm", "stat"], writes=["t_tm"])
        p.add("act", lambda e: e.activation(xn_tm[...], t_tm[...], AF.Square, accum_out=stat[:, 2:3]),
              reads=["t_tm"], writes=["xn_tm", "stat"])
        p.add("act", lambda e: e.activation(stat[:, 3:4], stat[:, 2:3], AF.Ln, bias=cbias[:, 0:1], scale=1.0 / D),
              reads=["stat", "cbias"], writes=["stat"])
        p.add("act", lambda e: e.activation(stat[:, 4:5], stat[:, 3:4], AF.Exp, scale=-0.5),
              reads=["stat"], writes=["stat"])
        p.add("dve", lambda e: e.tensor_scalar_mul(xn_tm[...], t_tm[...], stat[:, 4:5]),
              reads=["t_tm", "stat"], writes=["xn_tm"])
        p.add("dve", lambda e: e.reduce_sum(stat[:, 5:6], xn_tm[...], axis=AX.X), reads=["xn_tm"], writes=["stat"])
        p.add("dve", lambda e: e.tensor_scalar_mul(stat[:, 6:7], stat[:, 5:6], 1.0 / D), reads=["stat"], writes=["stat"])
        p.add("dve", lambda e: e.tensor_scalar(t_tm[...], xn_tm[...], stat[:, 6:7], None, ALU.subtract),
              reads=["xn_tm", "stat"], writes=["t_tm"])
        p.add("act", lambda e: e.activation(h_tm[...], t_tm[...], AF.Square, accum_out=stat[:, 7:8]),
              reads=["t_tm"], writes=["h_tm", "stat"])
        p.add("act", lambda e: e.activation(stat[:, 8:9], stat[:, 7:8], AF.Ln, bias=cbias[:, 0:1], scale=1.0 / D),
              reads=["stat", "cbias"], writes=["stat"])
        p.add("act", lambda e: e.activation(stat[:, 9:10], stat[:, 8:9], AF.Exp, scale=-0.5),
              reads=["stat"], writes=["stat"])
        p.add("dve", lambda e: e.scalar_tensor_tensor(h_tm[...], t_tm[...], stat[:, 9:10], lngrep[...], ALU.mult, ALU.mult),
              reads=["t_tm", "stat", "lngrep"], writes=["h_tm"])
        p.add("dve", lambda e: e.tensor_add(t_tm[...], h_tm[...], lnbrep[...]),
              reads=["h_tm", "lnbrep"], writes=["t_tm"])
        for src_ap, src_b, dst3, dst_b in ((xn_tm, "xn_tm", xn_fm3, "xn_fm"),
                                           (t_tm, "t_tm", h_fm3, "h_fm")):
            for half in range(2):
                for k4 in range(4):
                    k = half * 4 + k4
                    p.add("pe", lambda e, src_ap=src_ap, k=k, k4=k4:
                          e.transpose(ps[2][:, k4 * 128:(k4 + 1) * 128], src_ap[:, k * 128:(k + 1) * 128], ident[...]),
                          reads=[src_b, "ident"], writes=["ps2"])
                p.add("dve", lambda e, dst3=dst3, half=half, s=s:
                      e.tensor_copy(dst3[:, half * 4:(half + 1) * 4, s * 128:(s + 1) * 128],
                                    ps[2][:, 0:512].rearrange("p (a b) -> p a b", b=128)),
                      reads=["ps2"], writes=[dst_b])
        for k in range(8):
            p.add("pe", lambda e, k=k, s=s: e.matmul(ps[2][:, 0:8], h_fm3[:, k, s * 128:(s + 1) * 128],
                                                     rwT[:, k * NE:(k + 1) * NE], start=(k == 0), stop=(k == 7)),
                  reads=["h_fm", "rwT"], writes=["ps2"])
        p.add("dve", lambda e: e.tensor_copy(logit[...], ps[2][:, 0:8]), reads=["ps2"], writes=["logit"])
        p.add("dve", lambda e: e.reduce_max(stat[:, 10:11], logit[...], axis=AX.X), reads=["logit"], writes=["stat"])
        p.add("dve", lambda e: e.tensor_scalar_mul(stat[:, 11:12], stat[:, 10:11], -1.0), reads=["stat"], writes=["stat"])
        p.add("act", lambda e: e.activation(gE[...], logit[...], AF.Exp, bias=stat[:, 11:12], accum_out=stat[:, 12:13]),
              reads=["logit", "stat"], writes=["gE", "stat"])
        p.add("dve", lambda e: e.reciprocal(stat[:, 13:14], stat[:, 12:13]), reads=["stat"], writes=["stat"])
        p.add("dve", lambda e: e.tensor_scalar(gE[...], logit[...], stat[:, 10:11], None, ALU.is_equal),
              reads=["logit", "stat"], writes=["gE"])
        p.add("dve", lambda e: e.tensor_tensor_scan(gF[...], ones8[...], gE[...], 0.0, ALU.mult, ALU.add),
              reads=["gE", "ones8"], writes=["gF"])
        p.add("dve", lambda e: e.tensor_scalar(gF[...], gF[...], 1.0, None, ALU.is_equal),
              reads=["gF"], writes=["gF"])
        p.add("dve", lambda e: e.tensor_mul(gE[...], gE[...], gF[...]), reads=["gE", "gF"], writes=["gE"])
        p.add("dve", lambda e: e.tensor_scalar_mul(gF[...], gE[...], stat[:, 13:14]), reads=["gE", "stat"], writes=["gF"])
        p.add("pe", lambda e: e.transpose(ps[2][0:8, 0:128], gF[...], ident[...]), reads=["gF", "ident"], writes=["ps2"])
        p.add("dve", lambda e: e.tensor_copy(gtile[...], ps[2][0:8, 0:128]), reads=["ps2"], writes=["gtile"])
        p.add("sp", lambda e, t0=t0, s=s: e.dma_start(out=g_s[:, t0 + s * 128:t0 + (s + 1) * 128], in_=gtile[...]),
              reads=["gtile"], dma=True)
        if debug == "phaseA2" and it == 0 and s == 0:
            dbg("logit", logit[...], (128, NE), F32, reads=["logit"])
            dbg("gE", gE[...], (128, NE), F32, reads=["gE"])

            dbg("gF", gF[...], (128, NE), F32, reads=["gF"])
            dbg("gtile", gtile[...], (8, Q), F32, reads=["gtile"])
            dbg("statv", stat[...], (128, 16), F32, reads=["stat"])

    def in_proj_conv(it):
        if debug == "phaseA1" and it == 0:
            dbg("xnfm", xn_fm[...], (128, 8 * TILE), BF16, reads=["xn_fm"])
        for mc in range(MC):
            b = mc % 2
            p.add("sp", lambda e, mc=mc, b=b: e.dma_start(out=wt[b][...], in_=w_inT_s[mc, :, :]),
                  reads=[f"wt{b}"], writes=[f"wt{b}"], dma=True)
            if debug == "phaseA1" and it == 0 and mc == 0:
                dbg("wt0", wt[b][...], (128, 8 * 128), BF16, reads=[f"wt{b}"])
            for k in range(8):
                p.add("pe", lambda e, k=k, b=b: e.matmul(ps[b][:, 0:512], wt[b][:, k * 128:(k + 1) * 128],
                                                         xn_fm3[:, k, :], start=(k == 0), stop=(k == 7)),
                      reads=[f"wt{b}", "xn_fm"], writes=[f"ps{b}"])
            p.add("dve", lambda e, mc=mc, b=b: e.tensor_copy(zx[b][:, 0:3], tails[:, mc * 3:(mc + 1) * 3]),
                  reads=["tails"], writes=[f"zx{b}"])
            p.add("act", lambda e, b=b: e.copy(zx[b][:, 3:3 + TILE], ps[b][:, 0:512]),
                  reads=[f"ps{b}"], writes=[f"zx{b}"])
            p.add("dve", lambda e, mc=mc, b=b: e.tensor_copy(tails[:, mc * 3:(mc + 1) * 3], zx[b][:, TILE:TILE + 3]),
                  reads=[f"zx{b}"], writes=["tails"])
            p.add("dve", lambda e, mc=mc, b=b:
                  e.scalar_tensor_tensor(cvp[0][...], zx[b][:, 0:TILE], cw_col(mc, 0),
                                         convb_sb[:, mc:mc + 1].to_broadcast([128, TILE]), ALU.mult, ALU.add),
                  reads=[f"zx{b}", "convw", "convb"], writes=["cv0"])
            for kk in range(1, 4):
                src_cv, dst_cv = (kk + 1) % 2, kk % 2
                p.add("dve", lambda e, mc=mc, b=b, kk=kk, src_cv=src_cv, dst_cv=dst_cv:
                      e.scalar_tensor_tensor(cvp[dst_cv][...], zx[b][:, kk:kk + TILE], cw_col(mc, kk),
                                             cvp[src_cv][...], ALU.mult, ALU.add),
                      reads=[f"zx{b}", "convw", f"cv{src_cv}"], writes=[f"cv{dst_cv}"])
            if debug == "phaseA1" and it == 0 and mc in (0, 16, 32, 33):
                dbg(f"zx{mc}", zx[b][...], (128, TILE + 3), F32, reads=[f"zx{b}"])
                dbg(f"cv{mc}", cvp[1][...], (128, TILE), F32, reads=["cv1"])
            if mc < 16:
                p.add("act", lambda e, mc=mc: e.activation(sz3[:, mc, :], cvp[1][...], AF.Silu),
                      reads=["cv1"], writes=["sz"])
            elif mc < 32:
                for s4 in range(4):
                    p.add("pe", lambda e, s4=s4: e.transpose(ps[2][:, s4 * 128:(s4 + 1) * 128],
                                                             cvp[1][:, s4 * 128:(s4 + 1) * 128], ident[...]),
                          reads=["cv1", "ident"], writes=["ps2"])
                p.add("dve", lambda e, mc=mc: e.tensor_copy(xh3[:, :, (mc - 16) * 128:(mc - 15) * 128],
                                                            ps[2][:, 0:512].rearrange("p (a b) -> p a b", b=128)),
                      reads=["ps2"], writes=["xh"])
            elif mc == 32:
                p.add("dve", lambda e: e.tensor_copy(bc[...], cvp[1][...]), reads=["cv1"], writes=["bc"])
                p.add("sp", lambda e: e.dma_start(out=bcC_s[...], in_=bc[64:128, :]),
                      reads=["bc"], writes=["bcCs"], dma=True)
                p.add("sp", lambda e: e.dma_start(out=bcn[...], in_=bcC_s[...]),
                      reads=["bcCs", "bcn"], writes=["bcn"], dma=True)
            else:
                p.add("dve", lambda e: e.tensor_copy(dtr[...], cvp[1][0:H, :]), reads=["cv1"], writes=["dtr"])

    def dt_pipeline(it):
        p.add("act", lambda e: e.activation(dsl(D_S1), dtr[...], AF.Abs, bias=dtb_sb[:, 0:1]),
              reads=["dtr", "dtb"], writes=["dtw"])
        p.add("act", lambda e: e.activation(dsl(D_S2), dsl(D_S1), AF.Exp, scale=-1.0), reads=["dtw"], writes=["dtw"])
        p.add("act", lambda e: e.activation(dsl(D_S1), dsl(D_S2), AF.Ln, bias=cbias[0:H, 1:2]), reads=["dtw", "cbias"], writes=["dtw"])
        p.add("act", lambda e: e.activation(dsl(D_S2), dtr[...], AF.Relu, bias=dtb_sb[:, 0:1]),
              reads=["dtr", "dtb"], writes=["dtw"])
        p.add("dve", lambda e: e.tensor_add(dsl(D_DT), dsl(D_S1), dsl(D_S2)), reads=["dtw"], writes=["dtw"])
        p.add("dve", lambda e: e.tensor_scalar_mul(dsl(D_A), dsl(D_DT), alog_sb[:, 0:1]),
              reads=["dtw", "alog"], writes=["dtw"])
        p.add("act", lambda e: e.activation(dsl(D_LN), dsl(D_DT), AF.Ln), reads=["dtw"], writes=["dtw"])
        for c in range(NCH):
            cc = slice(c * Q, (c + 1) * Q)
            p.add("dve", lambda e, cc=cc: e.tensor_tensor_scan(dsl(D_S, cc), ones32[0:H, 0:Q], dsl(D_A, cc),
                                                               0.0, ALU.mult, ALU.add),
                  reads=["dtw", "ones32"], writes=["dtw"])
        p.add("dve", lambda e: e.tensor_sub(dsl(D_SML), dsl(D_S), dsl(D_LN)), reads=["dtw"], writes=["dtw"])
        for c in range(NCH):
            cc = slice(c * Q, (c + 1) * Q)
            p.add("dve", lambda e, c=c, cc=cc: e.tensor_scalar(dsl(D_S1, cc), dsl(D_S, cc),
                                                               dtw3[:, D_S, c * Q + Q - 1:c * Q + Q], None, ALU.subtract),
                  reads=["dtw"], writes=["dtw"])
            p.add("act", lambda e, cc=cc: e.activation(dsl(D_S2, cc), dsl(D_S1, cc), AF.Exp, scale=-1.0),
                  reads=["dtw"], writes=["dtw"])
            p.add("dve", lambda e, cc=cc: e.tensor_mul(dsl(D_W, cc), dsl(D_DT, cc), dsl(D_S2, cc)),
                  reads=["dtw"], writes=["dtw"])

    def chunk_scan(it, c):
        gc = it * NCH + c
        slot = gc % 2
        sr, sw = f"st{gc % 2}", f"st{(gc + 1) % 2}"
        cc = slice(c * Q, (c + 1) * Q)
        wr_s = p.add("sp", lambda e, slot=slot, cc=cc:
                     e.dma_start(out=srep_s[slot, :].unsqueeze(0).rearrange("a (h q) -> (a h) q", q=Q), in_=dsl(D_S, cc)),
                     reads=["dtw", f"sreps{slot}"], writes=[f"sreps{slot}"], dma=True)
        p.add("pe", lambda e, cc=cc: e.transpose(ps[2][:, 0:64], bc[0:64, cc], ident[0:64, 0:64]),
              reads=["bc", "ident"], writes=["ps2"])
        p.add("pe", lambda e, cc=cc: e.transpose(ps[2][:, 64:96], dsl(D_W, cc), ident[0:H, 0:H]),
              reads=["dtw", "ident"], writes=["ps2"])
        p.add("pe", lambda e, cc=cc: e.transpose(ps[2][:, 96:128], dsl(D_SML, cc), ident[0:H, 0:H]),
              reads=["dtw", "ident"], writes=["ps2"])
        p.add("pe", lambda e, cc=cc: e.transpose(ps[2][:, 128:160], dsl(D_S, cc), ident[0:H, 0:H]),
              reads=["dtw", "ident"], writes=["ps2"])
        p.add("dve", lambda e: e.tensor_copy(stkT[...], ps[2][:, 0:160]), reads=["ps2"], writes=["stkT"])
        p.add("act", lambda e: e.activation(expsT[...], stkT[:, 128:160], AF.Exp), reads=["stkT"], writes=["expsT"])
        p.add("pe", lambda e, cc=cc: e.matmul(ps[2][:, 0:128], bc[0:64, cc], bcn[0:64, cc], start=True, stop=True),
              reads=["bc", "bcn"], writes=["ps2"])
        p.add("act", lambda e: e.copy(cbt[...], ps[2][:, 0:128]), reads=["ps2"], writes=["cbt"])
        for half in range(2):
            rd = p.add("act", lambda e, slot=slot, half=half:
                       e.dma_start(out=srep[...], in_=srep_s[slot, half * 16 * Q:(half + 1) * 16 * Q]
                                   .unsqueeze(0).to_broadcast([128, 16 * Q])),
                       reads=[f"sreps{slot}", "srep"], writes=["srep"], dma=True, extra_deps=[wr_s])
            srep3 = srep.rearrange("p (h q) -> p h q", q=Q)
            p.add("act", lambda e, half=half, srep3=srep3:
                  e.activation(expsl[:, half * 16:(half + 1) * 16], srep3[0:64, :, Q - 1], AF.Exp),
                  reads=["srep"], writes=["expsl"])
            for hh in range(16):
                h_ = half * 16 + hh
                p.add("dve", lambda e, hh=hh, h_=h_:
                      e.scalar_tensor_tensor(difft[:, hh * Q:(hh + 1) * Q], srep[:, hh * Q:(hh + 1) * Q],
                                             stkT[:, 96 + h_:97 + h_], masklog[...], ALU.subtract, ALU.add),
                      reads=["srep", "stkT", "masklog"], writes=["difft"])
            p.add("act", lambda e: e.activation(srep[...], difft[...], AF.Exp), reads=["difft"], writes=["srep"])
            for hh in range(16):
                h_ = half * 16 + hh
                p.add("dve", lambda e, hh=hh, h_=h_: e.tensor_mul(mt[:, hh * Q:(hh + 1) * Q],
                                                                  srep[:, hh * Q:(hh + 1) * Q], cbt[...]),
                      reads=["srep", "cbt"], writes=["mt"])
            chunk_headgroups(c, gc, cc, sr, sw, half)
        p.add("pool", lambda e, c=c: e.tensor_mul(td[...], xh3[:, c, :], drep[...]), reads=["xh", "drep"], writes=["td"])
        p.add("pool", lambda e: e.tensor_add(difft[:, 0:DIN], ysb[...], td[...]), reads=["ysb", "td"], writes=["difft"])
        p.add("act", lambda e: e.activation(td[...], difft[:, 0:DIN], AF.Square, accum_out=stat[:, 14:15]),
              reads=["difft"], writes=["td", "stat"])
        p.add("act", lambda e: e.activation(stat[:, 15:16], stat[:, 14:15], AF.Ln, bias=cbias[:, 2:3], scale=1.0 / DIN),
              reads=["stat", "cbias"], writes=["stat"])
        p.add("act", lambda e: e.activation(stat[:, 14:15], stat[:, 15:16], AF.Exp, scale=-0.5),
              reads=["stat"], writes=["stat"])
        p.add("dve", lambda e: e.scalar_tensor_tensor(ysb[...], difft[:, 0:DIN], stat[:, 14:15], nwrep[...],
                                                      ALU.mult, ALU.mult),
              reads=["difft", "stat", "nwrep"], writes=["ysb"])
        for q4 in range(4):
            for j in range(4):
                fc = q4 * 4 + j
                p.add("pe", lambda e, fc=fc, j=j: e.transpose(ps[6][:, j * 128:(j + 1) * 128],
                                                              ysb[:, fc * 128:(fc + 1) * 128], ident[...]),
                      reads=["ysb", "ident"], writes=["ps6"])
            p.add("dve", lambda e, q4=q4, c=c:
                  e.tensor_mul(ygat3[:, q4 * 4:(q4 + 1) * 4, c * 128:(c + 1) * 128],
                               ps[6][:, 0:512].rearrange("p (a b) -> p a b", b=128),
                               sz3[:, q4 * 4:(q4 + 1) * 4, c * 128:(c + 1) * 128]),
                  reads=["ps6", "sz"], writes=["ygat"])
        if debug == "phaseA2" and it == 0 and c == 0:
            dbg("stkT", stkT[...], (128, 160), F32, reads=["stkT"])
            dbg("cbt", cbt[...], (128, Q), F32, reads=["cbt"])
            dbg("mtd", mt[...], (128, 16 * Q), BF16, reads=["mt"])
            dbg("ysbd", ysb[...], (128, DIN), F32, reads=["ysb"])
            dbg("st1d", statebuf[1][...], (64, DIN), F32, reads=["st1"])
            dbg("expsTd", expsT[...], (128, H), F32, reads=["expsT"])
            dbg("expsld", expsl[...], (64, H), F32, reads=["expsl"])
            dbg("bcd", bc[...], (128, TILE), F32, reads=["bc"])
            dbg("bcnd", bcn[...], (64, TILE), F32, reads=["bcn"])


    def chunk_headgroups(c, gc, cc, sr, sw, half):
        for hg in (half * 2, half * 2 + 1):
            for j in range(8):
                h_ = hg * 8 + j
                p.add("pe", lambda e, h_=h_, j=j, c=c: e.matmul(ps[3][:, j * 64:(j + 1) * 64],
                                                                mt[:, (h_ % 16) * Q:((h_ % 16) + 1) * Q],
                                                                xh3[:, c, h_ * 64:(h_ + 1) * 64], start=True, stop=True),
                      reads=["mt", "xh"], writes=["ps3"])
                p.add("pe", lambda e, h_=h_, j=j, cc=cc, sr=sr: e.matmul(ps[4][:, j * 64:(j + 1) * 64],
                                                                         bcn[0:64, cc],
                                                                         statebuf[gc % 2][:, h_ * 64:(h_ + 1) * 64],
                                                                         start=True, stop=True),
                      reads=["bcn", sr], writes=["ps4"])
                p.add("dve", lambda e, h_=h_, j=j: e.tensor_scalar_mul(bw[h_ % 2][...], stkT[:, 0:64],
                                                                       stkT[:, 64 + h_:65 + h_]),
                      reads=["stkT"], writes=[f"bw{h_ % 2}"])
                p.add("pe", lambda e, h_=h_, j=j, c=c: e.matmul(ps[5][0:64, j * 64:(j + 1) * 64],
                                                                bw[h_ % 2][...],
                                                                xh3[:, c, h_ * 64:(h_ + 1) * 64], start=True, stop=True),
                      reads=[f"bw{h_ % 2}", "xh"], writes=["ps5"])
            p.add("act", lambda e, hg=hg: e.copy(ysb[:, hg * 512:(hg + 1) * 512], ps[3][:, 0:512]),
                  reads=["ps3"], writes=["ysb"])
            for j in range(8):
                h_ = hg * 8 + j
                p.add("dve", lambda e, h_=h_, j=j:
                      e.scalar_tensor_tensor(ysb[:, h_ * 64:(h_ + 1) * 64], ps[4][:, j * 64:(j + 1) * 64],
                                             expsT[:, h_:h_ + 1], ysb[:, h_ * 64:(h_ + 1) * 64], ALU.mult, ALU.add),
                      reads=["ps4", "expsT", "ysb"], writes=["ysb"])
                p.add("dve", lambda e, h_=h_, j=j, gc=gc:
                      e.scalar_tensor_tensor(statebuf[(gc + 1) % 2][:, h_ * 64:(h_ + 1) * 64],
                                             statebuf[gc % 2][:, h_ * 64:(h_ + 1) * 64],
                                             expsl[:, h_:h_ + 1], ps[5][0:64, j * 64:(j + 1) * 64], ALU.mult, ALU.add),
                      reads=[sr, "expsl", "ps5"], writes=[sw])
    def out_proj(it):
        t0 = it * TILE
        for md in range(8):
            b = md % 2
            p.add("sp", lambda e, md=md, b=b: e.dma_start(out=wo[b][...], in_=w_outT_s[md, :, :]),
                  reads=[f"wo{b}"], writes=[f"wo{b}"], dma=True)
            for fc in range(16):
                p.add("pe", lambda e, fc=fc, b=b: e.matmul(ps[7][:, 0:512], wo[b][:, fc * 128:(fc + 1) * 128],
                                                           ygat3[:, fc, :], start=(fc == 0), stop=(fc == 15)),
                      reads=[f"wo{b}", "ygat"], writes=["ps7"])
            p.add("act", lambda e, md=md: e.copy(attnb[...], ps[7][:, 0:512]),
                  reads=["ps7"], writes=["attnb"])
            p.add("sp", lambda e, md=md, t0=t0: e.dma_start(out=attn_s[md, :, t0:t0 + TILE], in_=attnb[...]),
                  reads=["attnb"], dma=True)

    for it in range(ntiles):
        for s in range(NCH):
            ln_and_transpose(it, s)
        in_proj_conv(it)
        dt_pipeline(it)
        for c in range(NCH):
            chunk_scan(it, c)
        out_proj(it)
        t0 = it * TILE
        for k in range(8):
            p.add("act", lambda e, k=k, t0=t0: e.dma_start(out=h_s[k, :, t0:t0 + TILE], in_=h_fm3[:, k, :]),
                  reads=["h_fm"], dma=True)

    if debug in ("phaseA", "phaseA1", "phaseA2"):
        p.barrier()
        dbg("attn", attn_s[...], (8, 128, T), F32)
        dbg("h_s", h_s[...], (8, 128, T), BF16)
        dbg("g_s", g_s[...], (NE, T), F32)
        dbg("sz", sz[...], (128, 16 * TILE), BF16, reads=["sz"])
        dbg("xh", xh[...], (128, NCH * DIN), BF16, reads=["xh"])
        dbg("dtw", dtw[...], (H, TILE * 7), F32, reads=["dtw"])
        dbg("state", statebuf[(ntiles * NCH) % 2][...], (64, DIN), F32, reads=[f"st{(ntiles * NCH) % 2}"])


    p.barrier()
    pa.close()
    pb = contextlib.ExitStack()

    def sbb_(name, shape, dt, **kw):
        h = pb.enter_context(nc.sbuf_tensor(name, list(shape), dt, **kw))
        return h.ap()

    acc = sbb_("acc", (128, 8 * TB), F32)
    hb = sbb_("hb", (128, 8 * TB), BF16)
    h1g = sbb_("h1g", (128, 16 * TB), BF16)
    w1b = [sbb_(f"w1b{i}", (128, 8 * 128), BF16) for i in range(2)]
    w2b = [sbb_(f"w2b{i}", (128, 16 * 128), BF16) for i in range(2)]
    grep = [sbb_(f"grep{i}", (128, TB), F32) for i in range(2)]
    h1t = [sbb_(f"h1t{i}", (128, 512), BF16) for i in range(2)]
    xb = sbb_("xb", (128, D), F32)
    bident = sbb_("bident", (128, 128), F32)
    psb = [pb.enter_context(nc.psum_tensor(f"pb{i}", [128, 512], F32)).ap()
           for i in range(6)]
    acc3 = acc.rearrange("p (k t) -> p k t", t=TB)
    hb3 = hb.rearrange("p (k t) -> p k t", t=TB)
    h1g3 = h1g.rearrange("p (k t) -> p k t", t=TB)

    p.add("sp", lambda e: e.dma_start(out=bident[...], in_=ident_d[...]), writes=["bident"], dma=True)

    for bi in range(nblocks):
        t0 = bi * TB
        for md in range(8):
            p.add("sp", lambda e, md=md, t0=t0: e.dma_start(out=acc3[:, md, :], in_=attn_s[md, :, t0:t0 + TB]),
                  reads=["acc"], writes=["acc"], dma=True)
            p.add("act", lambda e, md=md, t0=t0: e.dma_start(out=hb3[:, md, :], in_=h_s[md, :, t0:t0 + TB]),
                  reads=["hb"], writes=["hb"], dma=True)
        for s in range(TB // 128):
            p.add("sp", lambda e, s=s, t0=t0: e.dma_start(out=xb[...], in_=x_d[t0 + s * 128:t0 + (s + 1) * 128, :]),
                  reads=["xb"], writes=["xb"], dma=True)
            for half in range(2):
                for k4 in range(4):
                    k = half * 4 + k4
                    p.add("pe", lambda e, k=k, k4=k4: e.transpose(psb[2][:, k4 * 128:(k4 + 1) * 128],
                                                                  xb[:, k * 128:(k + 1) * 128], bident[...]),
                          reads=["xb", "bident"], writes=["pb2"])
                p.add("dve", lambda e, half=half, s=s:
                      e.tensor_add(acc3[:, half * 4:(half + 1) * 4, s * 128:(s + 1) * 128],
                                   psb[2][:, 0:512].rearrange("p (a b) -> p a b", b=128),
                                   acc3[:, half * 4:(half + 1) * 4, s * 128:(s + 1) * 128]),
                      reads=["pb2", "acc"], writes=["acc"])
        for e_i in range(nexp):
            ge = e_i % 2
            if e_i < NE:
                p.add("sp", lambda e, e_i=e_i, ge=ge, t0=t0:
                      e.dma_start(out=grep[ge][...], in_=g_s[e_i, t0:t0 + TB].unsqueeze(0).to_broadcast([128, TB])),
                      reads=[f"grep{ge}"], writes=[f"grep{ge}"], dma=True)
            else:
                p.add("dve", lambda e, ge=ge: e.memset(grep[ge][...], 1.0),
                      reads=[f"grep{ge}"], writes=[f"grep{ge}"])
            for fc in range(16):
                wb = fc % 2
                p.add("sp", lambda e, e_i=e_i, fc=fc, wb=wb: e.dma_start(out=w1b[wb][...], in_=w1T_s[e_i, fc, :, :]),
                      reads=[f"w1b{wb}"], writes=[f"w1b{wb}"], dma=True)
                for ts2 in range(TB // 512):
                    pbk = f"pb{ts2}"
                    for k in range(8):
                        p.add("pe", lambda e, fc=fc, wb=wb, ts2=ts2, k=k:
                              e.matmul(psb[ts2][:, 0:512], w1b[wb][:, k * 128:(k + 1) * 128],
                                       hb3[:, k, ts2 * 512:(ts2 + 1) * 512], start=(k == 0), stop=(k == 7)),
                              reads=[f"w1b{wb}", "hb"], writes=[pbk])
                    p.add("act", lambda e, ts2=ts2: e.activation(h1t[ts2][...], psb[ts2][:, 0:512], AF.Silu),
                          reads=[pbk], writes=[f"h1t{ts2}"])
                    p.add("dve", lambda e, fc=fc, ts2=ts2, ge=ge:
                          e.tensor_mul(h1g3[:, fc, ts2 * 512:(ts2 + 1) * 512], h1t[ts2][...],
                                       grep[ge][:, ts2 * 512:(ts2 + 1) * 512]),
                          reads=[f"h1t{ts2}", f"grep{ge}"], writes=["h1g"])
            for md in range(8):
                wb = md % 2
                p.add("act", lambda e, e_i=e_i, md=md, wb=wb: e.dma_start(out=w2b[wb][...], in_=w2T_s[e_i, md, :, :]),
                      reads=[f"w2b{wb}"], writes=[f"w2b{wb}"], dma=True)
                for ts2 in range(TB // 512):
                    pbk = f"pb{4 + md % 2}"
                    for fc in range(16):
                        p.add("pe", lambda e, md=md, wb=wb, ts2=ts2, fc=fc:
                              e.matmul(psb[4 + md % 2][:, 0:512], w2b[wb][:, fc * 128:(fc + 1) * 128],
                                       h1g3[:, fc, ts2 * 512:(ts2 + 1) * 512], start=(fc == 0), stop=(fc == 15)),
                              reads=[f"w2b{wb}", "h1g"], writes=[pbk])
                    p.add("dve", lambda e, md=md, ts2=ts2:
                          e.tensor_add(acc3[:, md, ts2 * 512:(ts2 + 1) * 512], psb[4 + md % 2][:, 0:512],
                                       acc3[:, md, ts2 * 512:(ts2 + 1) * 512]),
                          reads=[pbk, "acc"], writes=["acc"])
        for md in range(8):
            p.add("sp", lambda e, md=md, t0=t0: e.dma_start(out=out_d[md, :, t0:t0 + TB], in_=acc3[:, md, :]),
                  reads=["acc"], dma=True)

    with nc.Block() as block:
        stack = p.emit(nc, block)
    stack.close()
    return nc, dbg_outs


def _mk_consts():
    ident = np.eye(128, dtype=np.float32)
    j = np.arange(128)[:, None]
    i = np.arange(128)[None, :]
    masklogT = np.where(i >= j, 0.0, NEG_BIG).astype(np.float32)
    iota8 = np.broadcast_to(np.arange(NE, dtype=np.float32)[None, :], (128, NE)).copy()
    return ident, masklogT, iota8


def _core_inputs(inputs, b):
    ident, masklogT, iota8 = _mk_consts()
    return {
        "x": np.ascontiguousarray(inputs["L0_attn_u"][b]),
        "w_in": inputs["L0_attn_in_proj_w"],
        "conv_w": inputs["L0_attn_conv_w"],
        "conv_b": inputs["L0_attn_conv_b"],
        "A_log": inputs["L0_attn_A_log"],
        "Dparam": inputs["L0_attn_Dparam"],
        "dt_bias": inputs["L0_attn_dt_bias"],
        "mnorm_w": inputs["L0_attn_norm_w"],
        "w_out": inputs["L0_attn_out_proj_w"],
        "w1": inputs["L0_ffn_expert_w1"],
        "w2": inputs["L0_ffn_expert_w2"],
        "sw1": inputs["L0_ffn_shared_w1"],
        "sw2": inputs["L0_ffn_shared_w2"],
        "rw": inputs["L0_ffn_router_w"],
        "ln_g": inputs["L0_ffn_ln_gamma"],
        "ln_b": inputs["L0_ffn_ln_beta"],
        "ident": ident,
        "masklogT": masklogT,
        "iota8": iota8,
    }


def kernel(**inputs):
    nc, _ = build_program()
    in_maps = [_core_inputs(inputs, b) for b in range(8)]
    res = run_bass_kernel_spmd(nc, in_maps, core_ids=list(range(8)))
    outs = []
    for r in res.results:
        o = np.asarray(r["out_fm"])
        outs.append(o.reshape(D, T).T)
    return np.stack(outs).astype(np.float32)


# version 02
# speedup vs baseline: 1.0548x; 1.0548x over previous
import os
import numpy as np
import ml_dtypes
import concourse.bass as bass
import concourse.mybir as mybir
from concourse.bass_utils import run_bass_kernel_spmd

F32 = mybir.dt.float32
BF16 = mybir.dt.bfloat16
AF = mybir.ActivationFunctionType
ALU = mybir.AluOpType
AX = mybir.AxisListType

T = 8192
D = 1024
CIN = 4256
DIN = 2048
H = 32
PDIM = 64
NST = 64
Q = 128
NE = 8
TILE = 512
NTILES = T // TILE
NCH = TILE // Q
MC = CIN // 128 + (1 if CIN % 128 else 0)
TB = 1024
NB = T // TB
LN_EPS = 1e-5
F32EPS = float(np.finfo(np.float32).eps)
NEG_BIG = -1e30

DEBUG = os.environ.get("KDEBUG", "")


ENGS = ("pe", "act", "dve", "pool", "sp")
DMAQ = ("sp", "act", "pool")


class Prog:
    def __init__(self):
        self.nodes = []
        self.last_w = {}
        self.readers = {}
        self.bar = None

    def barrier(self):
        last = {}
        for i, n in enumerate(self.nodes):
            last[(n["eng"], n["dma"])] = i
        self.bar = None
        nid = self.add("sp", lambda e: e.nop(), extra_deps=list(last.values()))
        self.bar = nid
        return nid

    def add(self, eng, fn, reads=(), writes=(), dma=False, extra_deps=()):
        deps = set(int(d) for d in extra_deps if d is not None)
        if self.bar is not None:
            deps.add(self.bar)
        for b in reads:
            if b in self.last_w:
                deps.add(self.last_w[b])
        for b in writes:
            if b in self.last_w:
                deps.add(self.last_w[b])
            for r in self.readers.get(b, ()):
                deps.add(r)
        nid = len(self.nodes)
        deps = sorted(d for d in deps if d < nid)
        self.nodes.append(dict(eng=eng, fn=fn, dma=bool(dma), deps=deps))
        for b in reads:
            self.readers.setdefault(b, []).append(nid)
        for b in writes:
            self.last_w[b] = nid
            self.readers[b] = []
        return nid

    def emit(self, nc, block):
        nodes = self.nodes
        need_inc = [False] * len(nodes)
        for n in nodes:
            for d in n["deps"]:
                if not nodes[d]["dma"]:
                    need_inc[d] = True
        eng_inc_prefix = {}
        dma_prefix = {}
        cnt = {e: 0 for e in ENGS}
        dcnt = {e: 0 for e in ENGS}
        for i, n in enumerate(nodes):
            if n["dma"]:
                dcnt[n["eng"]] += 1
                dma_prefix[i] = dcnt[n["eng"]]
            else:
                if need_inc[i]:
                    cnt[n["eng"]] += 1
                eng_inc_prefix[i] = cnt[n["eng"]]
        sems = {}
        dsems = {}
        import contextlib
        stack = contextlib.ExitStack()
        for e in ENGS:
            sems[e] = stack.enter_context(nc.semaphore(f"s_{e}"))
        for e in DMAQ:
            dsems[e] = stack.enter_context(nc.semaphore(f"d_{e}"))

        def emit_for(eng_name):
            def _(eng):
                waited = {}
                for i, n in enumerate(nodes):
                    if n["eng"] != eng_name:
                        continue
                    for d in n["deps"]:
                        dn = nodes[d]
                        if dn["dma"]:
                            sem, val = dsems[dn["eng"]], 16 * dma_prefix[d]
                        else:
                            sem, val = sems[dn["eng"]], eng_inc_prefix[d]
                        key = id(sem)
                        if waited.get(key, -1) >= val:
                            continue
                        eng.wait_ge(sem, val)
                        waited[key] = val
                    ins = n["fn"](eng)
                    if n["dma"]:
                        ins.then_inc(dsems[eng_name], 16)
                    elif need_inc[i]:
                        if ins is None:
                            eng.nop().then_inc(sems[eng_name], 1)
                        else:
                            ins.then_inc(sems[eng_name], 1)
                for e in DMAQ:
                    if dcnt[e]:
                        eng.wait_ge(dsems[e], 16 * dcnt[e])
            return _

        block.sync(emit_for("sp"))
        block.scalar(emit_for("act"))
        block.vector(emit_for("dve"))
        block.tensor(emit_for("pe"))
        block.gpsimd(emit_for("pool"))
        return stack


def eng_obj(nc, name):
    return {"pe": nc.tensor, "act": nc.scalar, "dve": nc.vector,
            "pool": nc.gpsimd, "sp": nc.sync}[name]


def build_program(ntiles=NTILES, nblocks=NB, nexp=NE + 1, debug=DEBUG):
    nc = bass.Bass("TRN2", target_bir_lowering=False)
    p = Prog()

    def din(name, shape, dt=F32):
        return nc.dram_tensor(name, list(shape), dt, kind="ExternalInput").ap()

    x_d = din("x", (T, D))
    w_in_d = din("w_in", (CIN, D))
    convw_d = din("conv_w", (CIN, 4))
    convb_d = din("conv_b", (CIN,))
    alog_d = din("A_log", (H,))
    dparam_d = din("Dparam", (DIN,))
    dtb_d = din("dt_bias", (H,))
    mnw_d = din("mnorm_w", (DIN,))
    w_out_d = din("w_out", (D, DIN))
    w1_d = din("w1", (NE, DIN, D))
    w2_d = din("w2", (NE, D, DIN))
    sw1_d = din("sw1", (DIN, D))
    sw2_d = din("sw2", (D, DIN))
    rw_d = din("rw", (NE, D))
    lng_d = din("ln_g", (D,))
    lnb_d = din("ln_b", (D,))
    ident_d = din("ident", (128, 128))
    masklog_d = din("masklogT", (128, 128))
    iota8_d = din("iota8", (128, NE))

    out_d = nc.dram_tensor("out_fm", [8, 128, T], F32, kind="ExternalOutput").ap()

    def dscratch(name, shape, dt):
        return nc.dram_tensor(name, list(shape), dt, kind="Internal").ap()

    w_inT_s = dscratch("w_inT_s", (MC, 128, 8 * 128), BF16)
    w_outT_s = dscratch("w_outT_s", (8, 128, 16 * 128), BF16)
    w1T_s = dscratch("w1T_s", (nexp, 16, 128, 8 * 128), BF16)
    w2T_s = dscratch("w2T_s", (nexp, 8, 128, 16 * 128), BF16)
    h_s = dscratch("h_s", (8, 128, T), BF16)
    g_s = dscratch("g_s", (NE, T), F32)
    attn_s = dscratch("attn_s", (8, 128, T), F32)
    srep_s = dscratch("srep_s", (2, H * Q), F32)
    bcC_s = dscratch("bcC_s", (64, TILE), F32)
    scl_dbg = dscratch("scl_dbg", (256, 128), F32) if debug else None
    sum_dbg = dscratch("sum_dbg", (256, 128), F32) if debug else None

    dbg_outs = {}

    def dbg(name, src_ap, shape, dt=F32, reads=(), eng="sp"):
        if not debug:
            return
        o = nc.dram_tensor("dbg_" + name, list(shape), dt, kind="ExternalOutput").ap()
        dbg_outs[name] = o
        p.add(eng, lambda e, o=o, s=src_ap: e.dma_start(out=o[...], in_=s),
              reads=list(reads), dma=True)

    sb_allocs = []

    def sb(name, shape, dt, **kw):
        h = nc.alloc_sbuf_tensor(name, list(shape), dt, **kw)
        sb_allocs.append(h)
        return h.ap()

    ident = sb("sb_ident", (128, 128), F32)
    masklog = sb("sb_masklog", (128, 128), F32)
    iota8 = sb("sb_iota8", (128, NE), F32)
    convw_sb = sb("convw_sb", (128, MC * 4), F32)
    convb_sb = sb("convb_sb", (128, MC), F32)
    alog_sb = sb("alog_sb", (H, 1), F32)
    dtb_sb = sb("dtb_sb", (H, 1), F32)
    drep = sb("drep", (128, DIN), BF16)
    nwrep = sb("nwrep", (128, DIN), F32)
    lngrep = sb("lngrep", (128, D), F32)
    lnbrep = sb("lnbrep", (128, D), F32)
    ones32 = sb("ones32", (H, Q), F32)
    rwT = sb("rwT", (128, 8 * NE), BF16)
    cbias = sb("cbias", (128, 4), F32)

    import contextlib
    ph0 = contextlib.ExitStack()

    def sb0(name, shape, dt, **kw):
        h = ph0.enter_context(nc.sbuf_tensor(name, list(shape), dt, **kw))
        return h.ap()

    q_f32 = [sb0(f"q_f32_{i}", (128, DIN), F32) for i in range(2)]
    q_aw = sb0("q_aw", (128, DIN), F32)
    q_aw2 = sb0("q_aw2", (128, DIN), F32)
    q_sg = sb0("q_sg", (128, DIN), F32)
    q_bf = [sb0(f"q_bf_{i}", (128, DIN), BF16) for i in range(2)]
    q_col = sb0("q_col", (128, 4), F32)
    stage = [sb0(f"stage_{i}", (128, 16 * 128), BF16, is_transpose_dest=True)
             for i in range(2)]
    rwpad = sb0("rwpad", (16, D), BF16, is_transpose_dest=True)
    rwtr = sb0("rwtr", (128, 16), BF16, is_transpose_dest=True)

    def A(e):
        return e

    p.add("sp", lambda e: e.dma_start(out=ident[...], in_=ident_d[...]), writes=["ident"], dma=True)
    p.add("sp", lambda e: e.dma_start(out=masklog[...], in_=masklog_d[...]), writes=["masklog"], dma=True)
    p.add("sp", lambda e: e.dma_start(out=iota8[...], in_=iota8_d[...]), writes=["iota8"], dma=True)
    p.add("sp", lambda e: e.dma_start(out=convw_sb[:, 0:33 * 4].rearrange("p (m k) -> p m k", k=4), in_=convw_d[0:33 * 128, :].rearrange("(m p) k -> p m k", p=128)),
          writes=["convw"], dma=True)
    p.add("sp", lambda e: e.dma_start(out=convw_sb[0:32, 33 * 4:34 * 4], in_=convw_d[33 * 128:CIN, :]),
          writes=["convw"], dma=True)
    p.add("sp", lambda e: e.dma_start(out=convb_sb[:, 0:33], in_=convb_d[0:33 * 128].rearrange("(m p) -> p m", p=128), allow_slow_non_contiguous=True),
          writes=["convb"], dma=True)
    p.add("sp", lambda e: e.dma_start(out=convb_sb[0:32, 33:34], in_=convb_d[33 * 128:CIN].unsqueeze(1)),
          writes=["convb"], dma=True)
    p.add("sp", lambda e: e.dma_start(out=alog_sb[...], in_=alog_d.unsqueeze(1)), writes=["alog"], dma=True)
    p.add("sp", lambda e: e.dma_start(out=dtb_sb[...], in_=dtb_d.unsqueeze(1)), writes=["dtb"], dma=True)
    p.add("pool", lambda e: e.dma_start(out=drep[...], in_=dparam_d.unsqueeze(0).to_broadcast([128, DIN])),
          writes=["drep"], dma=True)
    p.add("act", lambda e: e.dma_start(out=nwrep[...], in_=mnw_d.unsqueeze(0).to_broadcast([128, DIN])),
          writes=["nwrep"], dma=True)
    p.add("act", lambda e: e.dma_start(out=lngrep[...], in_=lng_d.unsqueeze(0).to_broadcast([128, D])),
          writes=["lngrep"], dma=True)
    p.add("act", lambda e: e.dma_start(out=lnbrep[...], in_=lnb_d.unsqueeze(0).to_broadcast([128, D])),
          writes=["lnbrep"], dma=True)
    p.add("dve", lambda e: e.memset(ones32[...], 1.0), writes=["ones32"])
    p.add("dve", lambda e: e.memset(cbias[:, 0:1], LN_EPS), writes=["cbias"])
    p.add("dve", lambda e: e.memset(cbias[:, 1:2], 1.0), reads=["cbias"], writes=["cbias"])
    p.add("dve", lambda e: e.memset(cbias[:, 2:3], F32EPS), reads=["cbias"], writes=["cbias"])
    p.add("dve", lambda e: e.memset(convw_sb[32:64, 33 * 4:34 * 4], 0.0), reads=["convw"], writes=["convw"])
    p.add("dve", lambda e: e.memset(convw_sb[64:128, 33 * 4:34 * 4], 0.0), reads=["convw"], writes=["convw"])
    p.add("dve", lambda e: e.memset(convb_sb[32:64, 33:34], 0.0), reads=["convb"], writes=["convb"])
    p.add("dve", lambda e: e.memset(convb_sb[64:128, 33:34], 0.0), reads=["convb"], writes=["convb"])

    p.add("dve", lambda e: e.memset(rwpad[...], 0.0), writes=["rwpad"])
    p.add("pool", lambda e: e.dma_start(out=rwpad[0:NE, :], in_=rw_d[...]), reads=["rwpad"], writes=["rwpad"], dma=True)
    for k in range(8):
        p.add("act", lambda e, k=k: e.dma_start_transpose(rwtr[...], rwpad[:, k * 128:(k + 1) * 128]),
              reads=["rwpad"], writes=["rwtr"], dma=True)
        p.add("dve", lambda e, k=k: e.tensor_copy(rwT[:, k * NE:(k + 1) * NE], rwtr[:, 0:NE]),
              reads=["rwtr"], writes=["rwT"])

    for md in range(8):
        buf = md % 2
        p.add("pool", lambda e, md=md, buf=buf: e.dma_start(out=q_bf[buf][...], in_=w_out_d[md * 128:(md + 1) * 128, :]),
              reads=[f"qbf{buf}"], writes=[f"qbf{buf}"], dma=True)
        for fc in range(16):
            p.add("act", lambda e, md=md, fc=fc, buf=buf:
                  e.dma_start_transpose(stage[buf][:, fc * 128:(fc + 1) * 128],
                                        q_bf[buf][:, fc * 128:(fc + 1) * 128]),
                  reads=[f"qbf{buf}"], writes=[f"stage{buf}"], dma=True)
        p.add("sp", lambda e, md=md, buf=buf:
              e.dma_start(out=w_outT_s[md, :, :], in_=stage[buf][:, 0:16 * 128]),
              reads=[f"stage{buf}"], writes=[f"stage{buf}"], dma=True)

    for mc in range(MC):
        rows = 128 if mc < 33 else 32
        buf = mc % 2
        p.add("pool", lambda e, mc=mc, rows=rows, buf=buf:
              e.dma_start(out=q_bf[buf][0:rows, 0:D], in_=w_in_d[mc * 128:mc * 128 + rows, :]),
              reads=[f"qbf{buf}"], writes=[f"qbf{buf}"], dma=True)
        if rows < 128:
            p.add("dve", lambda e, buf=buf, rows=rows: e.memset(q_bf[buf][rows:64, 0:D], 0.0),
                  reads=[f"qbf{buf}"], writes=[f"qbf{buf}"])
            p.add("dve", lambda e, buf=buf: e.memset(q_bf[buf][64:128, 0:D], 0.0),
                  reads=[f"qbf{buf}"], writes=[f"qbf{buf}"])
        for k in range(8):
            p.add("act", lambda e, mc=mc, k=k, buf=buf:
                  e.dma_start_transpose(stage[buf][:, k * 128:(k + 1) * 128], q_bf[buf][:, k * 128:(k + 1) * 128]),
                  reads=[f"qbf{buf}"], writes=[f"stage{buf}"], dma=True)
        p.add("sp", lambda e, mc=mc, buf=buf:
              e.dma_start(out=w_inT_s[mc, :, 0:8 * 128], in_=stage[buf][:, 0:8 * 128]),
              reads=[f"stage{buf}"], writes=[f"stage{buf}"], dma=True)

    qchunk_i = [0]
    def quant_rows(src_slice, K, buf, tag):
        ci = qchunk_i[0]; qchunk_i[0] += 1
        p.add("sp", lambda e, s=src_slice: e.dma_start(out=q_f32[buf][:, 0:K], in_=s),
              reads=[f"qf{buf}"], writes=[f"qf{buf}"], dma=True)
        p.add("dve", lambda e: e.reduce_sum(q_col[:, 0:1], q_f32[buf][:, 0:K], axis=AX.X, apply_absolute_value=True),
              reads=[f"qf{buf}"], writes=["qcol"])
        p.add("dve", lambda e, K=K: e.tensor_scalar(q_col[:, 1:2], q_col[:, 0:1], 0.5 / K, 0.5e-5, ALU.mult, ALU.max),
              reads=["qcol"], writes=["qcol2"])
        p.add("dve", lambda e, K=K: e.tensor_scalar(q_col[:, 2:3], q_col[:, 0:1], 1.0 / K, 1e-5, ALU.mult, ALU.max),
              reads=["qcol"], writes=["qcol2"])
        p.add("act", lambda e, K=K: e.activation(q_aw[:, 0:K], q_f32[buf][:, 0:K], AF.Abs),
              reads=[f"qf{buf}"], writes=["qaw"])
        p.add("act", lambda e, K=K: e.activation(q_sg[:, 0:K], q_f32[buf][:, 0:K], AF.Sign),
              reads=[f"qf{buf}"], writes=["qsg"])
        p.add("dve", lambda e, K=K: e.tensor_scalar(q_aw2[:, 0:K], q_aw[:, 0:K], q_col[:, 1:2], None, ALU.is_gt),
              reads=["qaw", "qcol2"], writes=["qaw2"])
        p.add("dve", lambda e, K=K: e.scalar_tensor_tensor(q_bf[buf][:, 0:K], q_aw2[:, 0:K], q_col[:, 2:3], q_sg[:, 0:K], ALU.mult, ALU.mult),
              reads=["qaw2", "qsg", "qcol2"], writes=[f"qbf{buf}"])
        if debug:
            p.add("pool", lambda e, ci=ci: e.dma_start(out=scl_dbg[ci, :].unsqueeze(0).rearrange("a p -> (a p)").unsqueeze(1), in_=q_col[:, 2:3]),
                  reads=["qcol2"], dma=True)
            p.add("pool", lambda e, ci=ci: e.dma_start(out=sum_dbg[ci, :].unsqueeze(0).rearrange("a p -> (a p)").unsqueeze(1), in_=q_col[:, 0:1]),
                  reads=["qcol"], dma=True)

    for e_i in range(nexp):
        for fc in range(16):
            buf = fc % 2
            src = (w1_d[e_i, fc * 128:(fc + 1) * 128, :] if e_i < NE else sw1_d[fc * 128:(fc + 1) * 128, :])
            quant_rows(src, D, buf, f"w1_{e_i}_{fc}")
            for k in range(8):
                p.add("act", lambda e, k=k, buf=buf:
                      e.dma_start_transpose(stage[buf][:, k * 128:(k + 1) * 128], q_bf[buf][:, k * 128:(k + 1) * 128]),
                      reads=[f"qbf{buf}"], writes=[f"stage{buf}"], dma=True)
            p.add("sp", lambda e, e_i=e_i, fc=fc, buf=buf:
                  e.dma_start(out=w1T_s[e_i, fc, :, :], in_=stage[buf][:, 0:8 * 128]),
                  reads=[f"stage{buf}"], writes=[f"stage{buf}"], dma=True)
        for md in range(8):
            buf = md % 2
            src = (w2_d[e_i, md * 128:(md + 1) * 128, :] if e_i < NE else sw2_d[md * 128:(md + 1) * 128, :])
            quant_rows(src, DIN, buf, f"w2_{e_i}_{md}")
            for fc in range(16):
                p.add("act", lambda e, fc=fc, buf=buf:
                      e.dma_start_transpose(stage[buf][:, fc * 128:(fc + 1) * 128], q_bf[buf][:, fc * 128:(fc + 1) * 128]),
                      reads=[f"qbf{buf}"], writes=[f"stage{buf}"], dma=True)
            p.add("sp", lambda e, e_i=e_i, md=md, buf=buf:
                  e.dma_start(out=w2T_s[e_i, md, :, :], in_=stage[buf][:, 0:16 * 128]),
                  reads=[f"stage{buf}"], writes=[f"stage{buf}"], dma=True)

    if debug == "phase0":
        dbg("scl", scl_dbg, (256, 128), F32, reads=["qcol2"])
        dbg("sum", sum_dbg, (256, 128), F32, reads=["qcol"])
        dbg("w_inT", w_inT_s[...], (MC, 128, 8 * 128), BF16, reads=["stage0", "stage1"])
        dbg("w1T", w1T_s[...], (nexp, 16, 128, 8 * 128), BF16, reads=["stage0", "stage1"])
        dbg("w2T", w2T_s[...], (nexp, 8, 128, 16 * 128), BF16, reads=["stage0", "stage1"])
        dbg("w_outT_sb", w_outT_s[...], (8, 128, 16 * 128), BF16, reads=["stage0", "stage1"])
        dbg("rwT_sb", rwT[...], (128, 8 * NE), BF16, reads=["rwT"])
        dbg("drep", drep[...], (128, DIN), F32, reads=["drep"])
        dbg("convw", convw_sb[...], (128, MC * 4), F32, reads=["convw"])


    p.barrier()
    ph0.close()
    pa = contextlib.ExitStack()

    def sba(name, shape, dt, **kw):
        h = pa.enter_context(nc.sbuf_tensor(name, list(shape), dt, **kw))
        return h.ap()

    x_tm = sba("x_tm", (128, D), F32)
    t_tm = sba("t_tm", (128, D), F32)
    xn_tm = sba("xn_tm", (128, D), F32)
    h_tm = sba("h_tm", (128, D), F32)
    stat = sba("stat", (128, 16), F32)
    xn_fm = sba("xn_fm", (128, 8 * TILE), BF16)
    h_fm = sba("h_fm", (128, 8 * TILE), BF16)
    sz = sba("sz", (128, 16 * TILE), BF16)
    xh = sba("xh", (128, NCH * DIN), BF16)
    bc = sba("bc", (128, TILE), F32)
    bcn = sba("bcn", (64, TILE), F32)
    dtr = sba("dtr", (H, TILE), F32)
    dtw = sba("dtw", (H, TILE * 7), F32)
    cvp = [sba(f"cv{i}", (128, TILE), F32) for i in range(2)]
    zx = [sba(f"zx{i}", (128, TILE + 3), F32) for i in range(2)]
    tails = sba("tails", (128, MC * 3), F32)
    srep = sba("srep", (128, 16 * Q), F32)
    difft = sba("difft", (128, 16 * Q), F32)
    mt = [sba(f"mt{i}", (128, 16 * Q), BF16) for i in range(2)]
    cbt = [sba(f"cbt{i}", (128, Q), F32) for i in range(2)]
    stkT = [sba(f"stkT{i}", (128, 160), F32) for i in range(2)]
    expsT = [sba(f"expsT{i}", (128, H), F32) for i in range(2)]
    expsl = [sba(f"expsl{i}", (64, H), F32) for i in range(2)]
    wxh0 = sba("wxh0", (128, DIN), BF16)
    wxh = [wxh0, wxh0]
    bwb = [sba(f"bwb{i}", (128, 64), BF16) for i in range(2)]
    yevt = sba("yevt", (128, 512), F32)
    ysb = sba("ysb", (128, DIN), F32)
    ygat = sba("ygat", (128, 16 * TILE), BF16)
    attnb = sba("attnb", (128, TILE), F32)
    statebuf = [sba(f"state{i}", (64, DIN), F32) for i in range(2)]
    wt = [sba(f"wt{i}", (128, 8 * 128), BF16) for i in range(2)]
    wo = [sba(f"wo{i}", (128, 16 * 128), BF16) for i in range(2)]
    logit = sba("logit", (128, NE), F32)
    gtile = sba("gtile", (8, Q), F32)
    gE = sba("gE", (128, NE), F32)
    ones8 = sba("ones8", (128, NE), F32)
    gF = sba("gF", (128, NE), F32)
    ps = [pa.enter_context(nc.psum_tensor(f"ps{i}", [128, 512], F32)).ap()
          for i in range(8)]

    p.add("dve", lambda e: e.memset(tails[...], 0.0), writes=["tails"])
    p.add("dve", lambda e: e.memset(ones8[...], 1.0), writes=["ones8"])
    p.add("dve", lambda e: e.memset(statebuf[0][...], 0.0), writes=["st0"])

    xn_fm3 = xn_fm.rearrange("p (k t) -> p k t", t=TILE)
    h_fm3 = h_fm.rearrange("p (k t) -> p k t", t=TILE)
    sz3 = sz.rearrange("p (k t) -> p k t", t=TILE)
    xh3 = xh.rearrange("p (s f) -> p s f", f=DIN)
    ygat3 = ygat.rearrange("p (k t) -> p k t", t=TILE)
    dtw3 = dtw.rearrange("p (i t) -> p i t", t=TILE)
    D_DT, D_LN, D_S, D_SML, D_W, D_S1, D_S2 = range(7)
    D_A = D_S1

    def cw_col(mc, k):
        return convw_sb[:, mc * 4 + k:mc * 4 + k + 1]

    def dsl(i, cols=None):
        a = dtw3[:, i, :] if cols is None else dtw3[:, i, cols]
        return a

    def ln_and_transpose(it, s):
        t0 = it * TILE
        p.add("sp", lambda e, t0=t0, s=s: e.dma_start(out=x_tm[...], in_=x_d[t0 + s * 128:t0 + (s + 1) * 128, :]),
              reads=["x_tm"], writes=["x_tm"], dma=True)
        p.add("dve", lambda e: e.reduce_sum(stat[:, 0:1], x_tm[...], axis=AX.X), reads=["x_tm"], writes=["stat"])
        p.add("dve", lambda e: e.tensor_scalar_mul(stat[:, 1:2], stat[:, 0:1], 1.0 / D), reads=["stat"], writes=["stat"])
        p.add("dve", lambda e: e.tensor_scalar(t_tm[...], x_tm[...], stat[:, 1:2], None, ALU.subtract),
              reads=["x_tm", "stat"], writes=["t_tm"])
        p.add("act", lambda e: e.activation(xn_tm[...], t_tm[...], AF.Square, accum_out=stat[:, 2:3]),
              reads=["t_tm"], writes=["xn_tm", "stat"])
        p.add("act", lambda e: e.activation(stat[:, 3:4], stat[:, 2:3], AF.Ln, bias=cbias[:, 0:1], scale=1.0 / D),
              reads=["stat", "cbias"], writes=["stat"])
        p.add("act", lambda e: e.activation(stat[:, 4:5], stat[:, 3:4], AF.Exp, scale=-0.5),
              reads=["stat"], writes=["stat"])
        p.add("dve", lambda e: e.tensor_scalar_mul(xn_tm[...], t_tm[...], stat[:, 4:5]),
              reads=["t_tm", "stat"], writes=["xn_tm"])
        p.add("dve", lambda e: e.reduce_sum(stat[:, 5:6], xn_tm[...], axis=AX.X), reads=["xn_tm"], writes=["stat"])
        p.add("dve", lambda e: e.tensor_scalar_mul(stat[:, 6:7], stat[:, 5:6], 1.0 / D), reads=["stat"], writes=["stat"])
        p.add("dve", lambda e: e.tensor_scalar(t_tm[...], xn_tm[...], stat[:, 6:7], None, ALU.subtract),
              reads=["xn_tm", "stat"], writes=["t_tm"])
        p.add("act", lambda e: e.activation(h_tm[...], t_tm[...], AF.Square, accum_out=stat[:, 7:8]),
              reads=["t_tm"], writes=["h_tm", "stat"])
        p.add("act", lambda e: e.activation(stat[:, 8:9], stat[:, 7:8], AF.Ln, bias=cbias[:, 0:1], scale=1.0 / D),
              reads=["stat", "cbias"], writes=["stat"])
        p.add("act", lambda e: e.activation(stat[:, 9:10], stat[:, 8:9], AF.Exp, scale=-0.5),
              reads=["stat"], writes=["stat"])
        p.add("dve", lambda e: e.scalar_tensor_tensor(h_tm[...], t_tm[...], stat[:, 9:10], lngrep[...], ALU.mult, ALU.mult),
              reads=["t_tm", "stat", "lngrep"], writes=["h_tm"])
        p.add("dve", lambda e: e.tensor_add(t_tm[...], h_tm[...], lnbrep[...]),
              reads=["h_tm", "lnbrep"], writes=["t_tm"])
        for src_ap, src_b, dst3, dst_b in ((xn_tm, "xn_tm", xn_fm3, "xn_fm"),
                                           (t_tm, "t_tm", h_fm3, "h_fm")):
            for half in range(2):
                for k4 in range(4):
                    k = half * 4 + k4
                    p.add("pe", lambda e, src_ap=src_ap, k=k, k4=k4:
                          e.transpose(ps[2][:, k4 * 128:(k4 + 1) * 128], src_ap[:, k * 128:(k + 1) * 128], ident[...]),
                          reads=[src_b, "ident"], writes=["ps2"])
                p.add("dve", lambda e, dst3=dst3, half=half, s=s:
                      e.tensor_copy(dst3[:, half * 4:(half + 1) * 4, s * 128:(s + 1) * 128],
                                    ps[2][:, 0:512].rearrange("p (a b) -> p a b", b=128)),
                      reads=["ps2"], writes=[dst_b])
        for k in range(8):
            p.add("pe", lambda e, k=k, s=s: e.matmul(ps[2][:, 0:8], h_fm3[:, k, s * 128:(s + 1) * 128],
                                                     rwT[:, k * NE:(k + 1) * NE], start=(k == 0), stop=(k == 7)),
                  reads=["h_fm", "rwT"], writes=["ps2"])
        p.add("dve", lambda e: e.tensor_copy(logit[...], ps[2][:, 0:8]), reads=["ps2"], writes=["logit"])
        p.add("dve", lambda e: e.reduce_max(stat[:, 10:11], logit[...], axis=AX.X), reads=["logit"], writes=["stat"])
        p.add("dve", lambda e: e.tensor_scalar_mul(stat[:, 11:12], stat[:, 10:11], -1.0), reads=["stat"], writes=["stat"])
        p.add("act", lambda e: e.activation(gE[...], logit[...], AF.Exp, bias=stat[:, 11:12], accum_out=stat[:, 12:13]),
              reads=["logit", "stat"], writes=["gE", "stat"])
        p.add("dve", lambda e: e.reciprocal(stat[:, 13:14], stat[:, 12:13]), reads=["stat"], writes=["stat"])
        p.add("dve", lambda e: e.tensor_scalar(gE[...], logit[...], stat[:, 10:11], None, ALU.is_equal),
              reads=["logit", "stat"], writes=["gE"])
        p.add("dve", lambda e: e.tensor_tensor_scan(gF[...], ones8[...], gE[...], 0.0, ALU.mult, ALU.add),
              reads=["gE", "ones8"], writes=["gF"])
        p.add("dve", lambda e: e.tensor_scalar(gF[...], gF[...], 1.0, None, ALU.is_equal),
              reads=["gF"], writes=["gF"])
        p.add("dve", lambda e: e.tensor_mul(gE[...], gE[...], gF[...]), reads=["gE", "gF"], writes=["gE"])
        p.add("dve", lambda e: e.tensor_scalar_mul(gF[...], gE[...], stat[:, 13:14]), reads=["gE", "stat"], writes=["gF"])
        p.add("pe", lambda e: e.transpose(ps[2][0:8, 0:128], gF[...], ident[...]), reads=["gF", "ident"], writes=["ps2"])
        p.add("dve", lambda e: e.tensor_copy(gtile[...], ps[2][0:8, 0:128]), reads=["ps2"], writes=["gtile"])
        p.add("sp", lambda e, t0=t0, s=s: e.dma_start(out=g_s[:, t0 + s * 128:t0 + (s + 1) * 128], in_=gtile[...]),
              reads=["gtile"], dma=True)
        if debug == "phaseA2" and it == 0 and s == 0:
            dbg("logit", logit[...], (128, NE), F32, reads=["logit"])
            dbg("gE", gE[...], (128, NE), F32, reads=["gE"])

            dbg("gF", gF[...], (128, NE), F32, reads=["gF"])
            dbg("gtile", gtile[...], (8, Q), F32, reads=["gtile"])
            dbg("statv", stat[...], (128, 16), F32, reads=["stat"])

    def in_proj_conv(it):
        if debug == "phaseA1" and it == 0:
            dbg("xnfm", xn_fm[...], (128, 8 * TILE), BF16, reads=["xn_fm"])
        for mc in range(MC):
            b = mc % 2
            p.add("sp", lambda e, mc=mc, b=b: e.dma_start(out=wt[b][...], in_=w_inT_s[mc, :, :]),
                  reads=[f"wt{b}"], writes=[f"wt{b}"], dma=True)
            if debug == "phaseA1" and it == 0 and mc == 0:
                dbg("wt0", wt[b][...], (128, 8 * 128), BF16, reads=[f"wt{b}"])
            for k in range(8):
                p.add("pe", lambda e, k=k, b=b: e.matmul(ps[b][:, 0:512], wt[b][:, k * 128:(k + 1) * 128],
                                                         xn_fm3[:, k, :], start=(k == 0), stop=(k == 7)),
                      reads=[f"wt{b}", "xn_fm"], writes=[f"ps{b}"])
            p.add("dve", lambda e, mc=mc, b=b: e.tensor_copy(zx[b][:, 0:3], tails[:, mc * 3:(mc + 1) * 3]),
                  reads=["tails"], writes=[f"zx{b}"])
            p.add("act", lambda e, b=b: e.copy(zx[b][:, 3:3 + TILE], ps[b][:, 0:512]),
                  reads=[f"ps{b}"], writes=[f"zx{b}"])
            p.add("dve", lambda e, mc=mc, b=b: e.tensor_copy(tails[:, mc * 3:(mc + 1) * 3], zx[b][:, TILE:TILE + 3]),
                  reads=[f"zx{b}"], writes=["tails"])
            p.add("dve", lambda e, mc=mc, b=b:
                  e.scalar_tensor_tensor(cvp[0][...], zx[b][:, 0:TILE], cw_col(mc, 0),
                                         convb_sb[:, mc:mc + 1].to_broadcast([128, TILE]), ALU.mult, ALU.add),
                  reads=[f"zx{b}", "convw", "convb"], writes=["cv0"])
            for kk in range(1, 4):
                src_cv, dst_cv = (kk + 1) % 2, kk % 2
                p.add("dve", lambda e, mc=mc, b=b, kk=kk, src_cv=src_cv, dst_cv=dst_cv:
                      e.scalar_tensor_tensor(cvp[dst_cv][...], zx[b][:, kk:kk + TILE], cw_col(mc, kk),
                                             cvp[src_cv][...], ALU.mult, ALU.add),
                      reads=[f"zx{b}", "convw", f"cv{src_cv}"], writes=[f"cv{dst_cv}"])
            if debug == "phaseA1" and it == 0 and mc in (0, 16, 32, 33):
                dbg(f"zx{mc}", zx[b][...], (128, TILE + 3), F32, reads=[f"zx{b}"])
                dbg(f"cv{mc}", cvp[1][...], (128, TILE), F32, reads=["cv1"])
            if mc < 16:
                p.add("act", lambda e, mc=mc: e.activation(sz3[:, mc, :], cvp[1][...], AF.Silu),
                      reads=["cv1"], writes=["sz"])
            elif mc < 32:
                for s4 in range(4):
                    p.add("pe", lambda e, s4=s4: e.transpose(ps[2][:, s4 * 128:(s4 + 1) * 128],
                                                             cvp[1][:, s4 * 128:(s4 + 1) * 128], ident[...]),
                          reads=["cv1", "ident"], writes=["ps2"])
                p.add("dve", lambda e, mc=mc: e.tensor_copy(xh3[:, :, (mc - 16) * 128:(mc - 15) * 128],
                                                            ps[2][:, 0:512].rearrange("p (a b) -> p a b", b=128)),
                      reads=["ps2"], writes=["xh"])
            elif mc == 32:
                p.add("dve", lambda e: e.tensor_copy(bc[...], cvp[1][...]), reads=["cv1"], writes=["bc"])
                p.add("sp", lambda e: e.dma_start(out=bcC_s[...], in_=bc[64:128, :]),
                      reads=["bc"], writes=["bcCs"], dma=True)
                p.add("sp", lambda e: e.dma_start(out=bcn[...], in_=bcC_s[...]),
                      reads=["bcCs", "bcn"], writes=["bcn"], dma=True)
            else:
                p.add("dve", lambda e: e.tensor_copy(dtr[...], cvp[1][0:H, :]), reads=["cv1"], writes=["dtr"])

    def dt_pipeline(it):
        p.add("act", lambda e: e.activation(dsl(D_S1), dtr[...], AF.Abs, bias=dtb_sb[:, 0:1]),
              reads=["dtr", "dtb"], writes=["dtw"])
        p.add("act", lambda e: e.activation(dsl(D_S2), dsl(D_S1), AF.Exp, scale=-1.0), reads=["dtw"], writes=["dtw"])
        p.add("act", lambda e: e.activation(dsl(D_S1), dsl(D_S2), AF.Ln, bias=cbias[0:H, 1:2]), reads=["dtw", "cbias"], writes=["dtw"])
        p.add("act", lambda e: e.activation(dsl(D_S2), dtr[...], AF.Relu, bias=dtb_sb[:, 0:1]),
              reads=["dtr", "dtb"], writes=["dtw"])
        p.add("dve", lambda e: e.tensor_add(dsl(D_DT), dsl(D_S1), dsl(D_S2)), reads=["dtw"], writes=["dtw"])
        p.add("dve", lambda e: e.tensor_scalar_mul(dsl(D_A), dsl(D_DT), alog_sb[:, 0:1]),
              reads=["dtw", "alog"], writes=["dtw"])
        p.add("act", lambda e: e.activation(dsl(D_LN), dsl(D_DT), AF.Ln), reads=["dtw"], writes=["dtw"])
        for c in range(NCH):
            cc = slice(c * Q, (c + 1) * Q)
            p.add("dve", lambda e, cc=cc: e.tensor_tensor_scan(dsl(D_S, cc), ones32[0:H, 0:Q], dsl(D_A, cc),
                                                               0.0, ALU.mult, ALU.add),
                  reads=["dtw", "ones32"], writes=["dtw"])
        p.add("dve", lambda e: e.tensor_sub(dsl(D_SML), dsl(D_S), dsl(D_LN)), reads=["dtw"], writes=["dtw"])
        for c in range(NCH):
            cc = slice(c * Q, (c + 1) * Q)
            p.add("dve", lambda e, c=c, cc=cc: e.tensor_scalar(dsl(D_S1, cc), dsl(D_S, cc),
                                                               dtw3[:, D_S, c * Q + Q - 1:c * Q + Q], None, ALU.subtract),
                  reads=["dtw"], writes=["dtw"])
            p.add("act", lambda e, cc=cc: e.activation(dsl(D_S2, cc), dsl(D_S1, cc), AF.Exp, scale=-1.0),
                  reads=["dtw"], writes=["dtw"])
            p.add("dve", lambda e, cc=cc: e.tensor_mul(dsl(D_W, cc), dsl(D_DT, cc), dsl(D_S2, cc)),
                  reads=["dtw"], writes=["dtw"])

    def chunk_scan(it, c):
        gc = it * NCH + c
        slot = gc % 2
        pr = gc % 2
        sr, sw = f"st{gc % 2}", f"st{(gc + 1) % 2}"
        cc = slice(c * Q, (c + 1) * Q)
        wr_s = p.add("sp", lambda e, slot=slot, cc=cc:
                     e.dma_start(out=srep_s[slot, :].unsqueeze(0).rearrange("a (h q) -> (a h) q", q=Q), in_=dsl(D_S, cc)),
                     reads=["dtw", f"sreps{slot}"], writes=[f"sreps{slot}"], dma=True)
        p.add("pe", lambda e, cc=cc: e.transpose(ps[2][:, 0:64], bc[0:64, cc], ident[0:64, 0:64]),
              reads=["bc", "ident"], writes=["ps2"])
        p.add("pe", lambda e, cc=cc: e.transpose(ps[2][:, 64:96], dsl(D_W, cc), ident[0:H, 0:H]),
              reads=["dtw", "ident"], writes=["ps2"])
        p.add("pe", lambda e, cc=cc: e.transpose(ps[2][:, 96:128], dsl(D_SML, cc), ident[0:H, 0:H]),
              reads=["dtw", "ident"], writes=["ps2"])
        p.add("pe", lambda e, cc=cc: e.transpose(ps[2][:, 128:160], dsl(D_S, cc), ident[0:H, 0:H]),
              reads=["dtw", "ident"], writes=["ps2"])
        p.add("dve", lambda e, pr=pr: e.tensor_copy(stkT[pr][...], ps[2][:, 0:160]),
              reads=["ps2"], writes=[f"stkT{pr}"])
        p.add("act", lambda e, pr=pr: e.activation(expsT[pr][...], stkT[pr][:, 128:160], AF.Exp),
              reads=[f"stkT{pr}"], writes=[f"expsT{pr}"])
        p.add("pe", lambda e, cc=cc: e.matmul(ps[2][:, 0:128], bc[0:64, cc], bcn[0:64, cc], start=True, stop=True),
              reads=["bc", "bcn"], writes=["ps2"])
        p.add("act", lambda e, pr=pr: e.copy(cbt[pr][...], ps[2][:, 0:128]), reads=["ps2"], writes=[f"cbt{pr}"])
        p.add("dve", lambda e, c=c, pr=pr:
              e.tensor_tensor(wxh[pr][...].rearrange("p (h e) -> p h e", e=PDIM),
                              xh3[:, c, :].rearrange("p (h e) -> p h e", e=PDIM),
                              stkT[pr][:, 64:96].unsqueeze(2).broadcast_to([128, H, PDIM]), ALU.mult),
              reads=["xh", f"stkT{pr}"], writes=["wxh0"])
        p.add("dve", lambda e, pr=pr: e.tensor_copy(bwb[pr][...], stkT[pr][:, 0:64]),
              reads=[f"stkT{pr}"], writes=[f"bwb{pr}"])
        for half in range(2):
            rd = p.add("act", lambda e, slot=slot, half=half:
                       e.dma_start(out=srep[...], in_=srep_s[slot, half * 16 * Q:(half + 1) * 16 * Q]
                                   .unsqueeze(0).to_broadcast([128, 16 * Q])),
                       reads=[f"sreps{slot}", "srep"], writes=["srep"], dma=True, extra_deps=[wr_s])
            srep3 = srep.rearrange("p (h q) -> p h q", q=Q)
            p.add("act", lambda e, half=half, srep3=srep3, pr=pr:
                  e.activation(expsl[pr][:, half * 16:(half + 1) * 16], srep3[0:64, :, Q - 1], AF.Exp),
                  reads=["srep"], writes=[f"expsl{pr}"])
            p.add("dve", lambda e, half=half, pr=pr:
                  e.tensor_tensor(difft[...].rearrange("p (h q) -> p h q", q=Q),
                                  srep[...].rearrange("p (h q) -> p h q", q=Q),
                                  stkT[pr][:, 96 + half * 16:96 + (half + 1) * 16].unsqueeze(2).broadcast_to([128, 16, Q]),
                                  ALU.subtract),
                  reads=["srep", f"stkT{pr}"], writes=["difft"])
            p.add("dve", lambda e:
                  e.tensor_tensor(difft[...].rearrange("p (h q) -> p h q", q=Q),
                                  difft[...].rearrange("p (h q) -> p h q", q=Q),
                                  masklog[...].unsqueeze(1).broadcast_to([128, 16, Q]), ALU.add),
                  reads=["difft"], writes=["difft"])
            p.add("act", lambda e: e.activation(srep[...], difft[...], AF.Exp), reads=["difft"], writes=["srep"])
            p.add("dve", lambda e, half=half, pr=pr:
                  e.tensor_tensor(mt[pr][...].rearrange("p (h q) -> p h q", q=Q),
                                  srep[...].rearrange("p (h q) -> p h q", q=Q),
                                  cbt[pr][...].unsqueeze(1).broadcast_to([128, 16, Q]), ALU.mult),
                  reads=["srep", f"cbt{pr}"], writes=[f"mt{pr}"])
            chunk_headgroups(c, gc, cc, sr, sw, half, pr)

        p.add("pool", lambda e, c=c: e.tensor_mul(difft[:, 0:DIN], xh3[:, c, :], drep[...]),
              reads=["xh", "drep"], writes=["difft"])
        p.add("pool", lambda e: e.tensor_add(ysb[...], ysb[...], difft[:, 0:DIN]), reads=["ysb", "difft"], writes=["ysb"])
        p.add("act", lambda e: e.activation(difft[:, 0:DIN], ysb[...], AF.Square, accum_out=stat[:, 14:15]),
              reads=["ysb"], writes=["difft", "stat"])
        p.add("act", lambda e: e.activation(stat[:, 15:16], stat[:, 14:15], AF.Ln, bias=cbias[:, 2:3], scale=1.0 / DIN),
              reads=["stat", "cbias"], writes=["stat"])
        p.add("act", lambda e: e.activation(stat[:, 14:15], stat[:, 15:16], AF.Exp, scale=-0.5),
              reads=["stat"], writes=["stat"])
        p.add("dve", lambda e: e.scalar_tensor_tensor(ysb[...], ysb[...], stat[:, 14:15], nwrep[...],
                                                      ALU.mult, ALU.mult),
              reads=["ysb", "stat", "nwrep"], writes=["ysb"])
        for q4 in range(4):
            for j in range(4):
                fc = q4 * 4 + j
                p.add("pe", lambda e, fc=fc, j=j: e.transpose(ps[6][:, j * 128:(j + 1) * 128],
                                                              ysb[:, fc * 128:(fc + 1) * 128], ident[...]),
                      reads=["ysb", "ident"], writes=["ps6"])
            p.add("dve", lambda e, q4=q4, c=c:
                  e.tensor_mul(ygat3[:, q4 * 4:(q4 + 1) * 4, c * 128:(c + 1) * 128],
                               ps[6][:, 0:512].rearrange("p (a b) -> p a b", b=128),
                               sz3[:, q4 * 4:(q4 + 1) * 4, c * 128:(c + 1) * 128]),
                  reads=["ps6", "sz"], writes=["ygat"])
        if debug == "phaseA2" and it == 0 and c == 0:
            dbg("stkT", stkT[0][...], (128, 160), F32, reads=["stkT0"])
            dbg("cbt", cbt[0][...], (128, Q), F32, reads=["cbt0"])
            dbg("mtd", mt[0][...], (128, 16 * Q), BF16, reads=["mt0"])
            dbg("ysbd", ysb[...], (128, DIN), F32, reads=["ysb"])
            dbg("st1d", statebuf[1][...], (64, DIN), F32, reads=["st1"])
            dbg("expsTd", expsT[0][...], (128, H), F32, reads=["expsT0"])
            dbg("expsld", expsl[0][...], (64, H), F32, reads=["expsl0"])
            dbg("bcd", bc[...], (128, TILE), F32, reads=["bc"])
            dbg("bcnd", bcn[...], (64, TILE), F32, reads=["bcn"])

    def chunk_headgroups(c, gc, cc, sr, sw, half, pr):
        for hg in (half * 2, half * 2 + 1):
            j0 = (hg % 2) * 8
            for j in range(8):
                h_ = hg * 8 + j
                p.add("pe", lambda e, h_=h_, j=j, c=c, pr=pr: e.matmul(ps[3][:, j * 64:(j + 1) * 64],
                                                                mt[pr][:, (h_ % 16) * Q:((h_ % 16) + 1) * Q],
                                                                xh3[:, c, h_ * 64:(h_ + 1) * 64], start=True, stop=True),
                      reads=[f"mt{pr}", "xh"], writes=["ps3"])
            p.add("pe", lambda e, hg=hg, cc=cc, gc=gc: e.matmul(ps[4][:, 0:512],
                                                                bcn[0:64, cc],
                                                                statebuf[gc % 2][:, hg * 512:(hg + 1) * 512],
                                                                start=True, stop=True),
                  reads=["bcn", sr], writes=["ps4"])
            p.add("pe", lambda e, hg=hg, c=c, pr=pr: e.matmul(ps[5][0:64, 0:512],
                                                              bwb[pr][...],
                                                              wxh[pr][:, hg * 512:(hg + 1) * 512], start=True, stop=True),
                  reads=[f"bwb{pr}", "wxh0"], writes=["ps5"])
            p.add("act", lambda e, hg=hg: e.copy(ysb[:, hg * 512:(hg + 1) * 512], ps[3][:, 0:512]),
                  reads=["ps3"], writes=["ysb"])
            p.add("dve", lambda e, hg=hg, pr=pr:
                  e.tensor_tensor(yevt[...].rearrange("p (h e) -> p h e", e=PDIM),
                                  ps[4][:, 0:512].rearrange("p (h e) -> p h e", e=PDIM),
                                  expsT[pr][:, hg * 8:(hg + 1) * 8].unsqueeze(2).broadcast_to([128, 8, PDIM]),
                                  ALU.mult),
                  reads=["ps4", f"expsT{pr}"], writes=["yevt"])
            p.add("dve", lambda e, hg=hg: e.tensor_add(ysb[:, hg * 512:(hg + 1) * 512],
                                                       ysb[:, hg * 512:(hg + 1) * 512], yevt[...]),
                  reads=["yevt", "ysb"], writes=["ysb"])
            p.add("dve", lambda e, hg=hg, gc=gc, pr=pr:
                  e.tensor_tensor(yevt[0:64, :].rearrange("p (h e) -> p h e", e=PDIM),
                                  statebuf[gc % 2][:, hg * 512:(hg + 1) * 512].rearrange("p (h e) -> p h e", e=PDIM),
                                  expsl[pr][:, hg * 8:(hg + 1) * 8].unsqueeze(2).broadcast_to([64, 8, PDIM]),
                                  ALU.mult),
                  reads=[sr, f"expsl{pr}"], writes=["yevt"])
            p.add("dve", lambda e, hg=hg, gc=gc:
                  e.tensor_add(statebuf[(gc + 1) % 2][:, hg * 512:(hg + 1) * 512], yevt[0:64, :],
                               ps[5][0:64, 0:512]),
                  reads=["yevt", "ps5"], writes=[sw])


    def out_proj(it):
        t0 = it * TILE
        for md in range(8):
            b = md % 2
            p.add("sp", lambda e, md=md, b=b: e.dma_start(out=wo[b][...], in_=w_outT_s[md, :, :]),
                  reads=[f"wo{b}"], writes=[f"wo{b}"], dma=True)
            for fc in range(16):
                p.add("pe", lambda e, fc=fc, b=b: e.matmul(ps[7][:, 0:512], wo[b][:, fc * 128:(fc + 1) * 128],
                                                           ygat3[:, fc, :], start=(fc == 0), stop=(fc == 15)),
                      reads=[f"wo{b}", "ygat"], writes=["ps7"])
            p.add("act", lambda e, md=md: e.copy(attnb[...], ps[7][:, 0:512]),
                  reads=["ps7"], writes=["attnb"])
            p.add("sp", lambda e, md=md, t0=t0: e.dma_start(out=attn_s[md, :, t0:t0 + TILE], in_=attnb[...]),
                  reads=["attnb"], dma=True)

    for it in range(ntiles):
        for s in range(NCH):
            ln_and_transpose(it, s)
        in_proj_conv(it)
        dt_pipeline(it)
        for c in range(NCH):
            chunk_scan(it, c)
        out_proj(it)
        t0 = it * TILE
        for k in range(8):
            p.add("act", lambda e, k=k, t0=t0: e.dma_start(out=h_s[k, :, t0:t0 + TILE], in_=h_fm3[:, k, :]),
                  reads=["h_fm"], dma=True)

    if debug in ("phaseA", "phaseA1", "phaseA2"):
        p.barrier()
        dbg("attn", attn_s[...], (8, 128, T), F32)
        dbg("h_s", h_s[...], (8, 128, T), BF16)
        dbg("g_s", g_s[...], (NE, T), F32)
        dbg("sz", sz[...], (128, 16 * TILE), BF16, reads=["sz"])
        dbg("xh", xh[...], (128, NCH * DIN), BF16, reads=["xh"])
        dbg("dtw", dtw[...], (H, TILE * 7), F32, reads=["dtw"])
        dbg("state", statebuf[(ntiles * NCH) % 2][...], (64, DIN), F32, reads=[f"st{(ntiles * NCH) % 2}"])


    p.barrier()
    pa.close()
    pb = contextlib.ExitStack()

    def sbb_(name, shape, dt, **kw):
        h = pb.enter_context(nc.sbuf_tensor(name, list(shape), dt, **kw))
        return h.ap()

    acc = sbb_("acc", (128, 8 * TB), F32)
    hb = sbb_("hb", (128, 8 * TB), BF16)
    h1g = sbb_("h1g", (128, 16 * TB), BF16)
    w1b = [sbb_(f"w1b{i}", (128, 8 * 128), BF16) for i in range(2)]
    w2b = [sbb_(f"w2b{i}", (128, 16 * 128), BF16) for i in range(2)]
    grep = [sbb_(f"grep{i}", (128, TB), F32) for i in range(2)]
    h1t = [sbb_(f"h1t{i}", (128, 512), BF16) for i in range(2)]
    xb = sbb_("xb", (128, D), F32)
    bident = sbb_("bident", (128, 128), F32)
    psb = [pb.enter_context(nc.psum_tensor(f"pb{i}", [128, 512], F32)).ap()
           for i in range(6)]
    acc3 = acc.rearrange("p (k t) -> p k t", t=TB)
    hb3 = hb.rearrange("p (k t) -> p k t", t=TB)
    h1g3 = h1g.rearrange("p (k t) -> p k t", t=TB)

    p.add("sp", lambda e: e.dma_start(out=bident[...], in_=ident_d[...]), writes=["bident"], dma=True)

    for bi in range(nblocks):
        t0 = bi * TB
        for md in range(8):
            p.add("sp", lambda e, md=md, t0=t0: e.dma_start(out=acc3[:, md, :], in_=attn_s[md, :, t0:t0 + TB]),
                  reads=["acc"], writes=["acc"], dma=True)
            p.add("act", lambda e, md=md, t0=t0: e.dma_start(out=hb3[:, md, :], in_=h_s[md, :, t0:t0 + TB]),
                  reads=["hb"], writes=["hb"], dma=True)
        for s in range(TB // 128):
            p.add("sp", lambda e, s=s, t0=t0: e.dma_start(out=xb[...], in_=x_d[t0 + s * 128:t0 + (s + 1) * 128, :]),
                  reads=["xb"], writes=["xb"], dma=True)
            for half in range(2):
                for k4 in range(4):
                    k = half * 4 + k4
                    p.add("pe", lambda e, k=k, k4=k4: e.transpose(psb[2][:, k4 * 128:(k4 + 1) * 128],
                                                                  xb[:, k * 128:(k + 1) * 128], bident[...]),
                          reads=["xb", "bident"], writes=["pb2"])
                p.add("dve", lambda e, half=half, s=s:
                      e.tensor_add(acc3[:, half * 4:(half + 1) * 4, s * 128:(s + 1) * 128],
                                   psb[2][:, 0:512].rearrange("p (a b) -> p a b", b=128),
                                   acc3[:, half * 4:(half + 1) * 4, s * 128:(s + 1) * 128]),
                      reads=["pb2", "acc"], writes=["acc"])
        for e_i in range(nexp):
            ge = e_i % 2
            if e_i < NE:
                p.add("sp", lambda e, e_i=e_i, ge=ge, t0=t0:
                      e.dma_start(out=grep[ge][...], in_=g_s[e_i, t0:t0 + TB].unsqueeze(0).to_broadcast([128, TB])),
                      reads=[f"grep{ge}"], writes=[f"grep{ge}"], dma=True)
            else:
                p.add("dve", lambda e, ge=ge: e.memset(grep[ge][...], 1.0),
                      reads=[f"grep{ge}"], writes=[f"grep{ge}"])
            for fc in range(16):
                wb = fc % 2
                p.add("sp", lambda e, e_i=e_i, fc=fc, wb=wb: e.dma_start(out=w1b[wb][...], in_=w1T_s[e_i, fc, :, :]),
                      reads=[f"w1b{wb}"], writes=[f"w1b{wb}"], dma=True)
                for ts2 in range(TB // 512):
                    pbk = f"pb{ts2}"
                    for k in range(8):
                        p.add("pe", lambda e, fc=fc, wb=wb, ts2=ts2, k=k:
                              e.matmul(psb[ts2][:, 0:512], w1b[wb][:, k * 128:(k + 1) * 128],
                                       hb3[:, k, ts2 * 512:(ts2 + 1) * 512], start=(k == 0), stop=(k == 7)),
                              reads=[f"w1b{wb}", "hb"], writes=[pbk])
                    p.add("act", lambda e, ts2=ts2: e.activation(h1t[ts2][...], psb[ts2][:, 0:512], AF.Silu),
                          reads=[pbk], writes=[f"h1t{ts2}"])
                    p.add("dve", lambda e, fc=fc, ts2=ts2, ge=ge:
                          e.tensor_mul(h1g3[:, fc, ts2 * 512:(ts2 + 1) * 512], h1t[ts2][...],
                                       grep[ge][:, ts2 * 512:(ts2 + 1) * 512]),
                          reads=[f"h1t{ts2}", f"grep{ge}"], writes=["h1g"])
            for md in range(8):
                wb = md % 2
                p.add("act", lambda e, e_i=e_i, md=md, wb=wb: e.dma_start(out=w2b[wb][...], in_=w2T_s[e_i, md, :, :]),
                      reads=[f"w2b{wb}"], writes=[f"w2b{wb}"], dma=True)
                for ts2 in range(TB // 512):
                    pbk = f"pb{4 + md % 2}"
                    for fc in range(16):
                        p.add("pe", lambda e, md=md, wb=wb, ts2=ts2, fc=fc:
                              e.matmul(psb[4 + md % 2][:, 0:512], w2b[wb][:, fc * 128:(fc + 1) * 128],
                                       h1g3[:, fc, ts2 * 512:(ts2 + 1) * 512], start=(fc == 0), stop=(fc == 15)),
                              reads=[f"w2b{wb}", "h1g"], writes=[pbk])
                    p.add("dve", lambda e, md=md, ts2=ts2:
                          e.tensor_add(acc3[:, md, ts2 * 512:(ts2 + 1) * 512], psb[4 + md % 2][:, 0:512],
                                       acc3[:, md, ts2 * 512:(ts2 + 1) * 512]),
                          reads=[pbk, "acc"], writes=["acc"])
        for md in range(8):
            p.add("sp", lambda e, md=md, t0=t0: e.dma_start(out=out_d[md, :, t0:t0 + TB], in_=acc3[:, md, :]),
                  reads=["acc"], dma=True)

    with nc.Block() as block:
        stack = p.emit(nc, block)
    stack.close()
    return nc, dbg_outs


def _mk_consts():
    ident = np.eye(128, dtype=np.float32)
    j = np.arange(128)[:, None]
    i = np.arange(128)[None, :]
    masklogT = np.where(i >= j, 0.0, NEG_BIG).astype(np.float32)
    iota8 = np.broadcast_to(np.arange(NE, dtype=np.float32)[None, :], (128, NE)).copy()
    return ident, masklogT, iota8


def _core_inputs(inputs, b):
    ident, masklogT, iota8 = _mk_consts()
    return {
        "x": np.ascontiguousarray(inputs["L0_attn_u"][b]),
        "w_in": inputs["L0_attn_in_proj_w"],
        "conv_w": inputs["L0_attn_conv_w"],
        "conv_b": inputs["L0_attn_conv_b"],
        "A_log": inputs["L0_attn_A_log"],
        "Dparam": inputs["L0_attn_Dparam"],
        "dt_bias": inputs["L0_attn_dt_bias"],
        "mnorm_w": inputs["L0_attn_norm_w"],
        "w_out": inputs["L0_attn_out_proj_w"],
        "w1": inputs["L0_ffn_expert_w1"],
        "w2": inputs["L0_ffn_expert_w2"],
        "sw1": inputs["L0_ffn_shared_w1"],
        "sw2": inputs["L0_ffn_shared_w2"],
        "rw": inputs["L0_ffn_router_w"],
        "ln_g": inputs["L0_ffn_ln_gamma"],
        "ln_b": inputs["L0_ffn_ln_beta"],
        "ident": ident,
        "masklogT": masklogT,
        "iota8": iota8,
    }


def kernel(**inputs):
    nc, _ = build_program()
    in_maps = [_core_inputs(inputs, b) for b in range(8)]
    res = run_bass_kernel_spmd(nc, in_maps, core_ids=list(range(8)))
    outs = []
    for r in res.results:
        o = np.asarray(r["out_fm"])
        outs.append(o.reshape(D, T).T)
    return np.stack(outs).astype(np.float32)


# version 03
# speedup vs baseline: 1.0653x; 1.0099x over previous
import os
import numpy as np
import ml_dtypes
import concourse.bass as bass
import concourse.mybir as mybir
from concourse.bass_utils import run_bass_kernel_spmd

F32 = mybir.dt.float32
BF16 = mybir.dt.bfloat16
AF = mybir.ActivationFunctionType
ALU = mybir.AluOpType
AX = mybir.AxisListType

T = 8192
D = 1024
CIN = 4256
DIN = 2048
H = 32
PDIM = 64
NST = 64
Q = 128
NE = 8
TILE = 512
NTILES = T // TILE
NCH = TILE // Q
MC = CIN // 128 + (1 if CIN % 128 else 0)
TB = 1024
NB = T // TB
LN_EPS = 1e-5
F32EPS = float(np.finfo(np.float32).eps)
NEG_BIG = -1e30

DEBUG = os.environ.get("KDEBUG", "")


ENGS = ("pe", "act", "dve", "pool", "sp")
DMAQ = ("sp", "act", "pool")


class Prog:
    def __init__(self):
        self.nodes = []
        self.last_w = {}
        self.readers = {}
        self.bar = None

    def barrier(self):
        last = {}
        for i, n in enumerate(self.nodes):
            last[(n["eng"], n["dma"])] = i
        self.bar = None
        nid = self.add("sp", lambda e: e.nop(), extra_deps=list(last.values()))
        self.bar = nid
        return nid

    def add(self, eng, fn, reads=(), writes=(), dma=False, extra_deps=()):
        deps = set(int(d) for d in extra_deps if d is not None)
        if self.bar is not None:
            deps.add(self.bar)
        for b in reads:
            if b in self.last_w:
                deps.add(self.last_w[b])
        for b in writes:
            if b in self.last_w:
                deps.add(self.last_w[b])
            for r in self.readers.get(b, ()):
                deps.add(r)
        nid = len(self.nodes)
        deps = sorted(d for d in deps if d < nid)
        self.nodes.append(dict(eng=eng, fn=fn, dma=bool(dma), deps=deps))
        for b in reads:
            self.readers.setdefault(b, []).append(nid)
        for b in writes:
            self.last_w[b] = nid
            self.readers[b] = []
        return nid

    def emit(self, nc, block):
        nodes = self.nodes
        need_inc = [False] * len(nodes)
        for n in nodes:
            for d in n["deps"]:
                if not nodes[d]["dma"]:
                    need_inc[d] = True
        eng_inc_prefix = {}
        dma_prefix = {}
        cnt = {e: 0 for e in ENGS}
        dcnt = {e: 0 for e in ENGS}
        for i, n in enumerate(nodes):
            if n["dma"]:
                dcnt[n["eng"]] += 1
                dma_prefix[i] = dcnt[n["eng"]]
            else:
                if need_inc[i]:
                    cnt[n["eng"]] += 1
                eng_inc_prefix[i] = cnt[n["eng"]]
        sems = {}
        dsems = {}
        import contextlib
        stack = contextlib.ExitStack()
        for e in ENGS:
            sems[e] = stack.enter_context(nc.semaphore(f"s_{e}"))
        for e in DMAQ:
            dsems[e] = stack.enter_context(nc.semaphore(f"d_{e}"))

        def emit_for(eng_name):
            def _(eng):
                waited = {}
                for i, n in enumerate(nodes):
                    if n["eng"] != eng_name:
                        continue
                    for d in n["deps"]:
                        dn = nodes[d]
                        if dn["dma"]:
                            sem, val = dsems[dn["eng"]], 16 * dma_prefix[d]
                        else:
                            sem, val = sems[dn["eng"]], eng_inc_prefix[d]
                        key = id(sem)
                        if waited.get(key, -1) >= val:
                            continue
                        eng.wait_ge(sem, val)
                        waited[key] = val
                    ins = n["fn"](eng)
                    if n["dma"]:
                        ins.then_inc(dsems[eng_name], 16)
                    elif need_inc[i]:
                        if ins is None:
                            eng.nop().then_inc(sems[eng_name], 1)
                        else:
                            ins.then_inc(sems[eng_name], 1)
                for e in DMAQ:
                    if dcnt[e]:
                        eng.wait_ge(dsems[e], 16 * dcnt[e])
            return _

        block.sync(emit_for("sp"))
        block.scalar(emit_for("act"))
        block.vector(emit_for("dve"))
        block.tensor(emit_for("pe"))
        block.gpsimd(emit_for("pool"))
        return stack


def eng_obj(nc, name):
    return {"pe": nc.tensor, "act": nc.scalar, "dve": nc.vector,
            "pool": nc.gpsimd, "sp": nc.sync}[name]


def build_program(ntiles=NTILES, nblocks=NB, nexp=NE + 1, debug=DEBUG):
    nc = bass.Bass("TRN2", target_bir_lowering=False)
    p = Prog()

    def din(name, shape, dt=F32):
        return nc.dram_tensor(name, list(shape), dt, kind="ExternalInput").ap()

    x_d = din("x", (T, D))
    w_in_d = din("w_in", (CIN, D))
    convw_d = din("conv_w", (CIN, 4))
    convb_d = din("conv_b", (CIN,))
    alog_d = din("A_log", (H,))
    dparam_d = din("Dparam", (DIN,))
    dtb_d = din("dt_bias", (H,))
    mnw_d = din("mnorm_w", (DIN,))
    w_out_d = din("w_out", (D, DIN))
    w1_d = din("w1", (NE, DIN, D))
    w2_d = din("w2", (NE, D, DIN))
    sw1_d = din("sw1", (DIN, D))
    sw2_d = din("sw2", (D, DIN))
    rw_d = din("rw", (NE, D))
    lng_d = din("ln_g", (D,))
    lnb_d = din("ln_b", (D,))
    ident_d = din("ident", (128, 128))
    masklog_d = din("masklogT", (128, 128))
    iota8_d = din("iota8", (128, NE))

    out_d = nc.dram_tensor("out_fm", [8, 128, T], F32, kind="ExternalOutput").ap()

    def dscratch(name, shape, dt):
        return nc.dram_tensor(name, list(shape), dt, kind="Internal").ap()

    w_inT_s = dscratch("w_inT_s", (MC, 128, 8 * 128), BF16)
    w_outT_s = dscratch("w_outT_s", (8, 128, 16 * 128), BF16)
    w1T_s = dscratch("w1T_s", (nexp, 16, 128, 8 * 128), BF16)
    w2T_s = dscratch("w2T_s", (nexp, 8, 128, 16 * 128), BF16)
    h_s = dscratch("h_s", (8, 128, T), BF16)
    g_s = dscratch("g_s", (NE, T), F32)
    attn_s = dscratch("attn_s", (8, 128, T), F32)
    srep_s = dscratch("srep_s", (2, H * Q), F32)
    bcC_s = dscratch("bcC_s", (64, TILE), F32)
    scl_dbg = dscratch("scl_dbg", (256, 128), F32) if debug else None
    sum_dbg = dscratch("sum_dbg", (256, 128), F32) if debug else None

    dbg_outs = {}

    def dbg(name, src_ap, shape, dt=F32, reads=(), eng="sp"):
        if not debug:
            return
        o = nc.dram_tensor("dbg_" + name, list(shape), dt, kind="ExternalOutput").ap()
        dbg_outs[name] = o
        p.add(eng, lambda e, o=o, s=src_ap: e.dma_start(out=o[...], in_=s),
              reads=list(reads), dma=True)

    sb_allocs = []

    def sb(name, shape, dt, **kw):
        h = nc.alloc_sbuf_tensor(name, list(shape), dt, **kw)
        sb_allocs.append(h)
        return h.ap()

    ident = sb("sb_ident", (128, 128), F32)
    masklog = sb("sb_masklog", (128, 128), F32)
    iota8 = sb("sb_iota8", (128, NE), F32)
    convw_sb = sb("convw_sb", (128, MC * 4), F32)
    convb_sb = sb("convb_sb", (128, MC), F32)
    alog_sb = sb("alog_sb", (H, 1), F32)
    dtb_sb = sb("dtb_sb", (H, 1), F32)
    drep = sb("drep", (128, DIN), BF16)
    nwrep = sb("nwrep", (128, DIN), F32)
    lngrep = sb("lngrep", (128, D), F32)
    lnbrep = sb("lnbrep", (128, D), F32)
    ones32 = sb("ones32", (H, Q), F32)
    rwT = sb("rwT", (128, 8 * NE), BF16)
    cbias = sb("cbias", (128, 4), F32)

    import contextlib
    ph0 = contextlib.ExitStack()

    def sb0(name, shape, dt, **kw):
        h = ph0.enter_context(nc.sbuf_tensor(name, list(shape), dt, **kw))
        return h.ap()

    q_f32 = [sb0(f"q_f32_{i}", (128, DIN), F32) for i in range(2)]
    q_aw = sb0("q_aw", (128, DIN), F32)
    q_aw2 = sb0("q_aw2", (128, DIN), F32)
    q_sg = sb0("q_sg", (128, DIN), F32)
    q_bf = [sb0(f"q_bf_{i}", (128, DIN), BF16) for i in range(2)]
    q_col = sb0("q_col", (128, 4), F32)
    stage = [sb0(f"stage_{i}", (128, 16 * 128), BF16, is_transpose_dest=True)
             for i in range(2)]
    rwpad = sb0("rwpad", (16, D), BF16, is_transpose_dest=True)
    rwtr = sb0("rwtr", (128, 16), BF16, is_transpose_dest=True)

    def A(e):
        return e

    p.add("sp", lambda e: e.dma_start(out=ident[...], in_=ident_d[...]), writes=["ident"], dma=True)
    p.add("sp", lambda e: e.dma_start(out=masklog[...], in_=masklog_d[...]), writes=["masklog"], dma=True)
    p.add("sp", lambda e: e.dma_start(out=iota8[...], in_=iota8_d[...]), writes=["iota8"], dma=True)
    p.add("sp", lambda e: e.dma_start(out=convw_sb[:, 0:33 * 4].rearrange("p (m k) -> p m k", k=4), in_=convw_d[0:33 * 128, :].rearrange("(m p) k -> p m k", p=128)),
          writes=["convw"], dma=True)
    p.add("sp", lambda e: e.dma_start(out=convw_sb[0:32, 33 * 4:34 * 4], in_=convw_d[33 * 128:CIN, :]),
          writes=["convw"], dma=True)
    p.add("sp", lambda e: e.dma_start(out=convb_sb[:, 0:33], in_=convb_d[0:33 * 128].rearrange("(m p) -> p m", p=128), allow_slow_non_contiguous=True),
          writes=["convb"], dma=True)
    p.add("sp", lambda e: e.dma_start(out=convb_sb[0:32, 33:34], in_=convb_d[33 * 128:CIN].unsqueeze(1)),
          writes=["convb"], dma=True)
    p.add("sp", lambda e: e.dma_start(out=alog_sb[...], in_=alog_d.unsqueeze(1)), writes=["alog"], dma=True)
    p.add("sp", lambda e: e.dma_start(out=dtb_sb[...], in_=dtb_d.unsqueeze(1)), writes=["dtb"], dma=True)
    p.add("pool", lambda e: e.dma_start(out=drep[...], in_=dparam_d.unsqueeze(0).to_broadcast([128, DIN])),
          writes=["drep"], dma=True)
    p.add("act", lambda e: e.dma_start(out=nwrep[...], in_=mnw_d.unsqueeze(0).to_broadcast([128, DIN])),
          writes=["nwrep"], dma=True)
    p.add("act", lambda e: e.dma_start(out=lngrep[...], in_=lng_d.unsqueeze(0).to_broadcast([128, D])),
          writes=["lngrep"], dma=True)
    p.add("act", lambda e: e.dma_start(out=lnbrep[...], in_=lnb_d.unsqueeze(0).to_broadcast([128, D])),
          writes=["lnbrep"], dma=True)
    p.add("dve", lambda e: e.memset(ones32[...], 1.0), writes=["ones32"])
    p.add("dve", lambda e: e.memset(cbias[:, 0:1], LN_EPS), writes=["cbias"])
    p.add("dve", lambda e: e.memset(cbias[:, 1:2], 1.0), reads=["cbias"], writes=["cbias"])
    p.add("dve", lambda e: e.memset(cbias[:, 2:3], F32EPS), reads=["cbias"], writes=["cbias"])
    p.add("dve", lambda e: e.memset(convw_sb[32:64, 33 * 4:34 * 4], 0.0), reads=["convw"], writes=["convw"])
    p.add("dve", lambda e: e.memset(convw_sb[64:128, 33 * 4:34 * 4], 0.0), reads=["convw"], writes=["convw"])
    p.add("dve", lambda e: e.memset(convb_sb[32:64, 33:34], 0.0), reads=["convb"], writes=["convb"])
    p.add("dve", lambda e: e.memset(convb_sb[64:128, 33:34], 0.0), reads=["convb"], writes=["convb"])

    p.add("dve", lambda e: e.memset(rwpad[...], 0.0), writes=["rwpad"])
    p.add("pool", lambda e: e.dma_start(out=rwpad[0:NE, :], in_=rw_d[...]), reads=["rwpad"], writes=["rwpad"], dma=True)
    for k in range(8):
        p.add("act", lambda e, k=k: e.dma_start_transpose(rwtr[...], rwpad[:, k * 128:(k + 1) * 128]),
              reads=["rwpad"], writes=["rwtr"], dma=True)
        p.add("dve", lambda e, k=k: e.tensor_copy(rwT[:, k * NE:(k + 1) * NE], rwtr[:, 0:NE]),
              reads=["rwtr"], writes=["rwT"])

    for md in range(8):
        buf = md % 2
        p.add("pool", lambda e, md=md, buf=buf: e.dma_start(out=q_bf[buf][...], in_=w_out_d[md * 128:(md + 1) * 128, :]),
              reads=[f"qbf{buf}"], writes=[f"qbf{buf}"], dma=True)
        for fc in range(16):
            p.add("act", lambda e, md=md, fc=fc, buf=buf:
                  e.dma_start_transpose(stage[buf][:, fc * 128:(fc + 1) * 128],
                                        q_bf[buf][:, fc * 128:(fc + 1) * 128]),
                  reads=[f"qbf{buf}"], writes=[f"stage{buf}"], dma=True)
        p.add("sp", lambda e, md=md, buf=buf:
              e.dma_start(out=w_outT_s[md, :, :], in_=stage[buf][:, 0:16 * 128]),
              reads=[f"stage{buf}"], writes=[f"stage{buf}"], dma=True)

    for mc in range(MC):
        rows = 128 if mc < 33 else 32
        buf = mc % 2
        p.add("pool", lambda e, mc=mc, rows=rows, buf=buf:
              e.dma_start(out=q_bf[buf][0:rows, 0:D], in_=w_in_d[mc * 128:mc * 128 + rows, :]),
              reads=[f"qbf{buf}"], writes=[f"qbf{buf}"], dma=True)
        if rows < 128:
            p.add("dve", lambda e, buf=buf, rows=rows: e.memset(q_bf[buf][rows:64, 0:D], 0.0),
                  reads=[f"qbf{buf}"], writes=[f"qbf{buf}"])
            p.add("dve", lambda e, buf=buf: e.memset(q_bf[buf][64:128, 0:D], 0.0),
                  reads=[f"qbf{buf}"], writes=[f"qbf{buf}"])
        for k in range(8):
            p.add("act", lambda e, mc=mc, k=k, buf=buf:
                  e.dma_start_transpose(stage[buf][:, k * 128:(k + 1) * 128], q_bf[buf][:, k * 128:(k + 1) * 128]),
                  reads=[f"qbf{buf}"], writes=[f"stage{buf}"], dma=True)
        p.add("sp", lambda e, mc=mc, buf=buf:
              e.dma_start(out=w_inT_s[mc, :, 0:8 * 128], in_=stage[buf][:, 0:8 * 128]),
              reads=[f"stage{buf}"], writes=[f"stage{buf}"], dma=True)

    qchunk_i = [0]
    def quant_rows(src_slice, K, buf, tag):
        ci = qchunk_i[0]; qchunk_i[0] += 1
        p.add("sp", lambda e, s=src_slice: e.dma_start(out=q_f32[buf][:, 0:K], in_=s),
              reads=[f"qf{buf}"], writes=[f"qf{buf}"], dma=True)
        p.add("dve", lambda e: e.reduce_sum(q_col[:, 0:1], q_f32[buf][:, 0:K], axis=AX.X, apply_absolute_value=True),
              reads=[f"qf{buf}"], writes=["qcol"])
        p.add("dve", lambda e, K=K: e.tensor_scalar(q_col[:, 1:2], q_col[:, 0:1], 0.5 / K, 0.5e-5, ALU.mult, ALU.max),
              reads=["qcol"], writes=["qcol2"])
        p.add("dve", lambda e, K=K: e.tensor_scalar(q_col[:, 2:3], q_col[:, 0:1], 1.0 / K, 1e-5, ALU.mult, ALU.max),
              reads=["qcol"], writes=["qcol2"])
        p.add("act", lambda e, K=K: e.activation(q_aw[:, 0:K], q_f32[buf][:, 0:K], AF.Abs),
              reads=[f"qf{buf}"], writes=["qaw"])
        p.add("act", lambda e, K=K: e.activation(q_sg[:, 0:K], q_f32[buf][:, 0:K], AF.Sign),
              reads=[f"qf{buf}"], writes=["qsg"])
        p.add("dve", lambda e, K=K: e.tensor_scalar(q_aw2[:, 0:K], q_aw[:, 0:K], q_col[:, 1:2], None, ALU.is_gt),
              reads=["qaw", "qcol2"], writes=["qaw2"])
        p.add("dve", lambda e, K=K: e.scalar_tensor_tensor(q_bf[buf][:, 0:K], q_aw2[:, 0:K], q_col[:, 2:3], q_sg[:, 0:K], ALU.mult, ALU.mult),
              reads=["qaw2", "qsg", "qcol2"], writes=[f"qbf{buf}"])
        if debug:
            p.add("pool", lambda e, ci=ci: e.dma_start(out=scl_dbg[ci, :].unsqueeze(0).rearrange("a p -> (a p)").unsqueeze(1), in_=q_col[:, 2:3]),
                  reads=["qcol2"], dma=True)
            p.add("pool", lambda e, ci=ci: e.dma_start(out=sum_dbg[ci, :].unsqueeze(0).rearrange("a p -> (a p)").unsqueeze(1), in_=q_col[:, 0:1]),
                  reads=["qcol"], dma=True)

    for e_i in range(nexp):
        for fc in range(16):
            buf = fc % 2
            src = (w1_d[e_i, fc * 128:(fc + 1) * 128, :] if e_i < NE else sw1_d[fc * 128:(fc + 1) * 128, :])
            quant_rows(src, D, buf, f"w1_{e_i}_{fc}")
            for k in range(8):
                p.add("act", lambda e, k=k, buf=buf:
                      e.dma_start_transpose(stage[buf][:, k * 128:(k + 1) * 128], q_bf[buf][:, k * 128:(k + 1) * 128]),
                      reads=[f"qbf{buf}"], writes=[f"stage{buf}"], dma=True)
            p.add("sp", lambda e, e_i=e_i, fc=fc, buf=buf:
                  e.dma_start(out=w1T_s[e_i, fc, :, :], in_=stage[buf][:, 0:8 * 128]),
                  reads=[f"stage{buf}"], writes=[f"stage{buf}"], dma=True)
        for md in range(8):
            buf = md % 2
            src = (w2_d[e_i, md * 128:(md + 1) * 128, :] if e_i < NE else sw2_d[md * 128:(md + 1) * 128, :])
            quant_rows(src, DIN, buf, f"w2_{e_i}_{md}")
            for fc in range(16):
                p.add("act", lambda e, fc=fc, buf=buf:
                      e.dma_start_transpose(stage[buf][:, fc * 128:(fc + 1) * 128], q_bf[buf][:, fc * 128:(fc + 1) * 128]),
                      reads=[f"qbf{buf}"], writes=[f"stage{buf}"], dma=True)
            p.add("sp", lambda e, e_i=e_i, md=md, buf=buf:
                  e.dma_start(out=w2T_s[e_i, md, :, :], in_=stage[buf][:, 0:16 * 128]),
                  reads=[f"stage{buf}"], writes=[f"stage{buf}"], dma=True)

    if debug == "phase0":
        dbg("scl", scl_dbg, (256, 128), F32, reads=["qcol2"])
        dbg("sum", sum_dbg, (256, 128), F32, reads=["qcol"])
        dbg("w_inT", w_inT_s[...], (MC, 128, 8 * 128), BF16, reads=["stage0", "stage1"])
        dbg("w1T", w1T_s[...], (nexp, 16, 128, 8 * 128), BF16, reads=["stage0", "stage1"])
        dbg("w2T", w2T_s[...], (nexp, 8, 128, 16 * 128), BF16, reads=["stage0", "stage1"])
        dbg("w_outT_sb", w_outT_s[...], (8, 128, 16 * 128), BF16, reads=["stage0", "stage1"])
        dbg("rwT_sb", rwT[...], (128, 8 * NE), BF16, reads=["rwT"])
        dbg("drep", drep[...], (128, DIN), F32, reads=["drep"])
        dbg("convw", convw_sb[...], (128, MC * 4), F32, reads=["convw"])


    p.barrier()
    ph0.close()
    pa = contextlib.ExitStack()

    def sba(name, shape, dt, **kw):
        h = pa.enter_context(nc.sbuf_tensor(name, list(shape), dt, **kw))
        return h.ap()

    x_tm = sba("x_tm", (128, D), F32)
    t_tm = sba("t_tm", (128, D), F32)
    xn_tm = sba("xn_tm", (128, D), F32)
    h_tm = sba("h_tm", (128, D), F32)
    stat = sba("stat", (128, 16), F32)
    xn_fm = sba("xn_fm", (128, 8 * TILE), BF16)
    h_fm = sba("h_fm", (128, 8 * TILE), BF16)
    sz = sba("sz", (128, 16 * TILE), BF16)
    xh = sba("xh", (128, NCH * DIN), BF16)
    bc = sba("bc", (128, TILE), F32)
    bcn = sba("bcn", (64, TILE), F32)
    dtr = sba("dtr", (H, TILE), F32)
    dtw = sba("dtw", (H, TILE * 7), F32)
    cvp = [sba(f"cv{i}", (128, TILE), F32) for i in range(2)]
    zx = [sba(f"zx{i}", (128, TILE + 3), F32) for i in range(2)]
    tails = sba("tails", (128, MC * 3), F32)
    srep = sba("srep", (128, 16 * Q), F32)
    difft = sba("difft", (128, 16 * Q), F32)
    mt = [sba(f"mt{i}", (128, 16 * Q), BF16) for i in range(2)]
    cbt = [sba(f"cbt{i}", (128, Q), F32) for i in range(2)]
    stkT = [sba(f"stkT{i}", (128, 160), F32) for i in range(2)]
    expsT = [sba(f"expsT{i}", (128, H), F32) for i in range(2)]
    expsl = [sba(f"expsl{i}", (64, H), F32) for i in range(2)]
    wxh0 = sba("wxh0", (128, DIN), BF16)
    wxh = [wxh0, wxh0]
    bwb = [sba(f"bwb{i}", (128, 64), BF16) for i in range(2)]
    yevt = sba("yevt", (128, 512), F32)
    ysb = sba("ysb", (128, DIN), F32)
    ygat = sba("ygat", (128, 16 * TILE), BF16)
    attnb = sba("attnb", (128, TILE), F32)
    statebuf = [sba(f"state{i}", (64, DIN), F32) for i in range(2)]
    wt = [sba(f"wt{i}", (128, 8 * 128), BF16) for i in range(2)]
    wo = [sba(f"wo{i}", (128, 16 * 128), BF16) for i in range(2)]
    logit = sba("logit", (128, NE), F32)
    gtile = sba("gtile", (8, Q), F32)
    gE = sba("gE", (128, NE), F32)
    ones8 = sba("ones8", (128, NE), F32)
    gF = sba("gF", (128, NE), F32)
    ps = [pa.enter_context(nc.psum_tensor(f"ps{i}", [128, 512], F32)).ap()
          for i in range(8)]

    p.add("dve", lambda e: e.memset(tails[...], 0.0), writes=["tails"])
    p.add("dve", lambda e: e.memset(ones8[...], 1.0), writes=["ones8"])
    p.add("dve", lambda e: e.memset(statebuf[0][...], 0.0), writes=["st0"])

    xn_fm3 = xn_fm.rearrange("p (k t) -> p k t", t=TILE)
    h_fm3 = h_fm.rearrange("p (k t) -> p k t", t=TILE)
    sz3 = sz.rearrange("p (k t) -> p k t", t=TILE)
    xh3 = xh.rearrange("p (s f) -> p s f", f=DIN)
    ygat3 = ygat.rearrange("p (k t) -> p k t", t=TILE)
    dtw3 = dtw.rearrange("p (i t) -> p i t", t=TILE)
    D_DT, D_LN, D_S, D_SML, D_W, D_S1, D_S2 = range(7)
    D_A = D_S1

    def cw_col(mc, k):
        return convw_sb[:, mc * 4 + k:mc * 4 + k + 1]

    def dsl(i, cols=None):
        a = dtw3[:, i, :] if cols is None else dtw3[:, i, cols]
        return a

    def ln_and_transpose(it, s):
        t0 = it * TILE
        p.add("sp", lambda e, t0=t0, s=s: e.dma_start(out=x_tm[...], in_=x_d[t0 + s * 128:t0 + (s + 1) * 128, :]),
              reads=["x_tm"], writes=["x_tm"], dma=True)
        p.add("dve", lambda e: e.reduce_sum(stat[:, 0:1], x_tm[...], axis=AX.X), reads=["x_tm"], writes=["stat"])
        p.add("dve", lambda e: e.tensor_scalar_mul(stat[:, 1:2], stat[:, 0:1], 1.0 / D), reads=["stat"], writes=["stat"])
        p.add("dve", lambda e: e.tensor_scalar(t_tm[...], x_tm[...], stat[:, 1:2], None, ALU.subtract),
              reads=["x_tm", "stat"], writes=["t_tm"])
        p.add("act", lambda e: e.activation(xn_tm[...], t_tm[...], AF.Square, accum_out=stat[:, 2:3]),
              reads=["t_tm"], writes=["xn_tm", "stat"])
        p.add("act", lambda e: e.activation(stat[:, 3:4], stat[:, 2:3], AF.Ln, bias=cbias[:, 0:1], scale=1.0 / D),
              reads=["stat", "cbias"], writes=["stat"])
        p.add("act", lambda e: e.activation(stat[:, 4:5], stat[:, 3:4], AF.Exp, scale=-0.5),
              reads=["stat"], writes=["stat"])
        p.add("dve", lambda e: e.tensor_scalar_mul(xn_tm[...], t_tm[...], stat[:, 4:5]),
              reads=["t_tm", "stat"], writes=["xn_tm"])
        p.add("dve", lambda e: e.reduce_sum(stat[:, 5:6], xn_tm[...], axis=AX.X), reads=["xn_tm"], writes=["stat"])
        p.add("dve", lambda e: e.tensor_scalar_mul(stat[:, 6:7], stat[:, 5:6], 1.0 / D), reads=["stat"], writes=["stat"])
        p.add("dve", lambda e: e.tensor_scalar(t_tm[...], xn_tm[...], stat[:, 6:7], None, ALU.subtract),
              reads=["xn_tm", "stat"], writes=["t_tm"])
        p.add("act", lambda e: e.activation(h_tm[...], t_tm[...], AF.Square, accum_out=stat[:, 7:8]),
              reads=["t_tm"], writes=["h_tm", "stat"])
        p.add("act", lambda e: e.activation(stat[:, 8:9], stat[:, 7:8], AF.Ln, bias=cbias[:, 0:1], scale=1.0 / D),
              reads=["stat", "cbias"], writes=["stat"])
        p.add("act", lambda e: e.activation(stat[:, 9:10], stat[:, 8:9], AF.Exp, scale=-0.5),
              reads=["stat"], writes=["stat"])
        p.add("dve", lambda e: e.scalar_tensor_tensor(h_tm[...], t_tm[...], stat[:, 9:10], lngrep[...], ALU.mult, ALU.mult),
              reads=["t_tm", "stat", "lngrep"], writes=["h_tm"])
        p.add("dve", lambda e: e.tensor_add(t_tm[...], h_tm[...], lnbrep[...]),
              reads=["h_tm", "lnbrep"], writes=["t_tm"])
        for src_ap, src_b, dst3, dst_b in ((xn_tm, "xn_tm", xn_fm3, "xn_fm"),
                                           (t_tm, "t_tm", h_fm3, "h_fm")):
            for half in range(2):
                for k4 in range(4):
                    k = half * 4 + k4
                    p.add("pe", lambda e, src_ap=src_ap, k=k, k4=k4:
                          e.transpose(ps[2][:, k4 * 128:(k4 + 1) * 128], src_ap[:, k * 128:(k + 1) * 128], ident[...]),
                          reads=[src_b, "ident"], writes=["ps2"])
                p.add("dve", lambda e, dst3=dst3, half=half, s=s:
                      e.tensor_copy(dst3[:, half * 4:(half + 1) * 4, s * 128:(s + 1) * 128],
                                    ps[2][:, 0:512].rearrange("p (a b) -> p a b", b=128)),
                      reads=["ps2"], writes=[dst_b])
        for k in range(8):
            p.add("pe", lambda e, k=k, s=s: e.matmul(ps[2][:, 0:8], h_fm3[:, k, s * 128:(s + 1) * 128],
                                                     rwT[:, k * NE:(k + 1) * NE], start=(k == 0), stop=(k == 7)),
                  reads=["h_fm", "rwT"], writes=["ps2"])
        p.add("dve", lambda e: e.tensor_copy(logit[...], ps[2][:, 0:8]), reads=["ps2"], writes=["logit"])
        p.add("dve", lambda e: e.reduce_max(stat[:, 10:11], logit[...], axis=AX.X), reads=["logit"], writes=["stat"])
        p.add("dve", lambda e: e.tensor_scalar_mul(stat[:, 11:12], stat[:, 10:11], -1.0), reads=["stat"], writes=["stat"])
        p.add("act", lambda e: e.activation(gE[...], logit[...], AF.Exp, bias=stat[:, 11:12], accum_out=stat[:, 12:13]),
              reads=["logit", "stat"], writes=["gE", "stat"])
        p.add("dve", lambda e: e.reciprocal(stat[:, 13:14], stat[:, 12:13]), reads=["stat"], writes=["stat"])
        p.add("dve", lambda e: e.tensor_scalar(gE[...], logit[...], stat[:, 10:11], None, ALU.is_equal),
              reads=["logit", "stat"], writes=["gE"])
        p.add("dve", lambda e: e.tensor_tensor_scan(gF[...], ones8[...], gE[...], 0.0, ALU.mult, ALU.add),
              reads=["gE", "ones8"], writes=["gF"])
        p.add("dve", lambda e: e.tensor_scalar(gF[...], gF[...], 1.0, None, ALU.is_equal),
              reads=["gF"], writes=["gF"])
        p.add("dve", lambda e: e.tensor_mul(gE[...], gE[...], gF[...]), reads=["gE", "gF"], writes=["gE"])
        p.add("dve", lambda e: e.tensor_scalar_mul(gF[...], gE[...], stat[:, 13:14]), reads=["gE", "stat"], writes=["gF"])
        p.add("pe", lambda e: e.transpose(ps[2][0:8, 0:128], gF[...], ident[...]), reads=["gF", "ident"], writes=["ps2"])
        p.add("dve", lambda e: e.tensor_copy(gtile[...], ps[2][0:8, 0:128]), reads=["ps2"], writes=["gtile"])
        p.add("sp", lambda e, t0=t0, s=s: e.dma_start(out=g_s[:, t0 + s * 128:t0 + (s + 1) * 128], in_=gtile[...]),
              reads=["gtile"], dma=True)
        if debug == "phaseA2" and it == 0 and s == 0:
            dbg("logit", logit[...], (128, NE), F32, reads=["logit"])
            dbg("gE", gE[...], (128, NE), F32, reads=["gE"])

            dbg("gF", gF[...], (128, NE), F32, reads=["gF"])
            dbg("gtile", gtile[...], (8, Q), F32, reads=["gtile"])
            dbg("statv", stat[...], (128, 16), F32, reads=["stat"])

    def in_proj_conv(it):
        if debug == "phaseA1" and it == 0:
            dbg("xnfm", xn_fm[...], (128, 8 * TILE), BF16, reads=["xn_fm"])
        for mc in range(MC):
            b = mc % 2
            p.add("sp", lambda e, mc=mc, b=b: e.dma_start(out=wt[b][...], in_=w_inT_s[mc, :, :]),
                  reads=[f"wt{b}"], writes=[f"wt{b}"], dma=True)
            if debug == "phaseA1" and it == 0 and mc == 0:
                dbg("wt0", wt[b][...], (128, 8 * 128), BF16, reads=[f"wt{b}"])
            for k in range(8):
                p.add("pe", lambda e, k=k, b=b: e.matmul(ps[b][:, 0:512], wt[b][:, k * 128:(k + 1) * 128],
                                                         xn_fm3[:, k, :], start=(k == 0), stop=(k == 7)),
                      reads=[f"wt{b}", "xn_fm"], writes=[f"ps{b}"])
            p.add("dve", lambda e, mc=mc, b=b: e.tensor_copy(zx[b][:, 0:3], tails[:, mc * 3:(mc + 1) * 3]),
                  reads=["tails"], writes=[f"zx{b}"])
            p.add("act", lambda e, b=b: e.copy(zx[b][:, 3:3 + TILE], ps[b][:, 0:512]),
                  reads=[f"ps{b}"], writes=[f"zx{b}"])
            p.add("dve", lambda e, mc=mc, b=b: e.tensor_copy(tails[:, mc * 3:(mc + 1) * 3], zx[b][:, TILE:TILE + 3]),
                  reads=[f"zx{b}"], writes=["tails"])
            p.add("dve", lambda e, mc=mc, b=b:
                  e.scalar_tensor_tensor(cvp[0][...], zx[b][:, 0:TILE], cw_col(mc, 0),
                                         convb_sb[:, mc:mc + 1].to_broadcast([128, TILE]), ALU.mult, ALU.add),
                  reads=[f"zx{b}", "convw", "convb"], writes=["cv0"])
            for kk in range(1, 4):
                src_cv, dst_cv = (kk + 1) % 2, kk % 2
                p.add("dve", lambda e, mc=mc, b=b, kk=kk, src_cv=src_cv, dst_cv=dst_cv:
                      e.scalar_tensor_tensor(cvp[dst_cv][...], zx[b][:, kk:kk + TILE], cw_col(mc, kk),
                                             cvp[src_cv][...], ALU.mult, ALU.add),
                      reads=[f"zx{b}", "convw", f"cv{src_cv}"], writes=[f"cv{dst_cv}"])
            if debug == "phaseA1" and it == 0 and mc in (0, 16, 32, 33):
                dbg(f"zx{mc}", zx[b][...], (128, TILE + 3), F32, reads=[f"zx{b}"])
                dbg(f"cv{mc}", cvp[1][...], (128, TILE), F32, reads=["cv1"])
            if mc < 16:
                p.add("act", lambda e, mc=mc: e.activation(sz3[:, mc, :], cvp[1][...], AF.Silu),
                      reads=["cv1"], writes=["sz"])
            elif mc < 32:
                for s4 in range(4):
                    p.add("pe", lambda e, s4=s4: e.transpose(ps[2][:, s4 * 128:(s4 + 1) * 128],
                                                             cvp[1][:, s4 * 128:(s4 + 1) * 128], ident[...]),
                          reads=["cv1", "ident"], writes=["ps2"])
                p.add("dve", lambda e, mc=mc: e.tensor_copy(xh3[:, :, (mc - 16) * 128:(mc - 15) * 128],
                                                            ps[2][:, 0:512].rearrange("p (a b) -> p a b", b=128)),
                      reads=["ps2"], writes=["xh"])
            elif mc == 32:
                p.add("dve", lambda e: e.tensor_copy(bc[...], cvp[1][...]), reads=["cv1"], writes=["bc"])
                p.add("sp", lambda e: e.dma_start(out=bcC_s[...], in_=bc[64:128, :]),
                      reads=["bc"], writes=["bcCs"], dma=True)
                p.add("sp", lambda e: e.dma_start(out=bcn[...], in_=bcC_s[...]),
                      reads=["bcCs", "bcn"], writes=["bcn"], dma=True)
            else:
                p.add("dve", lambda e: e.tensor_copy(dtr[...], cvp[1][0:H, :]), reads=["cv1"], writes=["dtr"])

    def dt_pipeline(it):
        p.add("act", lambda e: e.activation(dsl(D_S1), dtr[...], AF.Abs, bias=dtb_sb[:, 0:1]),
              reads=["dtr", "dtb"], writes=["dtw"])
        p.add("act", lambda e: e.activation(dsl(D_S2), dsl(D_S1), AF.Exp, scale=-1.0), reads=["dtw"], writes=["dtw"])
        p.add("act", lambda e: e.activation(dsl(D_S1), dsl(D_S2), AF.Ln, bias=cbias[0:H, 1:2]), reads=["dtw", "cbias"], writes=["dtw"])
        p.add("act", lambda e: e.activation(dsl(D_S2), dtr[...], AF.Relu, bias=dtb_sb[:, 0:1]),
              reads=["dtr", "dtb"], writes=["dtw"])
        p.add("dve", lambda e: e.tensor_add(dsl(D_DT), dsl(D_S1), dsl(D_S2)), reads=["dtw"], writes=["dtw"])
        p.add("dve", lambda e: e.tensor_scalar_mul(dsl(D_A), dsl(D_DT), alog_sb[:, 0:1]),
              reads=["dtw", "alog"], writes=["dtw"])
        p.add("act", lambda e: e.activation(dsl(D_LN), dsl(D_DT), AF.Ln), reads=["dtw"], writes=["dtw"])
        for c in range(NCH):
            cc = slice(c * Q, (c + 1) * Q)
            p.add("dve", lambda e, cc=cc: e.tensor_tensor_scan(dsl(D_S, cc), ones32[0:H, 0:Q], dsl(D_A, cc),
                                                               0.0, ALU.mult, ALU.add),
                  reads=["dtw", "ones32"], writes=["dtw"])
        p.add("dve", lambda e: e.tensor_sub(dsl(D_SML), dsl(D_S), dsl(D_LN)), reads=["dtw"], writes=["dtw"])
        for c in range(NCH):
            cc = slice(c * Q, (c + 1) * Q)
            p.add("dve", lambda e, c=c, cc=cc: e.tensor_scalar(dsl(D_S1, cc), dsl(D_S, cc),
                                                               dtw3[:, D_S, c * Q + Q - 1:c * Q + Q], None, ALU.subtract),
                  reads=["dtw"], writes=["dtw"])
            p.add("act", lambda e, cc=cc: e.activation(dsl(D_S2, cc), dsl(D_S1, cc), AF.Exp, scale=-1.0),
                  reads=["dtw"], writes=["dtw"])
            p.add("dve", lambda e, cc=cc: e.tensor_mul(dsl(D_W, cc), dsl(D_DT, cc), dsl(D_S2, cc)),
                  reads=["dtw"], writes=["dtw"])

    def chunk_scan(it, c):
        gc = it * NCH + c
        slot = gc % 2
        pr = gc % 2
        sr, sw = f"st{gc % 2}", f"st{(gc + 1) % 2}"
        cc = slice(c * Q, (c + 1) * Q)
        wr_s = p.add("sp", lambda e, slot=slot, cc=cc:
                     e.dma_start(out=srep_s[slot, :].unsqueeze(0).rearrange("a (h q) -> (a h) q", q=Q), in_=dsl(D_S, cc)),
                     reads=["dtw", f"sreps{slot}"], writes=[f"sreps{slot}"], dma=True)
        p.add("pe", lambda e, cc=cc: e.transpose(ps[2][:, 0:64], bc[0:64, cc], ident[0:64, 0:64]),
              reads=["bc", "ident"], writes=["ps2"])
        p.add("pe", lambda e, cc=cc: e.transpose(ps[2][:, 64:96], dsl(D_W, cc), ident[0:H, 0:H]),
              reads=["dtw", "ident"], writes=["ps2"])
        p.add("pe", lambda e, cc=cc: e.transpose(ps[2][:, 96:128], dsl(D_SML, cc), ident[0:H, 0:H]),
              reads=["dtw", "ident"], writes=["ps2"])
        p.add("pe", lambda e, cc=cc: e.transpose(ps[2][:, 128:160], dsl(D_S, cc), ident[0:H, 0:H]),
              reads=["dtw", "ident"], writes=["ps2"])
        p.add("dve", lambda e, pr=pr: e.tensor_copy(stkT[pr][...], ps[2][:, 0:160]),
              reads=["ps2"], writes=[f"stkT{pr}"])
        p.add("act", lambda e, pr=pr: e.activation(expsT[pr][...], stkT[pr][:, 128:160], AF.Exp),
              reads=[f"stkT{pr}"], writes=[f"expsT{pr}"])
        p.add("pe", lambda e, cc=cc: e.matmul(ps[2][:, 0:128], bc[0:64, cc], bcn[0:64, cc], start=True, stop=True),
              reads=["bc", "bcn"], writes=["ps2"])
        p.add("act", lambda e, pr=pr: e.copy(cbt[pr][...], ps[2][:, 0:128]), reads=["ps2"], writes=[f"cbt{pr}"])
        p.add("dve", lambda e, c=c, pr=pr:
              e.tensor_tensor(wxh[pr][...].rearrange("p (h e) -> p h e", e=PDIM),
                              xh3[:, c, :].rearrange("p (h e) -> p h e", e=PDIM),
                              stkT[pr][:, 64:96].unsqueeze(2).broadcast_to([128, H, PDIM]), ALU.mult),
              reads=["xh", f"stkT{pr}"], writes=["wxh0"])
        p.add("dve", lambda e, pr=pr: e.tensor_copy(bwb[pr][...], stkT[pr][:, 0:64]),
              reads=[f"stkT{pr}"], writes=[f"bwb{pr}"])
        for half in range(2):
            rd = p.add("act", lambda e, slot=slot, half=half:
                       e.dma_start(out=srep[...], in_=srep_s[slot, half * 16 * Q:(half + 1) * 16 * Q]
                                   .unsqueeze(0).to_broadcast([128, 16 * Q])),
                       reads=[f"sreps{slot}", "srep"], writes=["srep"], dma=True, extra_deps=[wr_s])
            srep3 = srep.rearrange("p (h q) -> p h q", q=Q)
            p.add("act", lambda e, half=half, srep3=srep3, pr=pr:
                  e.activation(expsl[pr][:, half * 16:(half + 1) * 16], srep3[0:64, :, Q - 1], AF.Exp),
                  reads=["srep"], writes=[f"expsl{pr}"])
            p.add("dve", lambda e, half=half, pr=pr:
                  e.tensor_tensor(difft[...].rearrange("p (h q) -> p h q", q=Q),
                                  srep[...].rearrange("p (h q) -> p h q", q=Q),
                                  stkT[pr][:, 96 + half * 16:96 + (half + 1) * 16].unsqueeze(2).broadcast_to([128, 16, Q]),
                                  ALU.subtract),
                  reads=["srep", f"stkT{pr}"], writes=["difft"])
            p.add("dve", lambda e:
                  e.tensor_tensor(difft[...].rearrange("p (h q) -> p h q", q=Q),
                                  difft[...].rearrange("p (h q) -> p h q", q=Q),
                                  masklog[...].unsqueeze(1).broadcast_to([128, 16, Q]), ALU.add),
                  reads=["difft"], writes=["difft"])
            p.add("act", lambda e: e.activation(srep[...], difft[...], AF.Exp), reads=["difft"], writes=["srep"])
            p.add("dve", lambda e, half=half, pr=pr:
                  e.tensor_tensor(mt[pr][...].rearrange("p (h q) -> p h q", q=Q),
                                  srep[...].rearrange("p (h q) -> p h q", q=Q),
                                  cbt[pr][...].unsqueeze(1).broadcast_to([128, 16, Q]), ALU.mult),
                  reads=["srep", f"cbt{pr}"], writes=[f"mt{pr}"])
            chunk_headgroups(c, gc, cc, sr, sw, half, pr)

        p.add("pool", lambda e, c=c: e.tensor_mul(difft[:, 0:DIN], xh3[:, c, :], drep[...]),
              reads=["xh", "drep"], writes=["difft"])
        p.add("pool", lambda e: e.tensor_add(ysb[...], ysb[...], difft[:, 0:DIN]), reads=["ysb", "difft"], writes=["ysb"])
        p.add("act", lambda e: e.activation(difft[:, 0:DIN], ysb[...], AF.Square, accum_out=stat[:, 14:15]),
              reads=["ysb"], writes=["difft", "stat"])
        p.add("act", lambda e: e.activation(stat[:, 15:16], stat[:, 14:15], AF.Ln, bias=cbias[:, 2:3], scale=1.0 / DIN),
              reads=["stat", "cbias"], writes=["stat"])
        p.add("act", lambda e: e.activation(stat[:, 14:15], stat[:, 15:16], AF.Exp, scale=-0.5),
              reads=["stat"], writes=["stat"])
        p.add("dve", lambda e: e.scalar_tensor_tensor(ysb[...], ysb[...], stat[:, 14:15], nwrep[...],
                                                      ALU.mult, ALU.mult),
              reads=["ysb", "stat", "nwrep"], writes=["ysb"])
        for q4 in range(4):
            for j in range(4):
                fc = q4 * 4 + j
                p.add("pe", lambda e, fc=fc, j=j: e.transpose(ps[6][:, j * 128:(j + 1) * 128],
                                                              ysb[:, fc * 128:(fc + 1) * 128], ident[...]),
                      reads=["ysb", "ident"], writes=["ps6"])
            p.add("dve", lambda e, q4=q4, c=c:
                  e.tensor_mul(ygat3[:, q4 * 4:(q4 + 1) * 4, c * 128:(c + 1) * 128],
                               ps[6][:, 0:512].rearrange("p (a b) -> p a b", b=128),
                               sz3[:, q4 * 4:(q4 + 1) * 4, c * 128:(c + 1) * 128]),
                  reads=["ps6", "sz"], writes=["ygat"])
        if debug == "phaseA2" and it == 0 and c == 0:
            dbg("stkT", stkT[0][...], (128, 160), F32, reads=["stkT0"])
            dbg("cbt", cbt[0][...], (128, Q), F32, reads=["cbt0"])
            dbg("mtd", mt[0][...], (128, 16 * Q), BF16, reads=["mt0"])
            dbg("ysbd", ysb[...], (128, DIN), F32, reads=["ysb"])
            dbg("st1d", statebuf[1][...], (64, DIN), F32, reads=["st1"])
            dbg("expsTd", expsT[0][...], (128, H), F32, reads=["expsT0"])
            dbg("expsld", expsl[0][...], (64, H), F32, reads=["expsl0"])
            dbg("bcd", bc[...], (128, TILE), F32, reads=["bc"])
            dbg("bcnd", bcn[...], (64, TILE), F32, reads=["bcn"])

    def chunk_headgroups(c, gc, cc, sr, sw, half, pr):
        for hg in (half * 2, half * 2 + 1):
            j0 = (hg % 2) * 8
            for j in range(8):
                h_ = hg * 8 + j
                p.add("pe", lambda e, h_=h_, j=j, c=c, pr=pr: e.matmul(ps[3][:, j * 64:(j + 1) * 64],
                                                                mt[pr][:, (h_ % 16) * Q:((h_ % 16) + 1) * Q],
                                                                xh3[:, c, h_ * 64:(h_ + 1) * 64], start=True, stop=True),
                      reads=[f"mt{pr}", "xh"], writes=["ps3"])
            p.add("pe", lambda e, hg=hg, cc=cc, gc=gc: e.matmul(ps[4][:, 0:512],
                                                                bcn[0:64, cc],
                                                                statebuf[gc % 2][:, hg * 512:(hg + 1) * 512],
                                                                start=True, stop=True),
                  reads=["bcn", sr], writes=["ps4"])
            p.add("pe", lambda e, hg=hg, c=c, pr=pr: e.matmul(ps[5][0:64, 0:512],
                                                              bwb[pr][...],
                                                              wxh[pr][:, hg * 512:(hg + 1) * 512], start=True, stop=True),
                  reads=[f"bwb{pr}", "wxh0"], writes=["ps5"])
            p.add("act", lambda e, hg=hg: e.copy(ysb[:, hg * 512:(hg + 1) * 512], ps[3][:, 0:512]),
                  reads=["ps3"], writes=["ysb"])
            p.add("dve", lambda e, hg=hg, pr=pr:
                  e.tensor_tensor(yevt[...].rearrange("p (h e) -> p h e", e=PDIM),
                                  ps[4][:, 0:512].rearrange("p (h e) -> p h e", e=PDIM),
                                  expsT[pr][:, hg * 8:(hg + 1) * 8].unsqueeze(2).broadcast_to([128, 8, PDIM]),
                                  ALU.mult),
                  reads=["ps4", f"expsT{pr}"], writes=["yevt"])
            p.add("dve", lambda e, hg=hg: e.tensor_add(ysb[:, hg * 512:(hg + 1) * 512],
                                                       ysb[:, hg * 512:(hg + 1) * 512], yevt[...]),
                  reads=["yevt", "ysb"], writes=["ysb"])
            p.add("dve", lambda e, hg=hg, gc=gc, pr=pr:
                  e.tensor_tensor(yevt[0:64, :].rearrange("p (h e) -> p h e", e=PDIM),
                                  statebuf[gc % 2][:, hg * 512:(hg + 1) * 512].rearrange("p (h e) -> p h e", e=PDIM),
                                  expsl[pr][:, hg * 8:(hg + 1) * 8].unsqueeze(2).broadcast_to([64, 8, PDIM]),
                                  ALU.mult),
                  reads=[sr, f"expsl{pr}"], writes=["yevt"])
            p.add("dve", lambda e, hg=hg, gc=gc:
                  e.tensor_add(statebuf[(gc + 1) % 2][:, hg * 512:(hg + 1) * 512], yevt[0:64, :],
                               ps[5][0:64, 0:512]),
                  reads=["yevt", "ps5"], writes=[sw])


    def out_proj(it):
        t0 = it * TILE
        for md in range(8):
            b = md % 2
            p.add("sp", lambda e, md=md, b=b: e.dma_start(out=wo[b][...], in_=w_outT_s[md, :, :]),
                  reads=[f"wo{b}"], writes=[f"wo{b}"], dma=True)
            for fc in range(16):
                p.add("pe", lambda e, fc=fc, b=b: e.matmul(ps[7][:, 0:512], wo[b][:, fc * 128:(fc + 1) * 128],
                                                           ygat3[:, fc, :], start=(fc == 0), stop=(fc == 15)),
                      reads=[f"wo{b}", "ygat"], writes=["ps7"])
            p.add("act", lambda e, md=md: e.copy(attnb[...], ps[7][:, 0:512]),
                  reads=["ps7"], writes=["attnb"])
            p.add("sp", lambda e, md=md, t0=t0: e.dma_start(out=attn_s[md, :, t0:t0 + TILE], in_=attnb[...]),
                  reads=["attnb"], dma=True)

    for it in range(ntiles):
        for s in range(NCH):
            ln_and_transpose(it, s)
        in_proj_conv(it)
        dt_pipeline(it)
        for c in range(NCH):
            chunk_scan(it, c)
        out_proj(it)
        t0 = it * TILE
        for k in range(8):
            p.add("act", lambda e, k=k, t0=t0: e.dma_start(out=h_s[k, :, t0:t0 + TILE], in_=h_fm3[:, k, :]),
                  reads=["h_fm"], dma=True)

    if debug in ("phaseA", "phaseA1", "phaseA2"):
        p.barrier()
        dbg("attn", attn_s[...], (8, 128, T), F32)
        dbg("h_s", h_s[...], (8, 128, T), BF16)
        dbg("g_s", g_s[...], (NE, T), F32)
        dbg("sz", sz[...], (128, 16 * TILE), BF16, reads=["sz"])
        dbg("xh", xh[...], (128, NCH * DIN), BF16, reads=["xh"])
        dbg("dtw", dtw[...], (H, TILE * 7), F32, reads=["dtw"])
        dbg("state", statebuf[(ntiles * NCH) % 2][...], (64, DIN), F32, reads=[f"st{(ntiles * NCH) % 2}"])


    p.barrier()
    pa.close()
    pb = contextlib.ExitStack()

    def sbb_(name, shape, dt, **kw):
        h = pb.enter_context(nc.sbuf_tensor(name, list(shape), dt, **kw))
        return h.ap()

    acc = sbb_("acc", (128, 8 * TB), F32)
    hb = sbb_("hb", (128, 8 * TB), BF16)
    h1g = sbb_("h1g", (128, 16 * TB), BF16)
    w1b = [sbb_(f"w1b{i}", (128, 8 * 128), BF16) for i in range(4)]
    w2b = [sbb_(f"w2b{i}", (128, 16 * 128), BF16) for i in range(2)]
    grep = [sbb_(f"grep{i}", (128, TB), F32) for i in range(2)]
    h1t = [sbb_(f"h1t{i}", (128, 512), BF16) for i in range(4)]
    xb = sbb_("xb", (128, D), F32)
    bident = sbb_("bident", (128, 128), F32)
    psb = [pb.enter_context(nc.psum_tensor(f"pb{i}", [128, 512], F32)).ap()
           for i in range(8)]
    acc3 = acc.rearrange("p (k t) -> p k t", t=TB)
    hb3 = hb.rearrange("p (k t) -> p k t", t=TB)
    h1g3 = h1g.rearrange("p (k t) -> p k t", t=TB)

    p.add("sp", lambda e: e.dma_start(out=bident[...], in_=ident_d[...]), writes=["bident"], dma=True)

    for bi in range(nblocks):
        t0 = bi * TB
        for md in range(8):
            p.add("sp", lambda e, md=md, t0=t0: e.dma_start(out=acc3[:, md, :], in_=attn_s[md, :, t0:t0 + TB]),
                  reads=["acc"], writes=["acc"], dma=True)
            p.add("act", lambda e, md=md, t0=t0: e.dma_start(out=hb3[:, md, :], in_=h_s[md, :, t0:t0 + TB]),
                  reads=["hb"], writes=["hb"], dma=True)
        for s in range(TB // 128):
            p.add("sp", lambda e, s=s, t0=t0: e.dma_start(out=xb[...], in_=x_d[t0 + s * 128:t0 + (s + 1) * 128, :]),
                  reads=["xb"], writes=["xb"], dma=True)
            for half in range(2):
                for k4 in range(4):
                    k = half * 4 + k4
                    p.add("pe", lambda e, k=k, k4=k4: e.transpose(psb[2][:, k4 * 128:(k4 + 1) * 128],
                                                                  xb[:, k * 128:(k + 1) * 128], bident[...]),
                          reads=["xb", "bident"], writes=["pb2"])
                p.add("dve", lambda e, half=half, s=s:
                      e.tensor_add(acc3[:, half * 4:(half + 1) * 4, s * 128:(s + 1) * 128],
                                   psb[2][:, 0:512].rearrange("p (a b) -> p a b", b=128),
                                   acc3[:, half * 4:(half + 1) * 4, s * 128:(s + 1) * 128]),
                      reads=["pb2", "acc"], writes=["acc"])
        for e_i in range(nexp):
            ge = e_i % 2
            if e_i < NE:
                p.add("sp", lambda e, e_i=e_i, ge=ge, t0=t0:
                      e.dma_start(out=grep[ge][...], in_=g_s[e_i, t0:t0 + TB].unsqueeze(0).to_broadcast([128, TB])),
                      reads=[f"grep{ge}"], writes=[f"grep{ge}"], dma=True)
            else:
                p.add("dve", lambda e, ge=ge: e.memset(grep[ge][...], 1.0),
                      reads=[f"grep{ge}"], writes=[f"grep{ge}"])
            for fc in range(16):
                wb = fc % 4
                p.add("sp", lambda e, e_i=e_i, fc=fc, wb=wb: e.dma_start(out=w1b[wb][...], in_=w1T_s[e_i, fc, :, :]),
                      reads=[f"w1b{wb}"], writes=[f"w1b{wb}"], dma=True)
                for ts2 in range(TB // 512):
                    pi = (fc % 2) * 2 + ts2
                    pbk = f"pb{pi}"
                    hi = (fc % 2) * 2 + ts2
                    for k in range(8):
                        p.add("pe", lambda e, fc=fc, wb=wb, ts2=ts2, k=k, pi=pi:
                              e.matmul(psb[pi][:, 0:512], w1b[wb][:, k * 128:(k + 1) * 128],
                                       hb3[:, k, ts2 * 512:(ts2 + 1) * 512], start=(k == 0), stop=(k == 7)),
                              reads=[f"w1b{wb}", "hb"], writes=[pbk])
                    p.add("act", lambda e, ts2=ts2, pi=pi, hi=hi: e.activation(h1t[hi][...], psb[pi][:, 0:512], AF.Silu),
                          reads=[pbk], writes=[f"h1t{hi}"])
                    p.add("dve", lambda e, fc=fc, ts2=ts2, ge=ge, hi=hi:
                          e.tensor_mul(h1g3[:, fc, ts2 * 512:(ts2 + 1) * 512], h1t[hi][...],
                                       grep[ge][:, ts2 * 512:(ts2 + 1) * 512]),
                          reads=[f"h1t{hi}", f"grep{ge}"], writes=["h1g"])
            for md in range(8):
                wb = md % 2
                p.add("act", lambda e, e_i=e_i, md=md, wb=wb: e.dma_start(out=w2b[wb][...], in_=w2T_s[e_i, md, :, :]),
                      reads=[f"w2b{wb}"], writes=[f"w2b{wb}"], dma=True)
                for ts2 in range(TB // 512):
                    pi = 4 + (md % 2) * 2 + ts2
                    pbk = f"pb{pi}"
                    for fc in range(16):
                        p.add("pe", lambda e, md=md, wb=wb, ts2=ts2, fc=fc, pi=pi:
                              e.matmul(psb[pi][:, 0:512], w2b[wb][:, fc * 128:(fc + 1) * 128],
                                       h1g3[:, fc, ts2 * 512:(ts2 + 1) * 512], start=(fc == 0), stop=(fc == 15)),
                              reads=[f"w2b{wb}", "h1g"], writes=[pbk])
                    p.add("dve", lambda e, md=md, ts2=ts2, pi=pi:
                          e.tensor_add(acc3[:, md, ts2 * 512:(ts2 + 1) * 512], psb[pi][:, 0:512],
                                       acc3[:, md, ts2 * 512:(ts2 + 1) * 512]),
                          reads=[pbk, "acc"], writes=["acc"])
        for md in range(8):
            p.add("sp", lambda e, md=md, t0=t0: e.dma_start(out=out_d[md, :, t0:t0 + TB], in_=acc3[:, md, :]),
                  reads=["acc"], dma=True)

    with nc.Block() as block:
        stack = p.emit(nc, block)
    stack.close()
    return nc, dbg_outs


def _mk_consts():
    ident = np.eye(128, dtype=np.float32)
    j = np.arange(128)[:, None]
    i = np.arange(128)[None, :]
    masklogT = np.where(i >= j, 0.0, NEG_BIG).astype(np.float32)
    iota8 = np.broadcast_to(np.arange(NE, dtype=np.float32)[None, :], (128, NE)).copy()
    return ident, masklogT, iota8


def _core_inputs(inputs, b):
    ident, masklogT, iota8 = _mk_consts()
    return {
        "x": np.ascontiguousarray(inputs["L0_attn_u"][b]),
        "w_in": inputs["L0_attn_in_proj_w"],
        "conv_w": inputs["L0_attn_conv_w"],
        "conv_b": inputs["L0_attn_conv_b"],
        "A_log": inputs["L0_attn_A_log"],
        "Dparam": inputs["L0_attn_Dparam"],
        "dt_bias": inputs["L0_attn_dt_bias"],
        "mnorm_w": inputs["L0_attn_norm_w"],
        "w_out": inputs["L0_attn_out_proj_w"],
        "w1": inputs["L0_ffn_expert_w1"],
        "w2": inputs["L0_ffn_expert_w2"],
        "sw1": inputs["L0_ffn_shared_w1"],
        "sw2": inputs["L0_ffn_shared_w2"],
        "rw": inputs["L0_ffn_router_w"],
        "ln_g": inputs["L0_ffn_ln_gamma"],
        "ln_b": inputs["L0_ffn_ln_beta"],
        "ident": ident,
        "masklogT": masklogT,
        "iota8": iota8,
    }


def kernel(**inputs):
    nc, _ = build_program()
    in_maps = [_core_inputs(inputs, b) for b in range(8)]
    res = run_bass_kernel_spmd(nc, in_maps, core_ids=list(range(8)))
    outs = []
    for r in res.results:
        o = np.asarray(r["out_fm"])
        outs.append(o.reshape(D, T).T)
    return np.stack(outs).astype(np.float32)


# version 05
# speedup vs baseline: 1.7828x; 1.6735x over previous
import os
import numpy as np
import ml_dtypes
import concourse.bass as bass
import concourse.mybir as mybir
from concourse.bass_utils import run_bass_kernel_spmd

F32 = mybir.dt.float32
BF16 = mybir.dt.bfloat16
AF = mybir.ActivationFunctionType
ALU = mybir.AluOpType
AX = mybir.AxisListType

T = 8192
D = 1024
CIN = 4256
DIN = 2048
H = 32
PDIM = 64
NST = 64
Q = 128
NE = 8
TILE = 512
NTILES = T // TILE
NCH = TILE // Q
MC = CIN // 128 + (1 if CIN % 128 else 0)
TB = 1024
NB = T // TB
LN_EPS = 1e-5
F32EPS = float(np.finfo(np.float32).eps)
NEG_BIG = -1e30

DEBUG = os.environ.get("KDEBUG", "")


ENGS = ("pe", "act", "dve", "pool", "sp")
DMAQ = ("sp", "act", "pool")


class Prog:
    def __init__(self):
        self.nodes = []
        self.writers = {}
        self.readers = {}
        self.bar = None

    def barrier(self):
        last = {}
        for i, n in enumerate(self.nodes):
            last[(n["eng"], n["dma"])] = i
        self.bar = None
        nid = self.add("sp", lambda e: e.nop(), extra_deps=list(last.values()))
        self.bar = nid
        return nid

    def add(self, eng, fn, reads=(), writes=(), dma=False, extra_deps=()):
        raw = set(int(d) for d in extra_deps if d is not None)
        war = set()
        if self.bar is not None:
            raw.add(self.bar)
        for b in reads:
            raw.update(self.writers.get(b, ()))
        for b in writes:
            war.update(self.writers.get(b, ()))
            war.update(self.readers.get(b, ()))
        nid = len(self.nodes)
        deps = set()
        for d in raw:
            if d < nid:
                deps.add(d)
        for d in war:
            if d < nid and (self.nodes[d]["dma"] or dma or self.nodes[d]["eng"] != eng):
                deps.add(d)
        self.nodes.append(dict(eng=eng, fn=fn, dma=bool(dma), deps=sorted(deps)))
        for b in reads:
            self.readers.setdefault(b, []).append(nid)
        for b in writes:
            if self.readers.get(b):
                self.writers[b] = [nid]
                self.readers[b] = []
            else:
                self.writers.setdefault(b, []).append(nid)
        return nid

    def emit(self, nc, block):
        nodes = self.nodes
        need_inc = [False] * len(nodes)
        for n in nodes:
            red = {}
            for d in n["deps"]:
                k = (nodes[d]["eng"], nodes[d]["dma"])
                if red.get(k, -1) < d:
                    red[k] = d
            n["deps"] = sorted(red.values())
            for d in n["deps"]:
                if not nodes[d]["dma"]:
                    need_inc[d] = True
        eng_inc_prefix = {}
        dma_prefix = {}
        cnt = {e: 0 for e in ENGS}
        dcnt = {e: 0 for e in ENGS}
        for i, n in enumerate(nodes):
            if n["dma"]:
                dcnt[n["eng"]] += 1
                dma_prefix[i] = dcnt[n["eng"]]
            else:
                if need_inc[i]:
                    cnt[n["eng"]] += 1
                eng_inc_prefix[i] = cnt[n["eng"]]
        sems = {}
        dsems = {}
        import contextlib
        stack = contextlib.ExitStack()
        for e in ENGS:
            sems[e] = stack.enter_context(nc.semaphore(f"s_{e}"))
        for e in DMAQ:
            dsems[e] = stack.enter_context(nc.semaphore(f"d_{e}"))

        def emit_for(eng_name):
            def _(eng):
                waited = {}
                for i, n in enumerate(nodes):
                    if n["eng"] != eng_name:
                        continue
                    for d in n["deps"]:
                        dn = nodes[d]
                        if dn["dma"]:
                            sem, val = dsems[dn["eng"]], 16 * dma_prefix[d]
                        else:
                            sem, val = sems[dn["eng"]], eng_inc_prefix[d]
                        key = id(sem)
                        if waited.get(key, -1) >= val:
                            continue
                        eng.wait_ge(sem, val)
                        waited[key] = val
                    ins = n["fn"](eng)
                    if n["dma"]:
                        ins.then_inc(dsems[eng_name], 16)
                    elif need_inc[i]:
                        if ins is None:
                            eng.nop().then_inc(sems[eng_name], 1)
                        else:
                            ins.then_inc(sems[eng_name], 1)
                for e in DMAQ:
                    if dcnt[e]:
                        eng.wait_ge(dsems[e], 16 * dcnt[e])
            return _

        block.sync(emit_for("sp"))
        block.scalar(emit_for("act"))
        block.vector(emit_for("dve"))
        block.tensor(emit_for("pe"))
        block.gpsimd(emit_for("pool"))
        return stack


def eng_obj(nc, name):
    return {"pe": nc.tensor, "act": nc.scalar, "dve": nc.vector,
            "pool": nc.gpsimd, "sp": nc.sync}[name]


def build_program(ntiles=NTILES, nblocks=NB, nexp=NE + 1, debug=DEBUG):
    nc = bass.Bass("TRN2", target_bir_lowering=False)
    p = Prog()

    def din(name, shape, dt=F32):
        return nc.dram_tensor(name, list(shape), dt, kind="ExternalInput").ap()

    x_d = din("x", (T, D))
    w_in_d = din("w_in", (CIN, D))
    convw_d = din("conv_w", (CIN, 4))
    convb_d = din("conv_b", (CIN,))
    alog_d = din("A_log", (H,))
    dparam_d = din("Dparam", (DIN,))
    dtb_d = din("dt_bias", (H,))
    mnw_d = din("mnorm_w", (DIN,))
    w_out_d = din("w_out", (D, DIN))
    w1_d = din("w1", (NE, DIN, D))
    w2_d = din("w2", (NE, D, DIN))
    sw1_d = din("sw1", (DIN, D))
    sw2_d = din("sw2", (D, DIN))
    rw_d = din("rw", (NE, D))
    lng_d = din("ln_g", (D,))
    lnb_d = din("ln_b", (D,))
    ident_d = din("ident", (128, 128))
    masklog_d = din("masklogT", (128, 128))
    iota8_d = din("iota8", (128, NE))

    out_d = nc.dram_tensor("out_fm", [8, 128, T], F32, kind="ExternalOutput").ap()

    def dscratch(name, shape, dt):
        return nc.dram_tensor(name, list(shape), dt, kind="Internal").ap()

    w_inT_s = dscratch("w_inT_s", (MC, 128, 8 * 128), BF16)
    w_outT_s = dscratch("w_outT_s", (8, 128, 16 * 128), BF16)
    w1T_s = dscratch("w1T_s", (nexp, 16, 128, 8 * 128), BF16)
    w2T_s = dscratch("w2T_s", (nexp, 8, 128, 16 * 128), BF16)
    h_s = dscratch("h_s", (8, 128, T), BF16)
    g_s = dscratch("g_s", (NE, T), F32)
    attn_s = dscratch("attn_s", (8, 128, T), F32)
    srep_s = dscratch("srep_s", (2, H * Q), F32)
    bcC_s = dscratch("bcC_s", (64, TILE), F32)
    scl_dbg = dscratch("scl_dbg", (256, 128), F32) if debug else None
    sum_dbg = dscratch("sum_dbg", (256, 128), F32) if debug else None

    dbg_outs = {}

    def dbg(name, src_ap, shape, dt=F32, reads=(), eng="sp"):
        if not debug:
            return
        o = nc.dram_tensor("dbg_" + name, list(shape), dt, kind="ExternalOutput").ap()
        dbg_outs[name] = o
        p.add(eng, lambda e, o=o, s=src_ap: e.dma_start(out=o[...], in_=s),
              reads=list(reads), dma=True)

    sb_allocs = []

    def sb(name, shape, dt, **kw):
        h = nc.alloc_sbuf_tensor(name, list(shape), dt, **kw)
        sb_allocs.append(h)
        return h.ap()

    ident = sb("sb_ident", (128, 128), F32)
    masklog = sb("sb_masklog", (128, 128), F32)
    iota8 = sb("sb_iota8", (128, NE), F32)
    convw_sb = sb("convw_sb", (128, MC * 4), F32)
    convb_sb = sb("convb_sb", (128, MC), F32)
    alog_sb = sb("alog_sb", (H, 1), F32)
    dtb_sb = sb("dtb_sb", (H, 1), F32)
    drep = sb("drep", (128, DIN), BF16)
    nwrep = sb("nwrep", (128, DIN), F32)
    lngrep = sb("lngrep", (128, D), F32)
    lnbrep = sb("lnbrep", (128, D), F32)
    ones32 = sb("ones32", (H, Q), F32)
    rwT = sb("rwT", (128, 8 * NE), BF16)
    cbias = sb("cbias", (128, 4), F32)

    import contextlib
    ph0 = contextlib.ExitStack()

    def sb0(name, shape, dt, **kw):
        h = ph0.enter_context(nc.sbuf_tensor(name, list(shape), dt, **kw))
        return h.ap()

    q_f32 = [sb0(f"q_f32_{i}", (128, DIN), F32) for i in range(2)]
    q_aw = sb0("q_aw", (128, DIN), F32)
    q_aw2 = sb0("q_aw2", (128, DIN), F32)
    q_sg = sb0("q_sg", (128, DIN), F32)
    q_bf = [sb0(f"q_bf_{i}", (128, DIN), BF16) for i in range(2)]
    q_col = sb0("q_col", (128, 4), F32)
    stage = [sb0(f"stage_{i}", (128, 16 * 128), BF16, is_transpose_dest=True)
             for i in range(2)]
    rwpad = sb0("rwpad", (16, D), BF16, is_transpose_dest=True)
    rwtr = sb0("rwtr", (128, 16), BF16, is_transpose_dest=True)

    def A(e):
        return e

    p.add("sp", lambda e: e.dma_start(out=ident[...], in_=ident_d[...]), writes=["ident"], dma=True)
    p.add("sp", lambda e: e.dma_start(out=masklog[...], in_=masklog_d[...]), writes=["masklog"], dma=True)
    p.add("sp", lambda e: e.dma_start(out=iota8[...], in_=iota8_d[...]), writes=["iota8"], dma=True)
    p.add("sp", lambda e: e.dma_start(out=convw_sb[:, 0:33 * 4].rearrange("p (m k) -> p m k", k=4), in_=convw_d[0:33 * 128, :].rearrange("(m p) k -> p m k", p=128)),
          writes=["convw"], dma=True)
    p.add("sp", lambda e: e.dma_start(out=convw_sb[0:32, 33 * 4:34 * 4], in_=convw_d[33 * 128:CIN, :]),
          writes=["convw"], dma=True)
    p.add("sp", lambda e: e.dma_start(out=convb_sb[:, 0:33], in_=convb_d[0:33 * 128].rearrange("(m p) -> p m", p=128), allow_slow_non_contiguous=True),
          writes=["convb"], dma=True)
    p.add("sp", lambda e: e.dma_start(out=convb_sb[0:32, 33:34], in_=convb_d[33 * 128:CIN].unsqueeze(1)),
          writes=["convb"], dma=True)
    p.add("sp", lambda e: e.dma_start(out=alog_sb[...], in_=alog_d.unsqueeze(1)), writes=["alog"], dma=True)
    p.add("sp", lambda e: e.dma_start(out=dtb_sb[...], in_=dtb_d.unsqueeze(1)), writes=["dtb"], dma=True)
    p.add("pool", lambda e: e.dma_start(out=drep[...], in_=dparam_d.unsqueeze(0).to_broadcast([128, DIN])),
          writes=["drep"], dma=True)
    p.add("act", lambda e: e.dma_start(out=nwrep[...], in_=mnw_d.unsqueeze(0).to_broadcast([128, DIN])),
          writes=["nwrep"], dma=True)
    p.add("act", lambda e: e.dma_start(out=lngrep[...], in_=lng_d.unsqueeze(0).to_broadcast([128, D])),
          writes=["lngrep"], dma=True)
    p.add("act", lambda e: e.dma_start(out=lnbrep[...], in_=lnb_d.unsqueeze(0).to_broadcast([128, D])),
          writes=["lnbrep"], dma=True)
    p.add("dve", lambda e: e.memset(ones32[...], 1.0), writes=["ones32"])
    p.add("dve", lambda e: e.memset(cbias[:, 0:1], LN_EPS), writes=["cbias"])
    p.add("dve", lambda e: e.memset(cbias[:, 1:2], 1.0), reads=["cbias"], writes=["cbias"])
    p.add("dve", lambda e: e.memset(cbias[:, 2:3], F32EPS), reads=["cbias"], writes=["cbias"])
    p.add("dve", lambda e: e.memset(convw_sb[32:64, 33 * 4:34 * 4], 0.0), reads=["convw"], writes=["convw"])
    p.add("dve", lambda e: e.memset(convw_sb[64:128, 33 * 4:34 * 4], 0.0), reads=["convw"], writes=["convw"])
    p.add("dve", lambda e: e.memset(convb_sb[32:64, 33:34], 0.0), reads=["convb"], writes=["convb"])
    p.add("dve", lambda e: e.memset(convb_sb[64:128, 33:34], 0.0), reads=["convb"], writes=["convb"])

    p.add("dve", lambda e: e.memset(rwpad[...], 0.0), writes=["rwpad"])
    p.add("pool", lambda e: e.dma_start(out=rwpad[0:NE, :], in_=rw_d[...]), reads=["rwpad"], writes=["rwpad"], dma=True)
    for k in range(8):
        p.add("act", lambda e, k=k: e.dma_start_transpose(rwtr[...], rwpad[:, k * 128:(k + 1) * 128]),
              reads=["rwpad"], writes=["rwtr"], dma=True)
        p.add("dve", lambda e, k=k: e.tensor_copy(rwT[:, k * NE:(k + 1) * NE], rwtr[:, 0:NE]),
              reads=["rwtr"], writes=["rwT"])

    for md in range(8):
        buf = md % 2
        p.add("pool", lambda e, md=md, buf=buf: e.dma_start(out=q_bf[buf][...], in_=w_out_d[md * 128:(md + 1) * 128, :]),
              reads=[f"qbf{buf}"], writes=[f"qbf{buf}"], dma=True)
        for fc in range(16):
            p.add("act", lambda e, md=md, fc=fc, buf=buf:
                  e.dma_start_transpose(stage[buf][:, fc * 128:(fc + 1) * 128],
                                        q_bf[buf][:, fc * 128:(fc + 1) * 128]),
                  reads=[f"qbf{buf}"], writes=[f"stage{buf}"], dma=True)
        p.add("sp", lambda e, md=md, buf=buf:
              e.dma_start(out=w_outT_s[md, :, :], in_=stage[buf][:, 0:16 * 128]),
              reads=[f"stage{buf}"], writes=[f"stage{buf}"], dma=True)

    for mc in range(MC):
        rows = 128 if mc < 33 else 32
        buf = mc % 2
        p.add("pool", lambda e, mc=mc, rows=rows, buf=buf:
              e.dma_start(out=q_bf[buf][0:rows, 0:D], in_=w_in_d[mc * 128:mc * 128 + rows, :]),
              reads=[f"qbf{buf}"], writes=[f"qbf{buf}"], dma=True)
        if rows < 128:
            p.add("dve", lambda e, buf=buf, rows=rows: e.memset(q_bf[buf][rows:64, 0:D], 0.0),
                  reads=[f"qbf{buf}"], writes=[f"qbf{buf}"])
            p.add("dve", lambda e, buf=buf: e.memset(q_bf[buf][64:128, 0:D], 0.0),
                  reads=[f"qbf{buf}"], writes=[f"qbf{buf}"])
        for k in range(8):
            p.add("act", lambda e, mc=mc, k=k, buf=buf:
                  e.dma_start_transpose(stage[buf][:, k * 128:(k + 1) * 128], q_bf[buf][:, k * 128:(k + 1) * 128]),
                  reads=[f"qbf{buf}"], writes=[f"stage{buf}"], dma=True)
        p.add("sp", lambda e, mc=mc, buf=buf:
              e.dma_start(out=w_inT_s[mc, :, 0:8 * 128], in_=stage[buf][:, 0:8 * 128]),
              reads=[f"stage{buf}"], writes=[f"stage{buf}"], dma=True)

    qchunk_i = [0]
    def quant_rows(src_slice, K, buf, tag):
        ci = qchunk_i[0]; qchunk_i[0] += 1
        p.add("sp", lambda e, s=src_slice: e.dma_start(out=q_f32[buf][:, 0:K], in_=s),
              reads=[f"qf{buf}"], writes=[f"qf{buf}"], dma=True)
        p.add("dve", lambda e: e.reduce_sum(q_col[:, 0:1], q_f32[buf][:, 0:K], axis=AX.X, apply_absolute_value=True),
              reads=[f"qf{buf}"], writes=["qcol"])
        p.add("dve", lambda e, K=K: e.tensor_scalar(q_col[:, 1:2], q_col[:, 0:1], 0.5 / K, 0.5e-5, ALU.mult, ALU.max),
              reads=["qcol"], writes=["qcol2"])
        p.add("dve", lambda e, K=K: e.tensor_scalar(q_col[:, 2:3], q_col[:, 0:1], 1.0 / K, 1e-5, ALU.mult, ALU.max),
              reads=["qcol"], writes=["qcol2"])
        p.add("act", lambda e, K=K: e.activation(q_aw[:, 0:K], q_f32[buf][:, 0:K], AF.Abs),
              reads=[f"qf{buf}"], writes=["qaw"])
        p.add("act", lambda e, K=K: e.activation(q_sg[:, 0:K], q_f32[buf][:, 0:K], AF.Sign),
              reads=[f"qf{buf}"], writes=["qsg"])
        p.add("dve", lambda e, K=K: e.tensor_scalar(q_aw2[:, 0:K], q_aw[:, 0:K], q_col[:, 1:2], None, ALU.is_gt),
              reads=["qaw", "qcol2"], writes=["qaw2"])
        p.add("dve", lambda e, K=K: e.scalar_tensor_tensor(q_bf[buf][:, 0:K], q_aw2[:, 0:K], q_col[:, 2:3], q_sg[:, 0:K], ALU.mult, ALU.mult),
              reads=["qaw2", "qsg", "qcol2"], writes=[f"qbf{buf}"])
        if debug:
            p.add("pool", lambda e, ci=ci: e.dma_start(out=scl_dbg[ci, :].unsqueeze(0).rearrange("a p -> (a p)").unsqueeze(1), in_=q_col[:, 2:3]),
                  reads=["qcol2"], dma=True)
            p.add("pool", lambda e, ci=ci: e.dma_start(out=sum_dbg[ci, :].unsqueeze(0).rearrange("a p -> (a p)").unsqueeze(1), in_=q_col[:, 0:1]),
                  reads=["qcol"], dma=True)

    for e_i in range(nexp):
        for fc in range(16):
            buf = fc % 2
            src = (w1_d[e_i, fc * 128:(fc + 1) * 128, :] if e_i < NE else sw1_d[fc * 128:(fc + 1) * 128, :])
            quant_rows(src, D, buf, f"w1_{e_i}_{fc}")
            for k in range(8):
                p.add("act", lambda e, k=k, buf=buf:
                      e.dma_start_transpose(stage[buf][:, k * 128:(k + 1) * 128], q_bf[buf][:, k * 128:(k + 1) * 128]),
                      reads=[f"qbf{buf}"], writes=[f"stage{buf}"], dma=True)
            p.add("sp", lambda e, e_i=e_i, fc=fc, buf=buf:
                  e.dma_start(out=w1T_s[e_i, fc, :, :], in_=stage[buf][:, 0:8 * 128]),
                  reads=[f"stage{buf}"], writes=[f"stage{buf}"], dma=True)
        for md in range(8):
            buf = md % 2
            src = (w2_d[e_i, md * 128:(md + 1) * 128, :] if e_i < NE else sw2_d[md * 128:(md + 1) * 128, :])
            quant_rows(src, DIN, buf, f"w2_{e_i}_{md}")
            for fc in range(16):
                p.add("act", lambda e, fc=fc, buf=buf:
                      e.dma_start_transpose(stage[buf][:, fc * 128:(fc + 1) * 128], q_bf[buf][:, fc * 128:(fc + 1) * 128]),
                      reads=[f"qbf{buf}"], writes=[f"stage{buf}"], dma=True)
            p.add("sp", lambda e, e_i=e_i, md=md, buf=buf:
                  e.dma_start(out=w2T_s[e_i, md, :, :], in_=stage[buf][:, 0:16 * 128]),
                  reads=[f"stage{buf}"], writes=[f"stage{buf}"], dma=True)

    if debug == "phase0":
        dbg("scl", scl_dbg, (256, 128), F32, reads=["qcol2"])
        dbg("sum", sum_dbg, (256, 128), F32, reads=["qcol"])
        dbg("w_inT", w_inT_s[...], (MC, 128, 8 * 128), BF16, reads=["stage0", "stage1"])
        dbg("w1T", w1T_s[...], (nexp, 16, 128, 8 * 128), BF16, reads=["stage0", "stage1"])
        dbg("w2T", w2T_s[...], (nexp, 8, 128, 16 * 128), BF16, reads=["stage0", "stage1"])
        dbg("w_outT_sb", w_outT_s[...], (8, 128, 16 * 128), BF16, reads=["stage0", "stage1"])
        dbg("rwT_sb", rwT[...], (128, 8 * NE), BF16, reads=["rwT"])
        dbg("drep", drep[...], (128, DIN), F32, reads=["drep"])
        dbg("convw", convw_sb[...], (128, MC * 4), F32, reads=["convw"])


    p.barrier()
    ph0.close()
    pa = contextlib.ExitStack()

    def sba(name, shape, dt, **kw):
        h = pa.enter_context(nc.sbuf_tensor(name, list(shape), dt, **kw))
        return h.ap()

    x_tm = sba("x_tm", (128, D), F32)
    t_tm = sba("t_tm", (128, D), F32)
    xn_tm = sba("xn_tm", (128, D), F32)
    h_tm = sba("h_tm", (128, D), F32)
    stat = sba("stat", (128, 16), F32)
    xn_fm = sba("xn_fm", (128, 8 * TILE), BF16)
    h_fm = sba("h_fm", (128, 8 * TILE), BF16)
    sz = sba("sz", (128, 16 * TILE), BF16)
    xh = sba("xh", (128, NCH * DIN), BF16)
    bc = sba("bc", (128, TILE), F32)
    bcn = sba("bcn", (64, TILE), F32)
    dtr = sba("dtr", (H, TILE), F32)
    dtw = sba("dtw", (H, TILE * 7), F32)
    cvp = [sba(f"cv{i}", (128, TILE), F32) for i in range(2)]
    zx = [sba(f"zx{i}", (128, TILE + 3), F32) for i in range(2)]
    tails = sba("tails", (128, MC * 3), F32)
    srep = sba("srep", (128, 16 * Q), F32)
    difft = sba("difft", (128, 16 * Q), F32)
    mt = [sba(f"mt{i}", (128, 16 * Q), BF16) for i in range(2)]
    cbt = [sba(f"cbt{i}", (128, Q), F32) for i in range(2)]
    stkT = [sba(f"stkT{i}", (128, 160), F32) for i in range(2)]
    expsT = [sba(f"expsT{i}", (128, H), F32) for i in range(2)]
    expsl = [sba(f"expsl{i}", (64, H), F32) for i in range(2)]
    wxh0 = sba("wxh0", (128, DIN), BF16)
    wxh = [wxh0, wxh0]
    bwb = [sba(f"bwb{i}", (128, 64), BF16) for i in range(2)]
    yevt = sba("yevt", (128, 512), F32)
    ysb = sba("ysb", (128, DIN), F32)
    ygat = sba("ygat", (128, 16 * TILE), BF16)
    attnb = sba("attnb", (128, TILE), F32)
    statebuf = [sba(f"state{i}", (64, DIN), F32) for i in range(2)]
    wt = [sba(f"wt{i}", (128, 8 * 128), BF16) for i in range(2)]
    wo = [sba(f"wo{i}", (128, 16 * 128), BF16) for i in range(2)]
    logit = sba("logit", (128, NE), F32)
    gtile = sba("gtile", (8, Q), F32)
    gE = sba("gE", (128, NE), F32)
    ones8 = sba("ones8", (128, NE), F32)
    gF = sba("gF", (128, NE), F32)
    ps = [pa.enter_context(nc.psum_tensor(f"ps{i}", [128, 512], F32)).ap()
          for i in range(8)]

    p.add("dve", lambda e: e.memset(tails[...], 0.0), writes=["tails"])
    p.add("dve", lambda e: e.memset(ones8[...], 1.0), writes=["ones8"])
    p.add("dve", lambda e: e.memset(statebuf[0][...], 0.0), writes=["st0"])

    xn_fm3 = xn_fm.rearrange("p (k t) -> p k t", t=TILE)
    h_fm3 = h_fm.rearrange("p (k t) -> p k t", t=TILE)
    sz3 = sz.rearrange("p (k t) -> p k t", t=TILE)
    xh3 = xh.rearrange("p (s f) -> p s f", f=DIN)
    ygat3 = ygat.rearrange("p (k t) -> p k t", t=TILE)
    dtw3 = dtw.rearrange("p (i t) -> p i t", t=TILE)
    D_DT, D_LN, D_S, D_SML, D_W, D_S1, D_S2 = range(7)
    D_A = D_S1

    def cw_col(mc, k):
        return convw_sb[:, mc * 4 + k:mc * 4 + k + 1]

    def dsl(i, cols=None):
        a = dtw3[:, i, :] if cols is None else dtw3[:, i, cols]
        return a

    def ln_and_transpose(it, s):
        t0 = it * TILE
        p.add("sp", lambda e, t0=t0, s=s: e.dma_start(out=x_tm[...], in_=x_d[t0 + s * 128:t0 + (s + 1) * 128, :]),
              reads=["x_tm"], writes=["x_tm"], dma=True)
        p.add("dve", lambda e: e.reduce_sum(stat[:, 0:1], x_tm[...], axis=AX.X), reads=["x_tm"], writes=["stat"])
        p.add("dve", lambda e: e.tensor_scalar_mul(stat[:, 1:2], stat[:, 0:1], 1.0 / D), reads=["stat"], writes=["stat"])
        p.add("dve", lambda e: e.tensor_scalar(t_tm[...], x_tm[...], stat[:, 1:2], None, ALU.subtract),
              reads=["x_tm", "stat"], writes=["t_tm"])
        p.add("act", lambda e: e.activation(xn_tm[...], t_tm[...], AF.Square, accum_out=stat[:, 2:3]),
              reads=["t_tm"], writes=["xn_tm", "stat"])
        p.add("act", lambda e: e.activation(stat[:, 3:4], stat[:, 2:3], AF.Ln, bias=cbias[:, 0:1], scale=1.0 / D),
              reads=["stat", "cbias"], writes=["stat"])
        p.add("act", lambda e: e.activation(stat[:, 4:5], stat[:, 3:4], AF.Exp, scale=-0.5),
              reads=["stat"], writes=["stat"])
        p.add("dve", lambda e: e.tensor_scalar_mul(xn_tm[...], t_tm[...], stat[:, 4:5]),
              reads=["t_tm", "stat"], writes=["xn_tm"])
        p.add("dve", lambda e: e.reduce_sum(stat[:, 5:6], xn_tm[...], axis=AX.X), reads=["xn_tm"], writes=["stat"])
        p.add("dve", lambda e: e.tensor_scalar_mul(stat[:, 6:7], stat[:, 5:6], 1.0 / D), reads=["stat"], writes=["stat"])
        p.add("dve", lambda e: e.tensor_scalar(t_tm[...], xn_tm[...], stat[:, 6:7], None, ALU.subtract),
              reads=["xn_tm", "stat"], writes=["t_tm"])
        p.add("act", lambda e: e.activation(h_tm[...], t_tm[...], AF.Square, accum_out=stat[:, 7:8]),
              reads=["t_tm"], writes=["h_tm", "stat"])
        p.add("act", lambda e: e.activation(stat[:, 8:9], stat[:, 7:8], AF.Ln, bias=cbias[:, 0:1], scale=1.0 / D),
              reads=["stat", "cbias"], writes=["stat"])
        p.add("act", lambda e: e.activation(stat[:, 9:10], stat[:, 8:9], AF.Exp, scale=-0.5),
              reads=["stat"], writes=["stat"])
        p.add("dve", lambda e: e.scalar_tensor_tensor(h_tm[...], t_tm[...], stat[:, 9:10], lngrep[...], ALU.mult, ALU.mult),
              reads=["t_tm", "stat", "lngrep"], writes=["h_tm"])
        p.add("dve", lambda e: e.tensor_add(t_tm[...], h_tm[...], lnbrep[...]),
              reads=["h_tm", "lnbrep"], writes=["t_tm"])
        for src_ap, src_b, dst3, dst_b in ((xn_tm, "xn_tm", xn_fm3, "xn_fm"),
                                           (t_tm, "t_tm", h_fm3, "h_fm")):
            for half in range(2):
                for k4 in range(4):
                    k = half * 4 + k4
                    p.add("pe", lambda e, src_ap=src_ap, k=k, k4=k4:
                          e.transpose(ps[2][:, k4 * 128:(k4 + 1) * 128], src_ap[:, k * 128:(k + 1) * 128], ident[...]),
                          reads=[src_b, "ident"], writes=["ps2"])
                p.add("dve", lambda e, dst3=dst3, half=half, s=s:
                      e.tensor_copy(dst3[:, half * 4:(half + 1) * 4, s * 128:(s + 1) * 128],
                                    ps[2][:, 0:512].rearrange("p (a b) -> p a b", b=128)),
                      reads=["ps2"], writes=[dst_b])
        for k in range(8):
            p.add("pe", lambda e, k=k, s=s: e.matmul(ps[2][:, 0:8], h_fm3[:, k, s * 128:(s + 1) * 128],
                                                     rwT[:, k * NE:(k + 1) * NE], start=(k == 0), stop=(k == 7)),
                  reads=["h_fm", "rwT"], writes=["ps2"])
        p.add("dve", lambda e: e.tensor_copy(logit[...], ps[2][:, 0:8]), reads=["ps2"], writes=["logit"])
        p.add("dve", lambda e: e.reduce_max(stat[:, 10:11], logit[...], axis=AX.X), reads=["logit"], writes=["stat"])
        p.add("dve", lambda e: e.tensor_scalar_mul(stat[:, 11:12], stat[:, 10:11], -1.0), reads=["stat"], writes=["stat"])
        p.add("act", lambda e: e.activation(gE[...], logit[...], AF.Exp, bias=stat[:, 11:12], accum_out=stat[:, 12:13]),
              reads=["logit", "stat"], writes=["gE", "stat"])
        p.add("dve", lambda e: e.reciprocal(stat[:, 13:14], stat[:, 12:13]), reads=["stat"], writes=["stat"])
        p.add("dve", lambda e: e.tensor_scalar(gE[...], logit[...], stat[:, 10:11], None, ALU.is_equal),
              reads=["logit", "stat"], writes=["gE"])
        p.add("dve", lambda e: e.tensor_tensor_scan(gF[...], ones8[...], gE[...], 0.0, ALU.mult, ALU.add),
              reads=["gE", "ones8"], writes=["gF"])
        p.add("dve", lambda e: e.tensor_scalar(gF[...], gF[...], 1.0, None, ALU.is_equal),
              reads=["gF"], writes=["gF"])
        p.add("dve", lambda e: e.tensor_mul(gE[...], gE[...], gF[...]), reads=["gE", "gF"], writes=["gE"])
        p.add("dve", lambda e: e.tensor_scalar_mul(gF[...], gE[...], stat[:, 13:14]), reads=["gE", "stat"], writes=["gF"])
        p.add("pe", lambda e: e.transpose(ps[2][0:8, 0:128], gF[...], ident[...]), reads=["gF", "ident"], writes=["ps2"])
        p.add("dve", lambda e: e.tensor_copy(gtile[...], ps[2][0:8, 0:128]), reads=["ps2"], writes=["gtile"])
        p.add("sp", lambda e, t0=t0, s=s: e.dma_start(out=g_s[:, t0 + s * 128:t0 + (s + 1) * 128], in_=gtile[...]),
              reads=["gtile"], dma=True)
        if debug == "phaseA2" and it == 0 and s == 0:
            dbg("logit", logit[...], (128, NE), F32, reads=["logit"])
            dbg("gE", gE[...], (128, NE), F32, reads=["gE"])

            dbg("gF", gF[...], (128, NE), F32, reads=["gF"])
            dbg("gtile", gtile[...], (8, Q), F32, reads=["gtile"])
            dbg("statv", stat[...], (128, 16), F32, reads=["stat"])

    def in_proj_conv(it):
        if debug == "phaseA1" and it == 0:
            dbg("xnfm", xn_fm[...], (128, 8 * TILE), BF16, reads=["xn_fm"])
        for mc in range(MC):
            b = mc % 2
            p.add("sp", lambda e, mc=mc, b=b: e.dma_start(out=wt[b][...], in_=w_inT_s[mc, :, :]),
                  reads=[f"wt{b}"], writes=[f"wt{b}"], dma=True)
            if debug == "phaseA1" and it == 0 and mc == 0:
                dbg("wt0", wt[b][...], (128, 8 * 128), BF16, reads=[f"wt{b}"])
            for k in range(8):
                p.add("pe", lambda e, k=k, b=b: e.matmul(ps[b][:, 0:512], wt[b][:, k * 128:(k + 1) * 128],
                                                         xn_fm3[:, k, :], start=(k == 0), stop=(k == 7)),
                      reads=[f"wt{b}", "xn_fm"], writes=[f"ps{b}"])
            p.add("dve", lambda e, mc=mc, b=b: e.tensor_copy(zx[b][:, 0:3], tails[:, mc * 3:(mc + 1) * 3]),
                  reads=["tails"], writes=[f"zx{b}"])
            p.add("act", lambda e, b=b: e.copy(zx[b][:, 3:3 + TILE], ps[b][:, 0:512]),
                  reads=[f"ps{b}"], writes=[f"zx{b}"])
            p.add("dve", lambda e, mc=mc, b=b: e.tensor_copy(tails[:, mc * 3:(mc + 1) * 3], zx[b][:, TILE:TILE + 3]),
                  reads=[f"zx{b}"], writes=["tails"])
            p.add("dve", lambda e, mc=mc, b=b:
                  e.scalar_tensor_tensor(cvp[0][...], zx[b][:, 0:TILE], cw_col(mc, 0),
                                         convb_sb[:, mc:mc + 1].to_broadcast([128, TILE]), ALU.mult, ALU.add),
                  reads=[f"zx{b}", "convw", "convb"], writes=["cv0"])
            for kk in range(1, 4):
                src_cv, dst_cv = (kk + 1) % 2, kk % 2
                p.add("dve", lambda e, mc=mc, b=b, kk=kk, src_cv=src_cv, dst_cv=dst_cv:
                      e.scalar_tensor_tensor(cvp[dst_cv][...], zx[b][:, kk:kk + TILE], cw_col(mc, kk),
                                             cvp[src_cv][...], ALU.mult, ALU.add),
                      reads=[f"zx{b}", "convw", f"cv{src_cv}"], writes=[f"cv{dst_cv}"])
            if debug == "phaseA1" and it == 0 and mc in (0, 16, 32, 33):
                dbg(f"zx{mc}", zx[b][...], (128, TILE + 3), F32, reads=[f"zx{b}"])
                dbg(f"cv{mc}", cvp[1][...], (128, TILE), F32, reads=["cv1"])
            if mc < 16:
                p.add("act", lambda e, mc=mc: e.activation(sz3[:, mc, :], cvp[1][...], AF.Silu),
                      reads=["cv1"], writes=["sz"])
            elif mc < 32:
                for s4 in range(4):
                    p.add("pe", lambda e, s4=s4: e.transpose(ps[2][:, s4 * 128:(s4 + 1) * 128],
                                                             cvp[1][:, s4 * 128:(s4 + 1) * 128], ident[...]),
                          reads=["cv1", "ident"], writes=["ps2"])
                p.add("dve", lambda e, mc=mc: e.tensor_copy(xh3[:, :, (mc - 16) * 128:(mc - 15) * 128],
                                                            ps[2][:, 0:512].rearrange("p (a b) -> p a b", b=128)),
                      reads=["ps2"], writes=["xh"])
            elif mc == 32:
                p.add("dve", lambda e: e.tensor_copy(bc[...], cvp[1][...]), reads=["cv1"], writes=["bc"])
                p.add("sp", lambda e: e.dma_start(out=bcC_s[...], in_=bc[64:128, :]),
                      reads=["bc"], writes=["bcCs"], dma=True)
                p.add("sp", lambda e: e.dma_start(out=bcn[...], in_=bcC_s[...]),
                      reads=["bcCs", "bcn"], writes=["bcn"], dma=True)
            else:
                p.add("dve", lambda e: e.tensor_copy(dtr[...], cvp[1][0:H, :]), reads=["cv1"], writes=["dtr"])

    def dt_pipeline(it):
        p.add("act", lambda e: e.activation(dsl(D_S1), dtr[...], AF.Abs, bias=dtb_sb[:, 0:1]),
              reads=["dtr", "dtb"], writes=["dtw"])
        p.add("act", lambda e: e.activation(dsl(D_S2), dsl(D_S1), AF.Exp, scale=-1.0), reads=["dtw"], writes=["dtw"])
        p.add("act", lambda e: e.activation(dsl(D_S1), dsl(D_S2), AF.Ln, bias=cbias[0:H, 1:2]), reads=["dtw", "cbias"], writes=["dtw"])
        p.add("act", lambda e: e.activation(dsl(D_S2), dtr[...], AF.Relu, bias=dtb_sb[:, 0:1]),
              reads=["dtr", "dtb"], writes=["dtw"])
        p.add("dve", lambda e: e.tensor_add(dsl(D_DT), dsl(D_S1), dsl(D_S2)), reads=["dtw"], writes=["dtw"])
        p.add("dve", lambda e: e.tensor_scalar_mul(dsl(D_A), dsl(D_DT), alog_sb[:, 0:1]),
              reads=["dtw", "alog"], writes=["dtw"])
        p.add("act", lambda e: e.activation(dsl(D_LN), dsl(D_DT), AF.Ln), reads=["dtw"], writes=["dtw"])
        for c in range(NCH):
            cc = slice(c * Q, (c + 1) * Q)
            p.add("dve", lambda e, cc=cc: e.tensor_tensor_scan(dsl(D_S, cc), ones32[0:H, 0:Q], dsl(D_A, cc),
                                                               0.0, ALU.mult, ALU.add),
                  reads=["dtw", "ones32"], writes=["dtw"])
        p.add("dve", lambda e: e.tensor_sub(dsl(D_SML), dsl(D_S), dsl(D_LN)), reads=["dtw"], writes=["dtw"])
        for c in range(NCH):
            cc = slice(c * Q, (c + 1) * Q)
            p.add("dve", lambda e, c=c, cc=cc: e.tensor_scalar(dsl(D_S1, cc), dsl(D_S, cc),
                                                               dtw3[:, D_S, c * Q + Q - 1:c * Q + Q], None, ALU.subtract),
                  reads=["dtw"], writes=["dtw"])
            p.add("act", lambda e, cc=cc: e.activation(dsl(D_S2, cc), dsl(D_S1, cc), AF.Exp, scale=-1.0),
                  reads=["dtw"], writes=["dtw"])
            p.add("dve", lambda e, cc=cc: e.tensor_mul(dsl(D_W, cc), dsl(D_DT, cc), dsl(D_S2, cc)),
                  reads=["dtw"], writes=["dtw"])

    def chunk_scan(it, c):
        gc = it * NCH + c
        slot = gc % 2
        pr = gc % 2
        sr, sw = f"st{gc % 2}", f"st{(gc + 1) % 2}"
        cc = slice(c * Q, (c + 1) * Q)
        wr_s = p.add("sp", lambda e, slot=slot, cc=cc:
                     e.dma_start(out=srep_s[slot, :].unsqueeze(0).rearrange("a (h q) -> (a h) q", q=Q), in_=dsl(D_S, cc)),
                     reads=["dtw", f"sreps{slot}"], writes=[f"sreps{slot}"], dma=True)
        p.add("pe", lambda e, cc=cc: e.transpose(ps[2][:, 0:64], bc[0:64, cc], ident[0:64, 0:64]),
              reads=["bc", "ident"], writes=["ps2"])
        p.add("pe", lambda e, cc=cc: e.transpose(ps[2][:, 64:96], dsl(D_W, cc), ident[0:H, 0:H]),
              reads=["dtw", "ident"], writes=["ps2"])
        p.add("pe", lambda e, cc=cc: e.transpose(ps[2][:, 96:128], dsl(D_SML, cc), ident[0:H, 0:H]),
              reads=["dtw", "ident"], writes=["ps2"])
        p.add("pe", lambda e, cc=cc: e.transpose(ps[2][:, 128:160], dsl(D_S, cc), ident[0:H, 0:H]),
              reads=["dtw", "ident"], writes=["ps2"])
        p.add("dve", lambda e, pr=pr: e.tensor_copy(stkT[pr][...], ps[2][:, 0:160]),
              reads=["ps2"], writes=[f"stkT{pr}"])
        p.add("act", lambda e, pr=pr: e.activation(expsT[pr][...], stkT[pr][:, 128:160], AF.Exp),
              reads=[f"stkT{pr}"], writes=[f"expsT{pr}"])
        p.add("pe", lambda e, cc=cc: e.matmul(ps[2][:, 0:128], bc[0:64, cc], bcn[0:64, cc], start=True, stop=True),
              reads=["bc", "bcn"], writes=["ps2"])
        p.add("act", lambda e, pr=pr: e.copy(cbt[pr][...], ps[2][:, 0:128]), reads=["ps2"], writes=[f"cbt{pr}"])
        p.add("dve", lambda e, c=c, pr=pr:
              e.tensor_tensor(wxh[pr][...].rearrange("p (h e) -> p h e", e=PDIM),
                              xh3[:, c, :].rearrange("p (h e) -> p h e", e=PDIM),
                              stkT[pr][:, 64:96].unsqueeze(2).broadcast_to([128, H, PDIM]), ALU.mult),
              reads=["xh", f"stkT{pr}"], writes=["wxh0"])
        p.add("dve", lambda e, pr=pr: e.tensor_copy(bwb[pr][...], stkT[pr][:, 0:64]),
              reads=[f"stkT{pr}"], writes=[f"bwb{pr}"])
        for half in range(2):
            rd = p.add("act", lambda e, slot=slot, half=half:
                       e.dma_start(out=srep[...], in_=srep_s[slot, half * 16 * Q:(half + 1) * 16 * Q]
                                   .unsqueeze(0).to_broadcast([128, 16 * Q])),
                       reads=[f"sreps{slot}", "srep"], writes=["srep"], dma=True, extra_deps=[wr_s])
            srep3 = srep.rearrange("p (h q) -> p h q", q=Q)
            p.add("act", lambda e, half=half, srep3=srep3, pr=pr:
                  e.activation(expsl[pr][:, half * 16:(half + 1) * 16], srep3[0:64, :, Q - 1], AF.Exp),
                  reads=["srep"], writes=[f"expsl{pr}"])
            p.add("dve", lambda e, half=half, pr=pr:
                  e.tensor_tensor(difft[...].rearrange("p (h q) -> p h q", q=Q),
                                  srep[...].rearrange("p (h q) -> p h q", q=Q),
                                  stkT[pr][:, 96 + half * 16:96 + (half + 1) * 16].unsqueeze(2).broadcast_to([128, 16, Q]),
                                  ALU.subtract),
                  reads=["srep", f"stkT{pr}"], writes=["difft"])
            p.add("dve", lambda e:
                  e.tensor_tensor(difft[...].rearrange("p (h q) -> p h q", q=Q),
                                  difft[...].rearrange("p (h q) -> p h q", q=Q),
                                  masklog[...].unsqueeze(1).broadcast_to([128, 16, Q]), ALU.add),
                  reads=["difft"], writes=["difft"])
            p.add("act", lambda e: e.activation(srep[...], difft[...], AF.Exp), reads=["difft"], writes=["srep"])
            p.add("dve", lambda e, half=half, pr=pr:
                  e.tensor_tensor(mt[pr][...].rearrange("p (h q) -> p h q", q=Q),
                                  srep[...].rearrange("p (h q) -> p h q", q=Q),
                                  cbt[pr][...].unsqueeze(1).broadcast_to([128, 16, Q]), ALU.mult),
                  reads=["srep", f"cbt{pr}"], writes=[f"mt{pr}"])
            chunk_headgroups(c, gc, cc, sr, sw, half, pr)

        p.add("pool", lambda e, c=c: e.tensor_mul(difft[:, 0:DIN], xh3[:, c, :], drep[...]),
              reads=["xh", "drep"], writes=["difft"])
        p.add("pool", lambda e: e.tensor_add(ysb[...], ysb[...], difft[:, 0:DIN]), reads=["ysb", "difft"], writes=["ysb"])
        p.add("act", lambda e: e.activation(difft[:, 0:DIN], ysb[...], AF.Square, accum_out=stat[:, 14:15]),
              reads=["ysb"], writes=["difft", "stat"])
        p.add("act", lambda e: e.activation(stat[:, 15:16], stat[:, 14:15], AF.Ln, bias=cbias[:, 2:3], scale=1.0 / DIN),
              reads=["stat", "cbias"], writes=["stat"])
        p.add("act", lambda e: e.activation(stat[:, 14:15], stat[:, 15:16], AF.Exp, scale=-0.5),
              reads=["stat"], writes=["stat"])
        p.add("dve", lambda e: e.scalar_tensor_tensor(ysb[...], ysb[...], stat[:, 14:15], nwrep[...],
                                                      ALU.mult, ALU.mult),
              reads=["ysb", "stat", "nwrep"], writes=["ysb"])
        for q4 in range(4):
            for j in range(4):
                fc = q4 * 4 + j
                p.add("pe", lambda e, fc=fc, j=j: e.transpose(ps[6][:, j * 128:(j + 1) * 128],
                                                              ysb[:, fc * 128:(fc + 1) * 128], ident[...]),
                      reads=["ysb", "ident"], writes=["ps6"])
            p.add("dve", lambda e, q4=q4, c=c:
                  e.tensor_mul(ygat3[:, q4 * 4:(q4 + 1) * 4, c * 128:(c + 1) * 128],
                               ps[6][:, 0:512].rearrange("p (a b) -> p a b", b=128),
                               sz3[:, q4 * 4:(q4 + 1) * 4, c * 128:(c + 1) * 128]),
                  reads=["ps6", "sz"], writes=["ygat"])
        if debug == "phaseA2" and it == 0 and c == 0:
            dbg("stkT", stkT[0][...], (128, 160), F32, reads=["stkT0"])
            dbg("cbt", cbt[0][...], (128, Q), F32, reads=["cbt0"])
            dbg("mtd", mt[0][...], (128, 16 * Q), BF16, reads=["mt0"])
            dbg("ysbd", ysb[...], (128, DIN), F32, reads=["ysb"])
            dbg("st1d", statebuf[1][...], (64, DIN), F32, reads=["st1"])
            dbg("expsTd", expsT[0][...], (128, H), F32, reads=["expsT0"])
            dbg("expsld", expsl[0][...], (64, H), F32, reads=["expsl0"])
            dbg("bcd", bc[...], (128, TILE), F32, reads=["bc"])
            dbg("bcnd", bcn[...], (64, TILE), F32, reads=["bcn"])

    def chunk_headgroups(c, gc, cc, sr, sw, half, pr):
        for hg in (half * 2, half * 2 + 1):
            j0 = (hg % 2) * 8
            for j in range(8):
                h_ = hg * 8 + j
                p.add("pe", lambda e, h_=h_, j=j, c=c, pr=pr: e.matmul(ps[3][:, j * 64:(j + 1) * 64],
                                                                mt[pr][:, (h_ % 16) * Q:((h_ % 16) + 1) * Q],
                                                                xh3[:, c, h_ * 64:(h_ + 1) * 64], start=True, stop=True),
                      reads=[f"mt{pr}", "xh"], writes=["ps3"])
            p.add("pe", lambda e, hg=hg, cc=cc, gc=gc: e.matmul(ps[4][:, 0:512],
                                                                bcn[0:64, cc],
                                                                statebuf[gc % 2][:, hg * 512:(hg + 1) * 512],
                                                                start=True, stop=True),
                  reads=["bcn", sr], writes=["ps4"])
            p.add("pe", lambda e, hg=hg, c=c, pr=pr: e.matmul(ps[5][0:64, 0:512],
                                                              bwb[pr][...],
                                                              wxh[pr][:, hg * 512:(hg + 1) * 512], start=True, stop=True),
                  reads=[f"bwb{pr}", "wxh0"], writes=["ps5"])
            p.add("act", lambda e, hg=hg: e.copy(ysb[:, hg * 512:(hg + 1) * 512], ps[3][:, 0:512]),
                  reads=["ps3"], writes=["ysb"])
            p.add("dve", lambda e, hg=hg, pr=pr:
                  e.tensor_tensor(yevt[...].rearrange("p (h e) -> p h e", e=PDIM),
                                  ps[4][:, 0:512].rearrange("p (h e) -> p h e", e=PDIM),
                                  expsT[pr][:, hg * 8:(hg + 1) * 8].unsqueeze(2).broadcast_to([128, 8, PDIM]),
                                  ALU.mult),
                  reads=["ps4", f"expsT{pr}"], writes=["yevt"])
            p.add("dve", lambda e, hg=hg: e.tensor_add(ysb[:, hg * 512:(hg + 1) * 512],
                                                       ysb[:, hg * 512:(hg + 1) * 512], yevt[...]),
                  reads=["yevt", "ysb"], writes=["ysb"])
            p.add("dve", lambda e, hg=hg, gc=gc, pr=pr:
                  e.tensor_tensor(yevt[0:64, :].rearrange("p (h e) -> p h e", e=PDIM),
                                  statebuf[gc % 2][:, hg * 512:(hg + 1) * 512].rearrange("p (h e) -> p h e", e=PDIM),
                                  expsl[pr][:, hg * 8:(hg + 1) * 8].unsqueeze(2).broadcast_to([64, 8, PDIM]),
                                  ALU.mult),
                  reads=[sr, f"expsl{pr}"], writes=["yevt"])
            p.add("dve", lambda e, hg=hg, gc=gc:
                  e.tensor_add(statebuf[(gc + 1) % 2][:, hg * 512:(hg + 1) * 512], yevt[0:64, :],
                               ps[5][0:64, 0:512]),
                  reads=["yevt", "ps5"], writes=[sw])


    def out_proj(it):
        t0 = it * TILE
        for md in range(8):
            b = md % 2
            p.add("sp", lambda e, md=md, b=b: e.dma_start(out=wo[b][...], in_=w_outT_s[md, :, :]),
                  reads=[f"wo{b}"], writes=[f"wo{b}"], dma=True)
            for fc in range(16):
                p.add("pe", lambda e, fc=fc, b=b: e.matmul(ps[7][:, 0:512], wo[b][:, fc * 128:(fc + 1) * 128],
                                                           ygat3[:, fc, :], start=(fc == 0), stop=(fc == 15)),
                      reads=[f"wo{b}", "ygat"], writes=["ps7"])
            p.add("act", lambda e, md=md: e.copy(attnb[...], ps[7][:, 0:512]),
                  reads=["ps7"], writes=["attnb"])
            p.add("sp", lambda e, md=md, t0=t0: e.dma_start(out=attn_s[md, :, t0:t0 + TILE], in_=attnb[...]),
                  reads=["attnb"], dma=True)

    for it in range(ntiles):
        for s in range(NCH):
            ln_and_transpose(it, s)
        in_proj_conv(it)
        dt_pipeline(it)
        for c in range(NCH):
            chunk_scan(it, c)
        out_proj(it)
        t0 = it * TILE
        for k in range(8):
            p.add("act", lambda e, k=k, t0=t0: e.dma_start(out=h_s[k, :, t0:t0 + TILE], in_=h_fm3[:, k, :]),
                  reads=["h_fm"], dma=True)

    if debug in ("phaseA", "phaseA1", "phaseA2"):
        p.barrier()
        dbg("attn", attn_s[...], (8, 128, T), F32)
        dbg("h_s", h_s[...], (8, 128, T), BF16)
        dbg("g_s", g_s[...], (NE, T), F32)
        dbg("sz", sz[...], (128, 16 * TILE), BF16, reads=["sz"])
        dbg("xh", xh[...], (128, NCH * DIN), BF16, reads=["xh"])
        dbg("dtw", dtw[...], (H, TILE * 7), F32, reads=["dtw"])
        dbg("state", statebuf[(ntiles * NCH) % 2][...], (64, DIN), F32, reads=[f"st{(ntiles * NCH) % 2}"])


    p.barrier()
    pa.close()
    pb = contextlib.ExitStack()

    def sbb_(name, shape, dt, **kw):
        h = pb.enter_context(nc.sbuf_tensor(name, list(shape), dt, **kw))
        return h.ap()

    acc = sbb_("acc", (128, 8 * TB), F32)
    hb = sbb_("hb", (128, 8 * TB), BF16)
    h1g = sbb_("h1g", (128, 16 * TB), BF16)
    w1b = [sbb_(f"w1b{i}", (128, 8 * 128), BF16) for i in range(4)]
    w2b = [sbb_(f"w2b{i}", (128, 16 * 128), BF16) for i in range(2)]
    grep = [sbb_(f"grep{i}", (128, TB), F32) for i in range(2)]
    h1t = [sbb_(f"h1t{i}", (128, 512), BF16) for i in range(4)]
    xb = sbb_("xb", (128, D), F32)
    bident = sbb_("bident", (128, 128), F32)
    psb = [pb.enter_context(nc.psum_tensor(f"pb{i}", [128, 512], F32)).ap()
           for i in range(8)]
    acc3 = acc.rearrange("p (k t) -> p k t", t=TB)
    hb3 = hb.rearrange("p (k t) -> p k t", t=TB)
    h1g3 = h1g.rearrange("p (k t) -> p k t", t=TB)

    p.add("sp", lambda e: e.dma_start(out=bident[...], in_=ident_d[...]), writes=["bident"], dma=True)

    for bi in range(nblocks):
        t0 = bi * TB
        for md in range(8):
            p.add("sp", lambda e, md=md, t0=t0: e.dma_start(out=acc3[:, md, :], in_=attn_s[md, :, t0:t0 + TB]),
                  reads=["acc"], writes=["acc"], dma=True)
            p.add("act", lambda e, md=md, t0=t0: e.dma_start(out=hb3[:, md, :], in_=h_s[md, :, t0:t0 + TB]),
                  reads=["hb"], writes=["hb"], dma=True)
        for s in range(TB // 128):
            p.add("sp", lambda e, s=s, t0=t0: e.dma_start(out=xb[...], in_=x_d[t0 + s * 128:t0 + (s + 1) * 128, :]),
                  reads=["xb"], writes=["xb"], dma=True)
            for half in range(2):
                for k4 in range(4):
                    k = half * 4 + k4
                    p.add("pe", lambda e, k=k, k4=k4: e.transpose(psb[2][:, k4 * 128:(k4 + 1) * 128],
                                                                  xb[:, k * 128:(k + 1) * 128], bident[...]),
                          reads=["xb", "bident"], writes=["pb2"])
                p.add("dve", lambda e, half=half, s=s:
                      e.tensor_add(acc3[:, half * 4:(half + 1) * 4, s * 128:(s + 1) * 128],
                                   psb[2][:, 0:512].rearrange("p (a b) -> p a b", b=128),
                                   acc3[:, half * 4:(half + 1) * 4, s * 128:(s + 1) * 128]),
                      reads=["pb2", "acc"], writes=["acc"])
        for e_i in range(nexp):
            ge = e_i % 2
            if e_i < NE:
                p.add("sp", lambda e, e_i=e_i, ge=ge, t0=t0:
                      e.dma_start(out=grep[ge][...], in_=g_s[e_i, t0:t0 + TB].unsqueeze(0).to_broadcast([128, TB])),
                      reads=[f"grep{ge}"], writes=[f"grep{ge}"], dma=True)
            else:
                p.add("dve", lambda e, ge=ge: e.memset(grep[ge][...], 1.0),
                      reads=[f"grep{ge}"], writes=[f"grep{ge}"])
            for fc in range(16):
                wb = fc % 4
                p.add("sp", lambda e, e_i=e_i, fc=fc, wb=wb: e.dma_start(out=w1b[wb][...], in_=w1T_s[e_i, fc, :, :]),
                      reads=[f"w1b{wb}"], writes=[f"w1b{wb}"], dma=True)
                for ts2 in range(TB // 512):
                    pi = (fc % 2) * 2 + ts2
                    pbk = f"pb{pi}"
                    hi = (fc % 2) * 2 + ts2
                    for k in range(8):
                        p.add("pe", lambda e, fc=fc, wb=wb, ts2=ts2, k=k, pi=pi:
                              e.matmul(psb[pi][:, 0:512], w1b[wb][:, k * 128:(k + 1) * 128],
                                       hb3[:, k, ts2 * 512:(ts2 + 1) * 512], start=(k == 0), stop=(k == 7)),
                              reads=[f"w1b{wb}", "hb"], writes=[pbk])
                    p.add("act", lambda e, ts2=ts2, pi=pi, hi=hi: e.activation(h1t[hi][...], psb[pi][:, 0:512], AF.Silu),
                          reads=[pbk], writes=[f"h1t{hi}"])
                    p.add("dve", lambda e, fc=fc, ts2=ts2, ge=ge, hi=hi:
                          e.tensor_mul(h1g3[:, fc, ts2 * 512:(ts2 + 1) * 512], h1t[hi][...],
                                       grep[ge][:, ts2 * 512:(ts2 + 1) * 512]),
                          reads=[f"h1t{hi}", f"grep{ge}"], writes=["h1g"])
            for md in range(8):
                wb = md % 2
                p.add("act", lambda e, e_i=e_i, md=md, wb=wb: e.dma_start(out=w2b[wb][...], in_=w2T_s[e_i, md, :, :]),
                      reads=[f"w2b{wb}"], writes=[f"w2b{wb}"], dma=True)
                for ts2 in range(TB // 512):
                    pi = 4 + (md % 2) * 2 + ts2
                    pbk = f"pb{pi}"
                    for fc in range(16):
                        p.add("pe", lambda e, md=md, wb=wb, ts2=ts2, fc=fc, pi=pi:
                              e.matmul(psb[pi][:, 0:512], w2b[wb][:, fc * 128:(fc + 1) * 128],
                                       h1g3[:, fc, ts2 * 512:(ts2 + 1) * 512], start=(fc == 0), stop=(fc == 15)),
                              reads=[f"w2b{wb}", "h1g"], writes=[pbk])
                    p.add("dve", lambda e, md=md, ts2=ts2, pi=pi:
                          e.tensor_add(acc3[:, md, ts2 * 512:(ts2 + 1) * 512], psb[pi][:, 0:512],
                                       acc3[:, md, ts2 * 512:(ts2 + 1) * 512]),
                          reads=[pbk, "acc"], writes=["acc"])
        for md in range(8):
            p.add("sp", lambda e, md=md, t0=t0: e.dma_start(out=out_d[md, :, t0:t0 + TB], in_=acc3[:, md, :]),
                  reads=["acc"], dma=True)

    with nc.Block() as block:
        stack = p.emit(nc, block)
    stack.close()
    return nc, dbg_outs


def _mk_consts():
    ident = np.eye(128, dtype=np.float32)
    j = np.arange(128)[:, None]
    i = np.arange(128)[None, :]
    masklogT = np.where(i >= j, 0.0, NEG_BIG).astype(np.float32)
    iota8 = np.broadcast_to(np.arange(NE, dtype=np.float32)[None, :], (128, NE)).copy()
    return ident, masklogT, iota8


def _core_inputs(inputs, b):
    ident, masklogT, iota8 = _mk_consts()
    return {
        "x": np.ascontiguousarray(inputs["L0_attn_u"][b]),
        "w_in": inputs["L0_attn_in_proj_w"],
        "conv_w": inputs["L0_attn_conv_w"],
        "conv_b": inputs["L0_attn_conv_b"],
        "A_log": inputs["L0_attn_A_log"],
        "Dparam": inputs["L0_attn_Dparam"],
        "dt_bias": inputs["L0_attn_dt_bias"],
        "mnorm_w": inputs["L0_attn_norm_w"],
        "w_out": inputs["L0_attn_out_proj_w"],
        "w1": inputs["L0_ffn_expert_w1"],
        "w2": inputs["L0_ffn_expert_w2"],
        "sw1": inputs["L0_ffn_shared_w1"],
        "sw2": inputs["L0_ffn_shared_w2"],
        "rw": inputs["L0_ffn_router_w"],
        "ln_g": inputs["L0_ffn_ln_gamma"],
        "ln_b": inputs["L0_ffn_ln_beta"],
        "ident": ident,
        "masklogT": masklogT,
        "iota8": iota8,
    }


def kernel(**inputs):
    nc, _ = build_program()
    in_maps = [_core_inputs(inputs, b) for b in range(8)]
    res = run_bass_kernel_spmd(nc, in_maps, core_ids=list(range(8)))
    outs = []
    for r in res.results:
        o = np.asarray(r["out_fm"])
        outs.append(o.reshape(D, T).T)
    return np.stack(outs).astype(np.float32)


# version 10
# speedup vs baseline: 2.2741x; 1.2756x over previous
import os
import numpy as np
import ml_dtypes
import concourse.bass as bass
import concourse.mybir as mybir
from concourse.bass_utils import run_bass_kernel_spmd

F32 = mybir.dt.float32
BF16 = mybir.dt.bfloat16
AF = mybir.ActivationFunctionType
ALU = mybir.AluOpType
AX = mybir.AxisListType

T = 8192
D = 1024
CIN = 4256
DIN = 2048
H = 32
PDIM = 64
NST = 64
Q = 128
NE = 8
TILE = 512
NTILES = T // TILE
NCH = TILE // Q
MC = CIN // 128 + (1 if CIN % 128 else 0)
TB = 1024
NB = T // TB
LN_EPS = 1e-5
F32EPS = float(np.finfo(np.float32).eps)
NEG_BIG = -1e30

DEBUG = os.environ.get("KDEBUG", "")


ENGS = ("pe", "act", "dve", "pool", "sp")
DMAQ = ("sp", "act", "pool")


class Prog:
    def __init__(self):
        self.nodes = []
        self.writers = {}
        self.readers = {}
        self.bar = None

    def barrier(self):
        last = {}
        for i, n in enumerate(self.nodes):
            last[(n["eng"], n["dma"])] = i
        self.bar = None
        nid = self.add("sp", lambda e: e.nop(), extra_deps=list(last.values()))
        self.bar = nid
        return nid

    def add(self, eng, fn, reads=(), writes=(), dma=False, extra_deps=()):
        raw = set(int(d) for d in extra_deps if d is not None)
        war = set()
        if self.bar is not None:
            raw.add(self.bar)
        for b in reads:
            raw.update(self.writers.get(b, ()))
        for b in writes:
            war.update(self.writers.get(b, ()))
            war.update(self.readers.get(b, ()))
        nid = len(self.nodes)
        deps = set()
        for d in raw:
            if d < nid:
                deps.add(d)
        for d in war:
            if d < nid and not (eng == "pe" and self.nodes[d]["eng"] == "pe"
                                and not self.nodes[d]["dma"] and not dma):
                deps.add(d)
        self.nodes.append(dict(eng=eng, fn=fn, dma=bool(dma), deps=sorted(deps)))
        for b in reads:
            self.readers.setdefault(b, []).append(nid)
        for b in writes:
            if self.readers.get(b):
                self.writers[b] = [nid]
                self.readers[b] = []
            else:
                self.writers.setdefault(b, []).append(nid)
        return nid

    def emit(self, nc, block):
        nodes = self.nodes
        need_inc = [False] * len(nodes)
        for n in nodes:
            for d in n["deps"]:
                if not nodes[d]["dma"]:
                    need_inc[d] = True
        eng_inc_prefix = {}
        dma_prefix = {}
        cnt = {e: 0 for e in ENGS}
        dcnt = {e: 0 for e in ENGS}
        for i, n in enumerate(nodes):
            if n["dma"]:
                dcnt[n["eng"]] += 1
                dma_prefix[i] = dcnt[n["eng"]]
            else:
                if need_inc[i]:
                    cnt[n["eng"]] += 1
                eng_inc_prefix[i] = cnt[n["eng"]]
        NDS = 12
        sems = {}
        dsems = {}
        import contextlib
        stack = contextlib.ExitStack()
        for e in ENGS:
            sems[e] = stack.enter_context(nc.semaphore(f"s_{e}"))
        for e in DMAQ:
            dsems[e] = [stack.enter_context(nc.semaphore(f"d_{e}{i}")) for i in range(NDS)]
        dma_sub = {}
        subcnt = {e: [0] * NDS for e in DMAQ}
        for i, n in enumerate(nodes):
            if n["dma"]:
                si = (dma_prefix[i] - 1) % NDS
                subcnt[n["eng"]][si] += 1
                dma_sub[i] = (si, subcnt[n["eng"]][si])

        def emit_for(eng_name):
            def _(eng):
                waited = {}
                for i, n in enumerate(nodes):
                    if n["eng"] != eng_name:
                        continue
                    for d in n["deps"]:
                        dn = nodes[d]
                        if dn["dma"]:
                            si, cnt = dma_sub[d]
                            sem, val = dsems[dn["eng"]][si], 16 * cnt
                        else:
                            sem, val = sems[dn["eng"]], eng_inc_prefix[d]
                        key = id(sem)
                        if waited.get(key, -1) >= val:
                            continue
                        eng.wait_ge(sem, val)
                        waited[key] = val
                    ins = n["fn"](eng)
                    if n["dma"]:
                        ins.then_inc(dsems[eng_name][dma_sub[i][0]], 16)
                    elif need_inc[i]:
                        if ins is None:
                            eng.nop().then_inc(sems[eng_name], 1)
                        else:
                            ins.then_inc(sems[eng_name], 1)
                for e in DMAQ:
                    for i2 in range(NDS):
                        if subcnt[e][i2]:
                            eng.wait_ge(dsems[e][i2], 16 * subcnt[e][i2])
            return _

        block.sync(emit_for("sp"))
        block.scalar(emit_for("act"))
        block.vector(emit_for("dve"))
        block.tensor(emit_for("pe"))
        block.gpsimd(emit_for("pool"))
        return stack


def eng_obj(nc, name):
    return {"pe": nc.tensor, "act": nc.scalar, "dve": nc.vector,
            "pool": nc.gpsimd, "sp": nc.sync}[name]


def build_program(ntiles=NTILES, nblocks=NB, nexp=NE + 1, debug=DEBUG):
    nc = bass.Bass("TRN2", target_bir_lowering=False)
    p = Prog()

    def din(name, shape, dt=F32):
        return nc.dram_tensor(name, list(shape), dt, kind="ExternalInput").ap()

    x_d = din("x", (T, D))
    w_in_d = din("w_in", (CIN, D))
    convw_d = din("conv_w", (CIN, 4))
    convb_d = din("conv_b", (CIN,))
    alog_d = din("A_log", (H,))
    dparam_d = din("Dparam", (DIN,))
    dtb_d = din("dt_bias", (H,))
    mnw_d = din("mnorm_w", (DIN,))
    w_out_d = din("w_out", (D, DIN))
    w1_d = din("w1", (NE, DIN, D))
    w2_d = din("w2", (NE, D, DIN))
    sw1_d = din("sw1", (DIN, D))
    sw2_d = din("sw2", (D, DIN))
    rw_d = din("rw", (NE, D))
    lng_d = din("ln_g", (D,))
    lnb_d = din("ln_b", (D,))
    ident_d = din("ident", (128, 128))
    identb_d = din("identb", (128, 128), BF16)
    masklog_d = din("masklogT", (128, 128))
    iota8_d = din("iota8", (128, NE))

    out_d = nc.dram_tensor("out_fm", [8, 128, T], F32, kind="ExternalOutput").ap()

    def dscratch(name, shape, dt):
        return nc.dram_tensor(name, list(shape), dt, kind="Internal").ap()

    w_inT_s = dscratch("w_inT_s", (MC, 128, 8 * 128), BF16)
    w_outT_s = dscratch("w_outT_s", (8, 128, 16 * 128), BF16)
    w1T_s = dscratch("w1T_s", (nexp, 16, 128, 8 * 128), BF16)
    w2T_s = dscratch("w2T_s", (nexp, 8, 128, 16 * 128), BF16)
    h_s = dscratch("h_s", (8, 128, T), BF16)
    g_s = dscratch("g_s", (NE, T), F32)
    attn_s = dscratch("attn_s", (8, 128, T), F32)
    srep_s = dscratch("srep_s", (2, H * Q), F32)
    bcC_s = dscratch("bcC_s", (64, TILE), F32)
    scl_dbg = dscratch("scl_dbg", (256, 128), F32) if debug else None
    sum_dbg = dscratch("sum_dbg", (256, 128), F32) if debug else None

    dbg_outs = {}

    def dbg(name, src_ap, shape, dt=F32, reads=(), eng="sp"):
        if not debug:
            return
        o = nc.dram_tensor("dbg_" + name, list(shape), dt, kind="ExternalOutput").ap()
        dbg_outs[name] = o
        p.add(eng, lambda e, o=o, s=src_ap: e.dma_start(out=o[...], in_=s),
              reads=list(reads), dma=True)

    sb_allocs = []

    def sb(name, shape, dt, **kw):
        h = nc.alloc_sbuf_tensor(name, list(shape), dt, **kw)
        sb_allocs.append(h)
        return h.ap()

    ident = sb("sb_ident", (128, 128), F32)
    masklog = sb("sb_masklog", (128, 128), F32)
    iota8 = sb("sb_iota8", (128, NE), F32)
    convw_sb = sb("convw_sb", (128, MC * 4), F32)
    convb_sb = sb("convb_sb", (128, MC), F32)
    alog_sb = sb("alog_sb", (H, 1), F32)
    dtb_sb = sb("dtb_sb", (H, 1), F32)
    drep = sb("drep", (128, DIN), BF16)
    nwrep = sb("nwrep", (128, DIN), F32)
    lngrep = sb("lngrep", (128, D), F32)
    lnbrep = sb("lnbrep", (128, D), F32)
    ones32 = sb("ones32", (H, Q), F32)
    rwT = sb("rwT", (128, 8 * NE), BF16)
    cbias = sb("cbias", (128, 4), F32)

    import contextlib
    ph0 = contextlib.ExitStack()

    def sb0(name, shape, dt, **kw):
        h = ph0.enter_context(nc.sbuf_tensor(name, list(shape), dt, **kw))
        return h.ap()

    NRING = 4
    q_f32 = [sb0(f"q_f32_{i}", (128, DIN), F32) for i in range(NRING)]
    q_aw = [sb0(f"q_aw_{i}", (128, DIN), F32) for i in range(2)]
    q_sg = [sb0(f"q_sg_{i}", (128, DIN), F32) for i in range(2)]
    q_bf = [sb0(f"q_bf_{i}", (128, DIN), BF16) for i in range(NRING)]
    q_col = [sb0(f"q_col_{i}", (128, 8), F32) for i in range(NRING)]
    stage = [sb0(f"stage_{i}", (128, 16 * 128), BF16) for i in range(NRING)]
    identb = sb0("sb_identb", (128, 128), BF16)
    ps0 = [ph0.enter_context(nc.psum_tensor(f"p0ps{i}", [128, 1024], BF16)).ap()
           for i in range(4)]
    rwpad = sb0("rwpad", (16, D), BF16, is_transpose_dest=True)
    rwtr = sb0("rwtr", (128, 16), BF16, is_transpose_dest=True)

    def A(e):
        return e

    p.add("sp", lambda e: e.dma_start(out=ident[...], in_=ident_d[...]), writes=["ident"], dma=True)
    p.add("sp", lambda e: e.dma_start(out=identb[...], in_=identb_d[...]), writes=["identb"], dma=True)
    p.add("sp", lambda e: e.dma_start(out=masklog[...], in_=masklog_d[...]), writes=["masklog"], dma=True)
    p.add("sp", lambda e: e.dma_start(out=iota8[...], in_=iota8_d[...]), writes=["iota8"], dma=True)
    p.add("sp", lambda e: e.dma_start(out=convw_sb[:, 0:33 * 4].rearrange("p (m k) -> p m k", k=4), in_=convw_d[0:33 * 128, :].rearrange("(m p) k -> p m k", p=128)),
          writes=["convw"], dma=True)
    p.add("sp", lambda e: e.dma_start(out=convw_sb[0:32, 33 * 4:34 * 4], in_=convw_d[33 * 128:CIN, :]),
          writes=["convw"], dma=True)
    p.add("sp", lambda e: e.dma_start(out=convb_sb[:, 0:33], in_=convb_d[0:33 * 128].rearrange("(m p) -> p m", p=128), allow_slow_non_contiguous=True),
          writes=["convb"], dma=True)
    p.add("sp", lambda e: e.dma_start(out=convb_sb[0:32, 33:34], in_=convb_d[33 * 128:CIN].unsqueeze(1)),
          writes=["convb"], dma=True)
    p.add("sp", lambda e: e.dma_start(out=alog_sb[...], in_=alog_d.unsqueeze(1)), writes=["alog"], dma=True)
    p.add("sp", lambda e: e.dma_start(out=dtb_sb[...], in_=dtb_d.unsqueeze(1)), writes=["dtb"], dma=True)
    p.add("pool", lambda e: e.dma_start(out=drep[...], in_=dparam_d.unsqueeze(0).to_broadcast([128, DIN])),
          writes=["drep"], dma=True)
    p.add("act", lambda e: e.dma_start(out=nwrep[...], in_=mnw_d.unsqueeze(0).to_broadcast([128, DIN])),
          writes=["nwrep"], dma=True)
    p.add("act", lambda e: e.dma_start(out=lngrep[...], in_=lng_d.unsqueeze(0).to_broadcast([128, D])),
          writes=["lngrep"], dma=True)
    p.add("act", lambda e: e.dma_start(out=lnbrep[...], in_=lnb_d.unsqueeze(0).to_broadcast([128, D])),
          writes=["lnbrep"], dma=True)
    p.add("dve", lambda e: e.memset(ones32[...], 1.0), writes=["ones32"])
    p.add("dve", lambda e: e.memset(cbias[:, 0:1], LN_EPS), writes=["cbias"])
    p.add("dve", lambda e: e.memset(cbias[:, 1:2], 1.0), reads=["cbias"], writes=["cbias"])
    p.add("dve", lambda e: e.memset(cbias[:, 2:3], F32EPS), reads=["cbias"], writes=["cbias"])
    p.add("dve", lambda e: e.memset(convw_sb[32:64, 33 * 4:34 * 4], 0.0), reads=["convw"], writes=["convw"])
    p.add("dve", lambda e: e.memset(convw_sb[64:128, 33 * 4:34 * 4], 0.0), reads=["convw"], writes=["convw"])
    p.add("dve", lambda e: e.memset(convb_sb[32:64, 33:34], 0.0), reads=["convb"], writes=["convb"])
    p.add("dve", lambda e: e.memset(convb_sb[64:128, 33:34], 0.0), reads=["convb"], writes=["convb"])

    p.add("dve", lambda e: e.memset(rwpad[...], 0.0), writes=["rwpad"])
    p.add("pool", lambda e: e.dma_start(out=rwpad[0:NE, :], in_=rw_d[...]), reads=["rwpad"], writes=["rwpad"], dma=True)
    for k in range(8):
        p.add("act", lambda e, k=k: e.dma_start_transpose(rwtr[...], rwpad[:, k * 128:(k + 1) * 128]),
              reads=["rwpad"], writes=["rwtr"], dma=True)
        p.add("dve", lambda e, k=k: e.tensor_copy(rwT[:, k * NE:(k + 1) * NE], rwtr[:, 0:NE]),
              reads=["rwtr"], writes=["rwT"])

    jobs = []
    for mc in range(MC):
        rows = 128 if mc < 33 else 32
        jobs.append(("cast", w_in_d[mc * 128:mc * 128 + rows, :], D, 8, w_inT_s[mc, :, :], rows))
    for md in range(8):
        jobs.append(("cast", w_out_d[md * 128:(md + 1) * 128, :], DIN, 16, w_outT_s[md, :, :], 128))
    for e_i in range(nexp):
        for fc in range(16):
            srcw = (w1_d[e_i, fc * 128:(fc + 1) * 128, :] if e_i < NE else sw1_d[fc * 128:(fc + 1) * 128, :])
            jobs.append(("quant", srcw, D, 8, w1T_s[e_i, fc, :, :], 128))
        for md in range(8):
            srcw = (w2_d[e_i, md * 128:(md + 1) * 128, :] if e_i < NE else sw2_d[md * 128:(md + 1) * 128, :])
            jobs.append(("quant", srcw, DIN, 16, w2T_s[e_i, md, :, :], 128))

    LA = 3

    def p0_load(i):
        kind, s, K, nt, out_ap, rows = jobs[i]
        r = i % NRING
        if kind == "cast":
            p.add("pool", lambda e, s=s, K=K, r=r, rows=rows: e.dma_start(out=q_bf[r][0:rows, 0:K], in_=s),
                  reads=[f"qbf{r}"], writes=[f"qbf{r}"], dma=True)
            if rows < 128:
                p.add("dve", lambda e, r=r, rows=rows, K=K: e.memset(q_bf[r][rows:64, 0:K], 0.0),
                      reads=[f"qbf{r}"], writes=[f"qbf{r}"])
                p.add("dve", lambda e, r=r, K=K: e.memset(q_bf[r][64:128, 0:K], 0.0),
                      reads=[f"qbf{r}"], writes=[f"qbf{r}"])
        else:
            p.add("sp", lambda e, s=s, K=K, r=r: e.dma_start(out=q_f32[r][:, 0:K], in_=s),
                  reads=[f"qf{r}"], writes=[f"qf{r}"], dma=True)

    def p0_process(i):
        kind, s, K, nt, out_ap, rows = jobs[i]
        r = i % NRING
        b = i % 2
        if kind == "quant":
            p.add("dve", lambda e, r=r, K=K: e.reduce_sum(q_col[r][:, 0:1], q_f32[r][:, 0:K], axis=AX.X,
                                                          apply_absolute_value=True),
                  reads=[f"qf{r}"], writes=[f"qcol{r}"])
            p.add("dve", lambda e, r=r, K=K: e.tensor_scalar(q_col[r][:, 1:2], q_col[r][:, 0:1], 0.5 / K, 0.5e-5,
                                                             ALU.mult, ALU.max),
                  reads=[f"qcol{r}"], writes=[f"qcol{r}"])
            p.add("dve", lambda e, r=r, K=K: e.tensor_scalar(q_col[r][:, 2:3], q_col[r][:, 0:1], 1.0 / K, 1e-5,
                                                             ALU.mult, ALU.max),
                  reads=[f"qcol{r}"], writes=[f"qcol{r}"])
            p.add("dve", lambda e, r=r: e.tensor_scalar_mul(q_col[r][:, 3:4], q_col[r][:, 1:2], -1.0),
                  reads=[f"qcol{r}"], writes=[f"qcol{r}"])
            p.add("dve", lambda e, r=r, b=b, K=K: e.tensor_scalar(q_aw[b][:, 0:K], q_f32[r][:, 0:K],
                                                                  q_col[r][:, 1:2], None, ALU.is_gt),
                  reads=[f"qf{r}", f"qcol{r}"], writes=[f"qaw{b}"])
            p.add("dve", lambda e, r=r, b=b, K=K: e.tensor_scalar(q_sg[b][:, 0:K], q_f32[r][:, 0:K],
                                                                  q_col[r][:, 3:4], None, ALU.is_lt),
                  reads=[f"qf{r}", f"qcol{r}"], writes=[f"qsg{b}"])
            p.add("pool", lambda e, r=r, b=b, K=K: e.tensor_sub(q_f32[r][:, 0:K], q_aw[b][:, 0:K], q_sg[b][:, 0:K]),
                  reads=[f"qaw{b}", f"qsg{b}"], writes=[f"qf{r}"])
            p.add("dve", lambda e, r=r, K=K: e.tensor_scalar_mul(q_bf[r][:, 0:K], q_f32[r][:, 0:K], q_col[r][:, 2:3]),
                  reads=[f"qf{r}", f"qcol{r}"], writes=[f"qbf{r}"])
        for g4 in range(nt // 4):
            pp = g4 % 4
            for j in range(4):
                k = g4 * 4 + j
                p.add("pe", lambda e, r=r, k=k, j=j, pp=pp:
                      e.transpose(ps0[pp][:, j * 128:(j + 1) * 128], q_bf[r][:, k * 128:(k + 1) * 128], identb[...]),
                      reads=[f"qbf{r}", "identb"], writes=[f"p0ps{pp}"])
            p.add("dve", lambda e, r=r, g4=g4, pp=pp:
                  e.tensor_copy(stage[r][:, g4 * 512:(g4 + 1) * 512], ps0[pp][:, 0:512]),
                  reads=[f"p0ps{pp}"], writes=[f"stage{r}"])
        p.add("act", lambda e, r=r, out_ap=out_ap, nt=nt: e.dma_start(out=out_ap, in_=stage[r][:, 0:nt * 128]),
              reads=[f"stage{r}"], writes=[f"stage{r}"], dma=True)

    for i in range(len(jobs)):
        if i == 0:
            for k in range(LA):
                p0_load(k)
        elif i + LA - 1 < len(jobs):
            p0_load(i + LA - 1)
        p0_process(i)

    if debug == "phase0":
        dbg("w_inT", w_inT_s[...], (MC, 128, 8 * 128), BF16, reads=["stage0", "stage1"])
        dbg("w1T", w1T_s[...], (nexp, 16, 128, 8 * 128), BF16, reads=["stage0", "stage1"])
        dbg("w2T", w2T_s[...], (nexp, 8, 128, 16 * 128), BF16, reads=["stage0", "stage1"])
        dbg("w_outT_sb", w_outT_s[...], (8, 128, 16 * 128), BF16, reads=["stage0", "stage1"])
        dbg("rwT_sb", rwT[...], (128, 8 * NE), BF16, reads=["rwT"])
        dbg("drep", drep[...], (128, DIN), BF16, reads=["drep"])
        dbg("convw", convw_sb[...], (128, MC * 4), F32, reads=["convw"])


    p.barrier()
    ph0.close()
    pa = contextlib.ExitStack()

    def sba(name, shape, dt, **kw):
        h = pa.enter_context(nc.sbuf_tensor(name, list(shape), dt, **kw))
        return h.ap()

    x_tm = sba("x_tm", (128, D), F32)
    t_tm = sba("t_tm", (128, D), F32)
    xn_tm = sba("xn_tm", (128, D), F32)
    h_tm = sba("h_tm", (128, D), F32)
    stat = sba("stat", (128, 16), F32)
    xn_fm = sba("xn_fm", (128, 8 * TILE), BF16)
    h_fm = sba("h_fm", (128, 8 * TILE), BF16)
    sz = sba("sz", (128, 16 * TILE), BF16)
    xh = sba("xh", (128, NCH * DIN), BF16)
    bc = sba("bc", (128, TILE), F32)
    bcn = sba("bcn", (64, TILE), F32)
    dtr = sba("dtr", (H, TILE), F32)
    dtw = sba("dtw", (H, TILE * 7), F32)
    cvp = [sba(f"cv{i}", (128, TILE), F32) for i in range(2)]
    zx = [sba(f"zx{i}", (128, TILE + 3), F32) for i in range(2)]
    tails = sba("tails", (128, MC * 3), F32)
    srep = sba("srep", (128, 16 * Q), F32)
    difft = sba("difft", (128, 16 * Q), F32)
    mt = [sba(f"mt{i}", (128, 16 * Q), BF16) for i in range(2)]
    cbt = [sba(f"cbt{i}", (128, Q), F32) for i in range(2)]
    stkT = [sba(f"stkT{i}", (128, 160), F32) for i in range(2)]
    expsT = [sba(f"expsT{i}", (128, H), F32) for i in range(2)]
    expsl = [sba(f"expsl{i}", (64, H), F32) for i in range(2)]
    wxh0 = sba("wxh0", (128, DIN), BF16)
    wxh = [wxh0, wxh0]
    bwb = [sba(f"bwb{i}", (128, 64), BF16) for i in range(2)]
    yevt = sba("yevt", (128, 512), F32)
    ysb = sba("ysb", (128, DIN), F32)
    ygat = sba("ygat", (128, 16 * TILE), BF16)
    attnb = sba("attnb", (128, TILE), F32)
    statebuf = [sba(f"state{i}", (64, DIN), F32) for i in range(2)]
    wt = [sba(f"wt{i}", (128, 8 * 128), BF16) for i in range(2)]
    wo = [sba(f"wo{i}", (128, 16 * 128), BF16) for i in range(2)]
    logit = sba("logit", (128, NE), F32)
    gtile = sba("gtile", (8, Q), F32)
    gE = sba("gE", (128, NE), F32)
    ones8 = sba("ones8", (128, NE), F32)
    gF = sba("gF", (128, NE), F32)
    ps = [pa.enter_context(nc.psum_tensor(f"ps{i}", [128, 512], F32)).ap()
          for i in range(8)]

    p.add("dve", lambda e: e.memset(tails[...], 0.0), writes=["tails"])
    p.add("dve", lambda e: e.memset(ones8[...], 1.0), writes=["ones8"])
    p.add("dve", lambda e: e.memset(statebuf[0][...], 0.0), writes=["st0"])

    xn_fm3 = xn_fm.rearrange("p (k t) -> p k t", t=TILE)
    h_fm3 = h_fm.rearrange("p (k t) -> p k t", t=TILE)
    sz3 = sz.rearrange("p (k t) -> p k t", t=TILE)
    xh3 = xh.rearrange("p (s f) -> p s f", f=DIN)
    ygat3 = ygat.rearrange("p (k t) -> p k t", t=TILE)
    dtw3 = dtw.rearrange("p (i t) -> p i t", t=TILE)
    D_DT, D_LN, D_S, D_SML, D_W, D_S1, D_S2 = range(7)
    D_A = D_S1

    def cw_col(mc, k):
        return convw_sb[:, mc * 4 + k:mc * 4 + k + 1]

    def dsl(i, cols=None):
        a = dtw3[:, i, :] if cols is None else dtw3[:, i, cols]
        return a

    def ln_and_transpose(it, s):
        t0 = it * TILE
        p.add("sp", lambda e, t0=t0, s=s: e.dma_start(out=x_tm[...], in_=x_d[t0 + s * 128:t0 + (s + 1) * 128, :]),
              reads=["x_tm"], writes=["x_tm"], dma=True)
        p.add("dve", lambda e: e.reduce_sum(stat[:, 0:1], x_tm[...], axis=AX.X), reads=["x_tm"], writes=["stat"])
        p.add("dve", lambda e: e.tensor_scalar_mul(stat[:, 1:2], stat[:, 0:1], 1.0 / D), reads=["stat"], writes=["stat"])
        p.add("dve", lambda e: e.tensor_scalar(t_tm[...], x_tm[...], stat[:, 1:2], None, ALU.subtract),
              reads=["x_tm", "stat"], writes=["t_tm"])
        p.add("act", lambda e: e.activation(xn_tm[...], t_tm[...], AF.Square, accum_out=stat[:, 2:3]),
              reads=["t_tm"], writes=["xn_tm", "stat"])
        p.add("act", lambda e: e.activation(stat[:, 3:4], stat[:, 2:3], AF.Ln, bias=cbias[:, 0:1], scale=1.0 / D),
              reads=["stat", "cbias"], writes=["stat"])
        p.add("act", lambda e: e.activation(stat[:, 4:5], stat[:, 3:4], AF.Exp, scale=-0.5),
              reads=["stat"], writes=["stat"])
        p.add("dve", lambda e: e.tensor_scalar_mul(xn_tm[...], t_tm[...], stat[:, 4:5]),
              reads=["t_tm", "stat"], writes=["xn_tm"])
        p.add("dve", lambda e: e.reduce_sum(stat[:, 5:6], xn_tm[...], axis=AX.X), reads=["xn_tm"], writes=["stat"])
        p.add("dve", lambda e: e.tensor_scalar_mul(stat[:, 6:7], stat[:, 5:6], 1.0 / D), reads=["stat"], writes=["stat"])
        p.add("dve", lambda e: e.tensor_scalar(t_tm[...], xn_tm[...], stat[:, 6:7], None, ALU.subtract),
              reads=["xn_tm", "stat"], writes=["t_tm"])
        p.add("act", lambda e: e.activation(h_tm[...], t_tm[...], AF.Square, accum_out=stat[:, 7:8]),
              reads=["t_tm"], writes=["h_tm", "stat"])
        p.add("act", lambda e: e.activation(stat[:, 8:9], stat[:, 7:8], AF.Ln, bias=cbias[:, 0:1], scale=1.0 / D),
              reads=["stat", "cbias"], writes=["stat"])
        p.add("act", lambda e: e.activation(stat[:, 9:10], stat[:, 8:9], AF.Exp, scale=-0.5),
              reads=["stat"], writes=["stat"])
        p.add("dve", lambda e: e.scalar_tensor_tensor(h_tm[...], t_tm[...], stat[:, 9:10], lngrep[...], ALU.mult, ALU.mult),
              reads=["t_tm", "stat", "lngrep"], writes=["h_tm"])
        p.add("dve", lambda e: e.tensor_add(t_tm[...], h_tm[...], lnbrep[...]),
              reads=["h_tm", "lnbrep"], writes=["t_tm"])
        for src_ap, src_b, dst3, dst_b in ((xn_tm, "xn_tm", xn_fm3, "xn_fm"),
                                           (t_tm, "t_tm", h_fm3, "h_fm")):
            for half in range(2):
                for k4 in range(4):
                    k = half * 4 + k4
                    p.add("pe", lambda e, src_ap=src_ap, k=k, k4=k4:
                          e.transpose(ps[2][:, k4 * 128:(k4 + 1) * 128], src_ap[:, k * 128:(k + 1) * 128], ident[...]),
                          reads=[src_b, "ident"], writes=["ps2"])
                p.add("dve", lambda e, dst3=dst3, half=half, s=s:
                      e.tensor_copy(dst3[:, half * 4:(half + 1) * 4, s * 128:(s + 1) * 128],
                                    ps[2][:, 0:512].rearrange("p (a b) -> p a b", b=128)),
                      reads=["ps2"], writes=[dst_b])
        for k in range(8):
            p.add("pe", lambda e, k=k, s=s: e.matmul(ps[2][:, 0:8], h_fm3[:, k, s * 128:(s + 1) * 128],
                                                     rwT[:, k * NE:(k + 1) * NE], start=(k == 0), stop=(k == 7)),
                  reads=["h_fm", "rwT"], writes=["ps2"])
        p.add("dve", lambda e: e.tensor_copy(logit[...], ps[2][:, 0:8]), reads=["ps2"], writes=["logit"])
        p.add("dve", lambda e: e.reduce_max(stat[:, 10:11], logit[...], axis=AX.X), reads=["logit"], writes=["stat"])
        p.add("dve", lambda e: e.tensor_scalar_mul(stat[:, 11:12], stat[:, 10:11], -1.0), reads=["stat"], writes=["stat"])
        p.add("act", lambda e: e.activation(gE[...], logit[...], AF.Exp, bias=stat[:, 11:12], accum_out=stat[:, 12:13]),
              reads=["logit", "stat"], writes=["gE", "stat"])
        p.add("dve", lambda e: e.reciprocal(stat[:, 13:14], stat[:, 12:13]), reads=["stat"], writes=["stat"])
        p.add("dve", lambda e: e.tensor_scalar(gE[...], logit[...], stat[:, 10:11], None, ALU.is_equal),
              reads=["logit", "stat"], writes=["gE"])
        p.add("dve", lambda e: e.tensor_tensor_scan(gF[...], ones8[...], gE[...], 0.0, ALU.mult, ALU.add),
              reads=["gE", "ones8"], writes=["gF"])
        p.add("dve", lambda e: e.tensor_scalar(gF[...], gF[...], 1.0, None, ALU.is_equal),
              reads=["gF"], writes=["gF"])
        p.add("dve", lambda e: e.tensor_mul(gE[...], gE[...], gF[...]), reads=["gE", "gF"], writes=["gE"])
        p.add("dve", lambda e: e.tensor_scalar_mul(gF[...], gE[...], stat[:, 13:14]), reads=["gE", "stat"], writes=["gF"])
        p.add("pe", lambda e: e.transpose(ps[2][0:8, 0:128], gF[...], ident[...]), reads=["gF", "ident"], writes=["ps2"])
        p.add("dve", lambda e: e.tensor_copy(gtile[...], ps[2][0:8, 0:128]), reads=["ps2"], writes=["gtile"])
        p.add("sp", lambda e, t0=t0, s=s: e.dma_start(out=g_s[:, t0 + s * 128:t0 + (s + 1) * 128], in_=gtile[...]),
              reads=["gtile"], dma=True)
        if debug == "phaseA2" and it == 0 and s == 0:
            dbg("logit", logit[...], (128, NE), F32, reads=["logit"])
            dbg("gE", gE[...], (128, NE), F32, reads=["gE"])

            dbg("gF", gF[...], (128, NE), F32, reads=["gF"])
            dbg("gtile", gtile[...], (8, Q), F32, reads=["gtile"])
            dbg("statv", stat[...], (128, 16), F32, reads=["stat"])

    def in_proj_conv(it):
        if debug == "phaseA1" and it == 0:
            dbg("xnfm", xn_fm[...], (128, 8 * TILE), BF16, reads=["xn_fm"])
        for mc in range(MC):
            b = mc % 2
            p.add("sp", lambda e, mc=mc, b=b: e.dma_start(out=wt[b][...], in_=w_inT_s[mc, :, :]),
                  reads=[f"wt{b}"], writes=[f"wt{b}"], dma=True)
            if debug == "phaseA1" and it == 0 and mc == 0:
                dbg("wt0", wt[b][...], (128, 8 * 128), BF16, reads=[f"wt{b}"])
            for k in range(8):
                p.add("pe", lambda e, k=k, b=b: e.matmul(ps[b][:, 0:512], wt[b][:, k * 128:(k + 1) * 128],
                                                         xn_fm3[:, k, :], start=(k == 0), stop=(k == 7)),
                      reads=[f"wt{b}", "xn_fm"], writes=[f"ps{b}"])
            p.add("dve", lambda e, mc=mc, b=b: e.tensor_copy(zx[b][:, 0:3], tails[:, mc * 3:(mc + 1) * 3]),
                  reads=["tails"], writes=[f"zx{b}"])
            p.add("act", lambda e, b=b: e.copy(zx[b][:, 3:3 + TILE], ps[b][:, 0:512]),
                  reads=[f"ps{b}"], writes=[f"zx{b}"])
            p.add("dve", lambda e, mc=mc, b=b: e.tensor_copy(tails[:, mc * 3:(mc + 1) * 3], zx[b][:, TILE:TILE + 3]),
                  reads=[f"zx{b}"], writes=["tails"])
            p.add("dve", lambda e, mc=mc, b=b:
                  e.scalar_tensor_tensor(cvp[0][...], zx[b][:, 0:TILE], cw_col(mc, 0),
                                         convb_sb[:, mc:mc + 1].to_broadcast([128, TILE]), ALU.mult, ALU.add),
                  reads=[f"zx{b}", "convw", "convb"], writes=["cv0"])
            for kk in range(1, 4):
                src_cv, dst_cv = (kk + 1) % 2, kk % 2
                p.add("dve", lambda e, mc=mc, b=b, kk=kk, src_cv=src_cv, dst_cv=dst_cv:
                      e.scalar_tensor_tensor(cvp[dst_cv][...], zx[b][:, kk:kk + TILE], cw_col(mc, kk),
                                             cvp[src_cv][...], ALU.mult, ALU.add),
                      reads=[f"zx{b}", "convw", f"cv{src_cv}"], writes=[f"cv{dst_cv}"])
            if debug == "phaseA1" and it == 0 and mc in (0, 16, 32, 33):
                dbg(f"zx{mc}", zx[b][...], (128, TILE + 3), F32, reads=[f"zx{b}"])
                dbg(f"cv{mc}", cvp[1][...], (128, TILE), F32, reads=["cv1"])
            if mc < 16:
                p.add("act", lambda e, mc=mc: e.activation(sz3[:, mc, :], cvp[1][...], AF.Silu),
                      reads=["cv1"], writes=["sz"])
            elif mc < 32:
                for s4 in range(4):
                    p.add("pe", lambda e, s4=s4: e.transpose(ps[2][:, s4 * 128:(s4 + 1) * 128],
                                                             cvp[1][:, s4 * 128:(s4 + 1) * 128], ident[...]),
                          reads=["cv1", "ident"], writes=["ps2"])
                p.add("dve", lambda e, mc=mc: e.tensor_copy(xh3[:, :, (mc - 16) * 128:(mc - 15) * 128],
                                                            ps[2][:, 0:512].rearrange("p (a b) -> p a b", b=128)),
                      reads=["ps2"], writes=["xh"])
            elif mc == 32:
                p.add("dve", lambda e: e.tensor_copy(bc[...], cvp[1][...]), reads=["cv1"], writes=["bc"])
                p.add("sp", lambda e: e.dma_start(out=bcC_s[...], in_=bc[64:128, :]),
                      reads=["bc"], writes=["bcCs"], dma=True)
                p.add("sp", lambda e: e.dma_start(out=bcn[...], in_=bcC_s[...]),
                      reads=["bcCs", "bcn"], writes=["bcn"], dma=True)
            else:
                p.add("dve", lambda e: e.tensor_copy(dtr[...], cvp[1][0:H, :]), reads=["cv1"], writes=["dtr"])

    def dt_pipeline(it):
        p.add("act", lambda e: e.activation(dsl(D_S1), dtr[...], AF.Abs, bias=dtb_sb[:, 0:1]),
              reads=["dtr", "dtb"], writes=["dtw"])
        p.add("act", lambda e: e.activation(dsl(D_S2), dsl(D_S1), AF.Exp, scale=-1.0), reads=["dtw"], writes=["dtw"])
        p.add("act", lambda e: e.activation(dsl(D_S1), dsl(D_S2), AF.Ln, bias=cbias[0:H, 1:2]), reads=["dtw", "cbias"], writes=["dtw"])
        p.add("act", lambda e: e.activation(dsl(D_S2), dtr[...], AF.Relu, bias=dtb_sb[:, 0:1]),
              reads=["dtr", "dtb"], writes=["dtw"])
        p.add("dve", lambda e: e.tensor_add(dsl(D_DT), dsl(D_S1), dsl(D_S2)), reads=["dtw"], writes=["dtw"])
        p.add("dve", lambda e: e.tensor_scalar_mul(dsl(D_A), dsl(D_DT), alog_sb[:, 0:1]),
              reads=["dtw", "alog"], writes=["dtw"])
        p.add("act", lambda e: e.activation(dsl(D_LN), dsl(D_DT), AF.Ln), reads=["dtw"], writes=["dtw"])
        for c in range(NCH):
            cc = slice(c * Q, (c + 1) * Q)
            p.add("dve", lambda e, cc=cc: e.tensor_tensor_scan(dsl(D_S, cc), ones32[0:H, 0:Q], dsl(D_A, cc),
                                                               0.0, ALU.mult, ALU.add),
                  reads=["dtw", "ones32"], writes=["dtw"])
        p.add("dve", lambda e: e.tensor_sub(dsl(D_SML), dsl(D_S), dsl(D_LN)), reads=["dtw"], writes=["dtw"])
        for c in range(NCH):
            cc = slice(c * Q, (c + 1) * Q)
            p.add("dve", lambda e, c=c, cc=cc: e.tensor_scalar(dsl(D_S1, cc), dsl(D_S, cc),
                                                               dtw3[:, D_S, c * Q + Q - 1:c * Q + Q], None, ALU.subtract),
                  reads=["dtw"], writes=["dtw"])
            p.add("act", lambda e, cc=cc: e.activation(dsl(D_S2, cc), dsl(D_S1, cc), AF.Exp, scale=-1.0),
                  reads=["dtw"], writes=["dtw"])
            p.add("dve", lambda e, cc=cc: e.tensor_mul(dsl(D_W, cc), dsl(D_DT, cc), dsl(D_S2, cc)),
                  reads=["dtw"], writes=["dtw"])

    def chunk_scan(it, c):
        gc = it * NCH + c
        slot = gc % 2
        pr = gc % 2
        sr, sw = f"st{gc % 2}", f"st{(gc + 1) % 2}"
        cc = slice(c * Q, (c + 1) * Q)
        wr_s = p.add("sp", lambda e, slot=slot, cc=cc:
                     e.dma_start(out=srep_s[slot, :].unsqueeze(0).rearrange("a (h q) -> (a h) q", q=Q), in_=dsl(D_S, cc)),
                     reads=["dtw", f"sreps{slot}"], writes=[f"sreps{slot}"], dma=True)
        p.add("pe", lambda e, cc=cc: e.transpose(ps[2][:, 0:64], bc[0:64, cc], ident[0:64, 0:64]),
              reads=["bc", "ident"], writes=["ps2"])
        p.add("pe", lambda e, cc=cc: e.transpose(ps[2][:, 64:96], dsl(D_W, cc), ident[0:H, 0:H]),
              reads=["dtw", "ident"], writes=["ps2"])
        p.add("pe", lambda e, cc=cc: e.transpose(ps[2][:, 96:128], dsl(D_SML, cc), ident[0:H, 0:H]),
              reads=["dtw", "ident"], writes=["ps2"])
        p.add("pe", lambda e, cc=cc: e.transpose(ps[2][:, 128:160], dsl(D_S, cc), ident[0:H, 0:H]),
              reads=["dtw", "ident"], writes=["ps2"])
        p.add("dve", lambda e, pr=pr: e.tensor_copy(stkT[pr][...], ps[2][:, 0:160]),
              reads=["ps2"], writes=[f"stkT{pr}"])
        p.add("act", lambda e, pr=pr: e.activation(expsT[pr][...], stkT[pr][:, 128:160], AF.Exp),
              reads=[f"stkT{pr}"], writes=[f"expsT{pr}"])
        p.add("pe", lambda e, cc=cc: e.matmul(ps[2][:, 0:128], bc[0:64, cc], bcn[0:64, cc], start=True, stop=True),
              reads=["bc", "bcn"], writes=["ps2"])
        p.add("act", lambda e, pr=pr: e.copy(cbt[pr][...], ps[2][:, 0:128]), reads=["ps2"], writes=[f"cbt{pr}"])
        p.add("dve", lambda e, c=c, pr=pr:
              e.tensor_tensor(wxh[pr][...].rearrange("p (h e) -> p h e", e=PDIM),
                              xh3[:, c, :].rearrange("p (h e) -> p h e", e=PDIM),
                              stkT[pr][:, 64:96].unsqueeze(2).broadcast_to([128, H, PDIM]), ALU.mult),
              reads=["xh", f"stkT{pr}"], writes=["wxh0"])
        p.add("dve", lambda e, pr=pr: e.tensor_copy(bwb[pr][...], stkT[pr][:, 0:64]),
              reads=[f"stkT{pr}"], writes=[f"bwb{pr}"])
        for half in range(2):
            rd = p.add("act", lambda e, slot=slot, half=half:
                       e.dma_start(out=srep[...], in_=srep_s[slot, half * 16 * Q:(half + 1) * 16 * Q]
                                   .unsqueeze(0).to_broadcast([128, 16 * Q])),
                       reads=[f"sreps{slot}", "srep"], writes=["srep"], dma=True, extra_deps=[wr_s])
            srep3 = srep.rearrange("p (h q) -> p h q", q=Q)
            p.add("act", lambda e, half=half, srep3=srep3, pr=pr:
                  e.activation(expsl[pr][:, half * 16:(half + 1) * 16], srep3[0:64, :, Q - 1], AF.Exp),
                  reads=["srep"], writes=[f"expsl{pr}"])
            p.add("dve", lambda e, half=half, pr=pr:
                  e.tensor_tensor(difft[...].rearrange("p (h q) -> p h q", q=Q),
                                  srep[...].rearrange("p (h q) -> p h q", q=Q),
                                  stkT[pr][:, 96 + half * 16:96 + (half + 1) * 16].unsqueeze(2).broadcast_to([128, 16, Q]),
                                  ALU.subtract),
                  reads=["srep", f"stkT{pr}"], writes=["difft"])
            p.add("dve", lambda e:
                  e.tensor_tensor(difft[...].rearrange("p (h q) -> p h q", q=Q),
                                  difft[...].rearrange("p (h q) -> p h q", q=Q),
                                  masklog[...].unsqueeze(1).broadcast_to([128, 16, Q]), ALU.add),
                  reads=["difft"], writes=["difft"])
            p.add("act", lambda e: e.activation(srep[...], difft[...], AF.Exp), reads=["difft"], writes=["srep"])
            p.add("dve", lambda e, half=half, pr=pr:
                  e.tensor_tensor(mt[pr][...].rearrange("p (h q) -> p h q", q=Q),
                                  srep[...].rearrange("p (h q) -> p h q", q=Q),
                                  cbt[pr][...].unsqueeze(1).broadcast_to([128, 16, Q]), ALU.mult),
                  reads=["srep", f"cbt{pr}"], writes=[f"mt{pr}"])
            chunk_headgroups(c, gc, cc, sr, sw, half, pr)

        p.add("pool", lambda e, c=c: e.tensor_mul(difft[:, 0:DIN], xh3[:, c, :], drep[...]),
              reads=["xh", "drep"], writes=["difft"])
        p.add("pool", lambda e: e.tensor_add(ysb[...], ysb[...], difft[:, 0:DIN]), reads=["ysb", "difft"], writes=["ysb"])
        p.add("act", lambda e: e.activation(difft[:, 0:DIN], ysb[...], AF.Square, accum_out=stat[:, 14:15]),
              reads=["ysb"], writes=["difft", "stat"])
        p.add("act", lambda e: e.activation(stat[:, 15:16], stat[:, 14:15], AF.Ln, bias=cbias[:, 2:3], scale=1.0 / DIN),
              reads=["stat", "cbias"], writes=["stat"])
        p.add("act", lambda e: e.activation(stat[:, 14:15], stat[:, 15:16], AF.Exp, scale=-0.5),
              reads=["stat"], writes=["stat"])
        p.add("dve", lambda e: e.scalar_tensor_tensor(ysb[...], ysb[...], stat[:, 14:15], nwrep[...],
                                                      ALU.mult, ALU.mult),
              reads=["ysb", "stat", "nwrep"], writes=["ysb"])
        for q4 in range(4):
            for j in range(4):
                fc = q4 * 4 + j
                p.add("pe", lambda e, fc=fc, j=j: e.transpose(ps[6][:, j * 128:(j + 1) * 128],
                                                              ysb[:, fc * 128:(fc + 1) * 128], ident[...]),
                      reads=["ysb", "ident"], writes=["ps6"])
            p.add("dve", lambda e, q4=q4, c=c:
                  e.tensor_mul(ygat3[:, q4 * 4:(q4 + 1) * 4, c * 128:(c + 1) * 128],
                               ps[6][:, 0:512].rearrange("p (a b) -> p a b", b=128),
                               sz3[:, q4 * 4:(q4 + 1) * 4, c * 128:(c + 1) * 128]),
                  reads=["ps6", "sz"], writes=["ygat"])
        if debug == "phaseA2" and it == 0 and c == 0:
            dbg("stkT", stkT[0][...], (128, 160), F32, reads=["stkT0"])
            dbg("cbt", cbt[0][...], (128, Q), F32, reads=["cbt0"])
            dbg("mtd", mt[0][...], (128, 16 * Q), BF16, reads=["mt0"])
            dbg("ysbd", ysb[...], (128, DIN), F32, reads=["ysb"])
            dbg("st1d", statebuf[1][...], (64, DIN), F32, reads=["st1"])
            dbg("expsTd", expsT[0][...], (128, H), F32, reads=["expsT0"])
            dbg("expsld", expsl[0][...], (64, H), F32, reads=["expsl0"])
            dbg("bcd", bc[...], (128, TILE), F32, reads=["bc"])
            dbg("bcnd", bcn[...], (64, TILE), F32, reads=["bcn"])

    def chunk_headgroups(c, gc, cc, sr, sw, half, pr):
        for hg in (half * 2, half * 2 + 1):
            j0 = (hg % 2) * 8
            for j in range(8):
                h_ = hg * 8 + j
                p.add("pe", lambda e, h_=h_, j=j, c=c, pr=pr: e.matmul(ps[3][:, j * 64:(j + 1) * 64],
                                                                mt[pr][:, (h_ % 16) * Q:((h_ % 16) + 1) * Q],
                                                                xh3[:, c, h_ * 64:(h_ + 1) * 64], start=True, stop=True),
                      reads=[f"mt{pr}", "xh"], writes=["ps3"])
            p.add("pe", lambda e, hg=hg, cc=cc, gc=gc: e.matmul(ps[4][:, 0:512],
                                                                bcn[0:64, cc],
                                                                statebuf[gc % 2][:, hg * 512:(hg + 1) * 512],
                                                                start=True, stop=True),
                  reads=["bcn", sr], writes=["ps4"])
            p.add("pe", lambda e, hg=hg, c=c, pr=pr: e.matmul(ps[5][0:64, 0:512],
                                                              bwb[pr][...],
                                                              wxh[pr][:, hg * 512:(hg + 1) * 512], start=True, stop=True),
                  reads=[f"bwb{pr}", "wxh0"], writes=["ps5"])
            p.add("act", lambda e, hg=hg: e.copy(ysb[:, hg * 512:(hg + 1) * 512], ps[3][:, 0:512]),
                  reads=["ps3"], writes=["ysb"])
            p.add("dve", lambda e, hg=hg, pr=pr:
                  e.tensor_tensor(yevt[...].rearrange("p (h e) -> p h e", e=PDIM),
                                  ps[4][:, 0:512].rearrange("p (h e) -> p h e", e=PDIM),
                                  expsT[pr][:, hg * 8:(hg + 1) * 8].unsqueeze(2).broadcast_to([128, 8, PDIM]),
                                  ALU.mult),
                  reads=["ps4", f"expsT{pr}"], writes=["yevt"])
            p.add("dve", lambda e, hg=hg: e.tensor_add(ysb[:, hg * 512:(hg + 1) * 512],
                                                       ysb[:, hg * 512:(hg + 1) * 512], yevt[...]),
                  reads=["yevt", "ysb"], writes=["ysb"])
            p.add("dve", lambda e, hg=hg, gc=gc, pr=pr:
                  e.tensor_tensor(yevt[0:64, :].rearrange("p (h e) -> p h e", e=PDIM),
                                  statebuf[gc % 2][:, hg * 512:(hg + 1) * 512].rearrange("p (h e) -> p h e", e=PDIM),
                                  expsl[pr][:, hg * 8:(hg + 1) * 8].unsqueeze(2).broadcast_to([64, 8, PDIM]),
                                  ALU.mult),
                  reads=[sr, f"expsl{pr}"], writes=["yevt"])
            p.add("dve", lambda e, hg=hg, gc=gc:
                  e.tensor_add(statebuf[(gc + 1) % 2][:, hg * 512:(hg + 1) * 512], yevt[0:64, :],
                               ps[5][0:64, 0:512]),
                  reads=["yevt", "ps5"], writes=[sw])


    def out_proj(it):
        t0 = it * TILE
        for md in range(8):
            b = md % 2
            p.add("sp", lambda e, md=md, b=b: e.dma_start(out=wo[b][...], in_=w_outT_s[md, :, :]),
                  reads=[f"wo{b}"], writes=[f"wo{b}"], dma=True)
            for fc in range(16):
                p.add("pe", lambda e, fc=fc, b=b: e.matmul(ps[7][:, 0:512], wo[b][:, fc * 128:(fc + 1) * 128],
                                                           ygat3[:, fc, :], start=(fc == 0), stop=(fc == 15)),
                      reads=[f"wo{b}", "ygat"], writes=["ps7"])
            p.add("act", lambda e, md=md: e.copy(attnb[...], ps[7][:, 0:512]),
                  reads=["ps7"], writes=["attnb"])
            p.add("sp", lambda e, md=md, t0=t0: e.dma_start(out=attn_s[md, :, t0:t0 + TILE], in_=attnb[...]),
                  reads=["attnb"], dma=True)

    for it in range(ntiles):
        for s in range(NCH):
            ln_and_transpose(it, s)
        in_proj_conv(it)
        dt_pipeline(it)
        for c in range(NCH):
            chunk_scan(it, c)
        out_proj(it)
        t0 = it * TILE
        for k in range(8):
            p.add("act", lambda e, k=k, t0=t0: e.dma_start(out=h_s[k, :, t0:t0 + TILE], in_=h_fm3[:, k, :]),
                  reads=["h_fm"], dma=True)

    if debug in ("phaseA", "phaseA1", "phaseA2"):
        p.barrier()
        dbg("attn", attn_s[...], (8, 128, T), F32)
        dbg("h_s", h_s[...], (8, 128, T), BF16)
        dbg("g_s", g_s[...], (NE, T), F32)
        dbg("sz", sz[...], (128, 16 * TILE), BF16, reads=["sz"])
        dbg("xh", xh[...], (128, NCH * DIN), BF16, reads=["xh"])
        dbg("dtw", dtw[...], (H, TILE * 7), F32, reads=["dtw"])
        dbg("state", statebuf[(ntiles * NCH) % 2][...], (64, DIN), F32, reads=[f"st{(ntiles * NCH) % 2}"])


    p.barrier()
    pa.close()
    pb = contextlib.ExitStack()

    def sbb_(name, shape, dt, **kw):
        h = pb.enter_context(nc.sbuf_tensor(name, list(shape), dt, **kw))
        return h.ap()

    acc = sbb_("acc", (128, 8 * TB), F32)
    hb = sbb_("hb", (128, 8 * TB), BF16)
    h1g = sbb_("h1g", (128, 16 * TB), BF16)
    w1b = [sbb_(f"w1b{i}", (128, 8 * 128), BF16) for i in range(4)]
    w2b = [sbb_(f"w2b{i}", (128, 16 * 128), BF16) for i in range(2)]
    grep = [sbb_(f"grep{i}", (128, TB), F32) for i in range(2)]
    h1t = [sbb_(f"h1t{i}", (128, 512), BF16) for i in range(4)]
    xb = sbb_("xb", (128, D), F32)
    bident = sbb_("bident", (128, 128), F32)
    psb = [pb.enter_context(nc.psum_tensor(f"pb{i}", [128, 512], F32)).ap()
           for i in range(8)]
    acc3 = acc.rearrange("p (k t) -> p k t", t=TB)
    hb3 = hb.rearrange("p (k t) -> p k t", t=TB)
    h1g3 = h1g.rearrange("p (k t) -> p k t", t=TB)

    p.add("sp", lambda e: e.dma_start(out=bident[...], in_=ident_d[...]), writes=["bident"], dma=True)

    for bi in range(nblocks):
        t0 = bi * TB
        for md in range(8):
            p.add("sp", lambda e, md=md, t0=t0: e.dma_start(out=acc3[:, md, :], in_=attn_s[md, :, t0:t0 + TB]),
                  reads=["acc"], writes=["acc"], dma=True)
            p.add("act", lambda e, md=md, t0=t0: e.dma_start(out=hb3[:, md, :], in_=h_s[md, :, t0:t0 + TB]),
                  reads=["hb"], writes=["hb"], dma=True)
        for s in range(TB // 128):
            p.add("sp", lambda e, s=s, t0=t0: e.dma_start(out=xb[...], in_=x_d[t0 + s * 128:t0 + (s + 1) * 128, :]),
                  reads=["xb"], writes=["xb"], dma=True)
            for half in range(2):
                for k4 in range(4):
                    k = half * 4 + k4
                    p.add("pe", lambda e, k=k, k4=k4: e.transpose(psb[2][:, k4 * 128:(k4 + 1) * 128],
                                                                  xb[:, k * 128:(k + 1) * 128], bident[...]),
                          reads=["xb", "bident"], writes=["pb2"])
                p.add("dve", lambda e, half=half, s=s:
                      e.tensor_add(acc3[:, half * 4:(half + 1) * 4, s * 128:(s + 1) * 128],
                                   psb[2][:, 0:512].rearrange("p (a b) -> p a b", b=128),
                                   acc3[:, half * 4:(half + 1) * 4, s * 128:(s + 1) * 128]),
                      reads=["pb2", "acc"], writes=["acc"])
        for e_i in range(nexp):
            ge = e_i % 2
            if e_i < NE:
                p.add("sp", lambda e, e_i=e_i, ge=ge, t0=t0:
                      e.dma_start(out=grep[ge][...], in_=g_s[e_i, t0:t0 + TB].unsqueeze(0).to_broadcast([128, TB])),
                      reads=[f"grep{ge}"], writes=[f"grep{ge}"], dma=True)
            else:
                p.add("dve", lambda e, ge=ge: e.memset(grep[ge][...], 1.0),
                      reads=[f"grep{ge}"], writes=[f"grep{ge}"])
            for fc in range(16):
                wb = fc % 4
                p.add("sp", lambda e, e_i=e_i, fc=fc, wb=wb: e.dma_start(out=w1b[wb][...], in_=w1T_s[e_i, fc, :, :]),
                      reads=[f"w1b{wb}"], writes=[f"w1b{wb}"], dma=True)
                for ts2 in range(TB // 512):
                    pi = (fc % 2) * 2 + ts2
                    pbk = f"pb{pi}"
                    hi = (fc % 2) * 2 + ts2
                    for k in range(8):
                        p.add("pe", lambda e, fc=fc, wb=wb, ts2=ts2, k=k, pi=pi:
                              e.matmul(psb[pi][:, 0:512], w1b[wb][:, k * 128:(k + 1) * 128],
                                       hb3[:, k, ts2 * 512:(ts2 + 1) * 512], start=(k == 0), stop=(k == 7)),
                              reads=[f"w1b{wb}", "hb"], writes=[pbk])
                    p.add("act", lambda e, ts2=ts2, pi=pi, hi=hi: e.activation(h1t[hi][...], psb[pi][:, 0:512], AF.Silu),
                          reads=[pbk], writes=[f"h1t{hi}"])
                    p.add("dve", lambda e, fc=fc, ts2=ts2, ge=ge, hi=hi:
                          e.tensor_mul(h1g3[:, fc, ts2 * 512:(ts2 + 1) * 512], h1t[hi][...],
                                       grep[ge][:, ts2 * 512:(ts2 + 1) * 512]),
                          reads=[f"h1t{hi}", f"grep{ge}"], writes=["h1g"])
            for md in range(8):
                wb = md % 2
                p.add("act", lambda e, e_i=e_i, md=md, wb=wb: e.dma_start(out=w2b[wb][...], in_=w2T_s[e_i, md, :, :]),
                      reads=[f"w2b{wb}"], writes=[f"w2b{wb}"], dma=True)
                for ts2 in range(TB // 512):
                    pi = 4 + (md % 2) * 2 + ts2
                    pbk = f"pb{pi}"
                    for fc in range(16):
                        p.add("pe", lambda e, md=md, wb=wb, ts2=ts2, fc=fc, pi=pi:
                              e.matmul(psb[pi][:, 0:512], w2b[wb][:, fc * 128:(fc + 1) * 128],
                                       h1g3[:, fc, ts2 * 512:(ts2 + 1) * 512], start=(fc == 0), stop=(fc == 15)),
                              reads=[f"w2b{wb}", "h1g"], writes=[pbk])
                    p.add("dve", lambda e, md=md, ts2=ts2, pi=pi:
                          e.tensor_add(acc3[:, md, ts2 * 512:(ts2 + 1) * 512], psb[pi][:, 0:512],
                                       acc3[:, md, ts2 * 512:(ts2 + 1) * 512]),
                          reads=[pbk, "acc"], writes=["acc"])
        for md in range(8):
            p.add("sp", lambda e, md=md, t0=t0: e.dma_start(out=out_d[md, :, t0:t0 + TB], in_=acc3[:, md, :]),
                  reads=["acc"], dma=True)

    with nc.Block() as block:
        stack = p.emit(nc, block)
    stack.close()
    return nc, dbg_outs


def _mk_consts():
    ident = np.eye(128, dtype=np.float32)
    j = np.arange(128)[:, None]
    i = np.arange(128)[None, :]
    masklogT = np.where(i >= j, 0.0, NEG_BIG).astype(np.float32)
    iota8 = np.broadcast_to(np.arange(NE, dtype=np.float32)[None, :], (128, NE)).copy()
    return ident, masklogT, iota8


def _core_inputs(inputs, b):
    ident, masklogT, iota8 = _mk_consts()
    return {
        "x": np.ascontiguousarray(inputs["L0_attn_u"][b]),
        "w_in": inputs["L0_attn_in_proj_w"],
        "conv_w": inputs["L0_attn_conv_w"],
        "conv_b": inputs["L0_attn_conv_b"],
        "A_log": inputs["L0_attn_A_log"],
        "Dparam": inputs["L0_attn_Dparam"],
        "dt_bias": inputs["L0_attn_dt_bias"],
        "mnorm_w": inputs["L0_attn_norm_w"],
        "w_out": inputs["L0_attn_out_proj_w"],
        "w1": inputs["L0_ffn_expert_w1"],
        "w2": inputs["L0_ffn_expert_w2"],
        "sw1": inputs["L0_ffn_shared_w1"],
        "sw2": inputs["L0_ffn_shared_w2"],
        "rw": inputs["L0_ffn_router_w"],
        "ln_g": inputs["L0_ffn_ln_gamma"],
        "ln_b": inputs["L0_ffn_ln_beta"],
        "ident": ident,
        "identb": ident.astype(ml_dtypes.bfloat16),
        "masklogT": masklogT,
        "iota8": iota8,
    }


def kernel(**inputs):
    nc, _ = build_program()
    in_maps = [_core_inputs(inputs, b) for b in range(8)]
    res = run_bass_kernel_spmd(nc, in_maps, core_ids=list(range(8)))
    outs = []
    for r in res.results:
        o = np.asarray(r["out_fm"])
        outs.append(o.reshape(D, T).T)
    return np.stack(outs).astype(np.float32)
